# version 04
# baseline (speedup 1.0000x reference)
import math
import numpy as np
import ml_dtypes
from contextlib import ExitStack

import concourse.bass as bass
import concourse.mybir as mybir
from concourse.bass_utils import run_bass_kernel_spmd

F32 = mybir.dt.float32
BF16 = mybir.dt.bfloat16
AF = mybir.ActivationFunctionType
ALU = mybir.AluOpType
AX = mybir.AxisListType.X

NCORES = 8
T = 2048
W = 512
NB = 4
D = 128
H = 16
HK = 4
DM = 2048
KD = 16
NE = 20
IC = 11
CAP = 20.0
EPS = 1e-5
INV_SQRT_D = 1.0 / math.sqrt(D)

BF = ml_dtypes.bfloat16



class Prog:

    def __init__(self, nc, es):
        self.nc = nc
        self.cnt = {}
        self.sems = {}
        self.phases = []
        self.cur = None

    def mksem(self, es, name):
        self.sems[name] = es.enter_context(self.nc.semaphore(name))
        self.cnt[name] = 0

    def phase(self, name):
        self.cur = {"name": name, "t": [], "v": [], "s": [], "g": [], "y": []}
        self.phases.append(self.cur)

    def replay(self):
        for ph in self.phases:
            with self.nc.Block() as blk:
                if ph["g"]:
                    @blk.gpsimd
                    def _(g, ops=ph["g"]):
                        for f in ops:
                            f(g)
                if ph["t"]:
                    @blk.tensor
                    def _(t, ops=ph["t"]):
                        for f in ops:
                            f(t)
                if ph["s"]:
                    @blk.scalar
                    def _(s, ops=ph["s"]):
                        for f in ops:
                            f(s)
                if ph["v"]:
                    @blk.vector
                    def _(v, ops=ph["v"]):
                        for f in ops:
                            f(v)
                if ph["y"]:
                    @blk.sync
                    def _(y, ops=ph["y"]):
                        for f in ops:
                            f(y)


class ES_:
    def __init__(self, prog, key, own=None):
        self.p = prog
        self.key = key
        self.own = own

    def raw(self, fn):
        self.p.cur[self.key].append(fn)

    def wait(self, sem, val):
        if val is None or val <= 0:
            return
        self.raw(lambda eng, s=self.p.sems[sem], v=val: eng.wait_ge(s, v))

    def wait_gate(self, gate):
        if gate:
            self.wait(gate[0], gate[1])

    def op(self, fn, inc=None, n=1):
        if inc is None:
            self.raw(lambda eng, f=fn: f(eng))
            return None
        self.p.cnt[inc] += n
        tgt = self.p.cnt[inc]
        self.raw(lambda eng, f=fn, s=self.p.sems[inc], m=n: f(eng).then_inc(s, m))
        return tgt

    def do(self, fn, extra_waits=()):
        for sv in extra_waits:
            if sv:
                self.wait(sv[0], sv[1])
        self.wait(self.own, self.p.cnt[self.own])
        return self.op(fn, inc=self.own)



def _tile_lhsT(wT, oc_chunks):
    din, dout = wT.shape
    assert din == DM and dout == oc_chunks * 128
    return np.ascontiguousarray(
        wT.reshape(KD, 128, oc_chunks, 128).transpose(2, 1, 0, 3))


def host_prep(inputs):
    d = {}
    x = np.asarray(inputs["L0_attn_x"])[0]
    xT = np.ascontiguousarray(x.T)
    xT4 = xT.reshape(KD, 128, NCORES * T)
    xm, xh = [], []
    for c in range(NCORES):
        xm.append(np.ascontiguousarray(
            xT4[:, :, c * T:(c + 1) * T].transpose(1, 0, 2)))
        h0 = c * T - W if c > 0 else 0
        xh.append(np.ascontiguousarray(
            xT4[:, :, h0:h0 + W].transpose(1, 0, 2)))
    d["xm"], d["xh"] = xm, xh

    d["wqt"] = _tile_lhsT(np.ascontiguousarray(np.asarray(inputs["L0_attn_wq"]).T), 16)
    d["wkt"] = _tile_lhsT(np.ascontiguousarray(np.asarray(inputs["L0_attn_wk"]).T), 4)
    d["wot"] = _tile_lhsT(np.ascontiguousarray(np.asarray(inputs["L0_attn_wo"]).T), 16)
    d["wvt"] = np.ascontiguousarray(
        np.ascontiguousarray(np.asarray(inputs["L0_attn_wv"]).T)
        .reshape(KD, 128, 512).transpose(1, 0, 2))
    d["routert"] = np.ascontiguousarray(
        np.ascontiguousarray(np.asarray(inputs["L0_ffn_router_w"]).T)
        .reshape(KD, 128, 16).transpose(1, 0, 2))
    d["shgw"] = np.ascontiguousarray(
        np.asarray(inputs["L0_ffn_sh_gate_w"]).reshape(KD, 128, 1).transpose(1, 0, 2))

    wg = np.asarray(inputs["L0_ffn_w_gate"])
    wu = np.asarray(inputs["L0_ffn_w_up"])
    wd = np.asarray(inputs["L0_ffn_w_down"])
    shg = np.asarray(inputs["L0_ffn_sh_gate"])
    shu = np.asarray(inputs["L0_ffn_sh_up"])
    shd = np.asarray(inputs["L0_ffn_sh_down"])

    def tile_gu(m):
        return m.reshape(KD, 128, IC, 128).transpose(2, 1, 0, 3)

    def tile_dn(m):
        return m.reshape(IC, 128, KD, 128).transpose(2, 1, 0, 3)

    wg_all = np.empty((NE, IC, 128, KD, 128), np.float32)
    wu_all = np.empty((NE, IC, 128, KD, 128), np.float32)
    wd_all = np.empty((NE, KD, 128, IC, 128), np.float32)
    for e in range(16):
        wg_all[e] = tile_gu(wg[e])
        wu_all[e] = tile_gu(wu[e])
        wd_all[e] = tile_dn(wd[e])
    for sx in range(4):
        wg_all[16 + sx] = tile_gu(shg[:, sx * 1408:(sx + 1) * 1408])
        wu_all[16 + sx] = tile_gu(shu[:, sx * 1408:(sx + 1) * 1408])
        wd_all[16 + sx] = tile_dn(shd[sx * 1408:(sx + 1) * 1408])
    d["wg_all"], d["wu_all"], d["wd_all"] = wg_all, wu_all, wd_all

    inv = 10000.0 ** (-np.arange(0, D, 2, dtype=np.float64) / D)
    c2s, s2s = [], []
    for c in range(NCORES):
        pos = np.arange(c * T - W, c * T + T, dtype=np.float64)
        ang = np.outer(pos, inv)
        cosv = np.cos(ang).T.astype(np.float32)
        sinv = np.sin(ang).T.astype(np.float32)
        c2 = np.empty((128, T + W), np.float32)
        s2 = np.empty((128, T + W), np.float32)
        c2[0::2] = cosv
        c2[1::2] = cosv
        s2[0::2] = -sinv
        s2[1::2] = sinv
        c2s.append(c2.astype(BF))
        s2s.append(s2.astype(BF))
    d["c2"], d["s2"] = c2s, s2s

    j = np.arange(1024)[:, None]
    a = np.arange(W)[None, :]
    mR = ((j < W) & (j >= a)) | ((j >= W) & (j - W <= a))
    mA0 = (j >= W) & (j - W <= a)
    d["maskR"] = np.ascontiguousarray(
        mR.astype(BF).reshape(8, 128, W).transpose(1, 0, 2))
    d["maskA0"] = np.ascontiguousarray(
        mA0.astype(BF).reshape(8, 128, W).transpose(1, 0, 2))

    psw = np.zeros((128, 128), np.float32)
    i2 = np.arange(0, 128, 2)
    psw[i2, i2 + 1] = 1.0
    psw[i2 + 1, i2] = 1.0
    d["pswap"] = psw.astype(BF)
    d["identb"] = np.eye(128).astype(BF)
    d["identf"] = np.eye(128, dtype=np.float32)
    return d



def build_nc():
    nc = bass.Bass("TRN2", target_bir_lowering=False)

    dr = {}
    def din(name, shape, dt=F32):
        dr[name] = nc.dram_tensor(name, list(shape), dt, kind="ExternalInput").ap()
    din("xm", (128, KD, T)); din("xh", (128, KD, W))
    din("wqt", (16, 128, KD, 128)); din("wkt", (4, 128, KD, 128))
    din("wvt", (128, KD, 512)); din("wot", (16, 128, KD, 128))
    din("routert", (128, KD, 16)); din("shgw", (128, KD, 1))
    din("wg_all", (NE, IC, 128, KD, 128)); din("wu_all", (NE, IC, 128, KD, 128))
    din("wd_all", (NE, KD, 128, IC, 128))
    din("c2", (128, T + W), BF16); din("s2", (128, T + W), BF16)
    din("maskR", (128, 8, W), BF16); din("maskA0", (128, 8, W), BF16)
    din("pswap", (128, 128), BF16); din("identb", (128, 128), BF16)
    din("identf", (128, 128))
    y = nc.dram_tensor("y", [T, DM], F32, kind="ExternalOutput").ap()
    comb_dram = nc.dram_tensor("comb_scr", [16, T], F32, kind="Internal").ap()

    es = ExitStack()
    P = Prog(nc, es)
    for sname in ["p0", "wq0", "wq1", "wq2", "wg0", "wg1", "wu0", "wu1", "wd0", "wd1",
                  "cr0", "cr1", "cmb", "pe", "act", "dve", "out0", "out1"]:
        P.mksem(es, sname)
    t = ES_(P, "t")
    v = ES_(P, "v", own="dve")
    s = ES_(P, "s", own="act")
    g = ES_(P, "g")
    yq = ES_(P, "y")
    cnt = P.cnt
    gate = {}

    def sb(stack, name, shape, dt):
        return stack.enter_context(nc.sbuf_tensor(name, list(shape), dt))

    x_res = sb(es, "x_res", (128, KD, T), BF16)
    ones_cb = sb(es, "ones_cb", (128, 1), BF16)
    ones_rb = sb(es, "ones_rb", (1, 128), BF16)

    ps = [es.enter_context(nc.psum_tensor(f"ps{i}", [128, 512], F32))
          for i in range(2)]
    ps_sc = es.enter_context(nc.psum_tensor("ps_sc", [128, 1024], F32))
    ps.append(ps_sc.ap()[:, 0:512])
    ps.append(ps_sc.ap()[:, 512:1024])
    ps += [es.enter_context(nc.psum_tensor(f"ps{i}", [128, 512], F32))
           for i in (4, 5)]
    pst = [es.enter_context(nc.psum_tensor(f"pst{i}", [128, 128], BF16))
           for i in range(2)]

    def dma(ap_out, ap_in, sem):
        return g.op(lambda e, o=ap_out, i=ap_in: e.dma_start(out=o, in_=i),
                    inc=sem, n=16)

    att = ExitStack()
    ones_rf = sb(att, "ones_rf", (1, 128), F32)
    pswap = sb(att, "pswap_t", (128, 128), BF16)
    wvt_sb = sb(att, "wvt_sb", (128, KD, 512), BF16)
    kT = sb(att, "kT", (128, 2, HK, W), BF16)
    v_tok = sb(att, "v_tok", (128, 2, 4, W), BF16)
    c2_sb = sb(att, "c2_sb", (128, W), BF16)
    s2_sb = sb(att, "s2_sb", (128, W), BF16)
    mask_sb = sb(att, "mask_sb", (128, 8, W), BF16)
    wqo_sb = sb(att, "wqo_sb", (128, 3, KD, 128), BF16)
    qT = sb(att, "qT", (128, 2, W), BF16)
    q_raw = sb(att, "q_raw", (128, 2, W), BF16)
    rtmp1 = sb(att, "rtmp1", (128, W), F32)
    rtmp2 = sb(att, "rtmp2", (128, W), F32)
    expT = sb(att, "expT", (128, 2, 8, W), BF16)
    etmp = sb(att, "etmp", (128, 2, 2 * W), BF16)
    ttmp = sb(att, "ttmp", (128, 2 * W), F32)
    rb_sb = sb(att, "rb_sb", (128, 2, W), F32)
    recip_row = sb(att, "recip_row", (1, 2, W), F32)
    attn_out = sb(att, "attn_out", (128, KD, W), BF16)
    s1_pp = sb(att, "s1_pp", (128, 2, W), BF16)
    sq_pp = sb(att, "sq_pp", (128, 2, W), BF16)
    lnrow = sb(att, "lnrow", (1, 3, W), F32)
    mb_sb = sb(att, "mb_sb", (128, W), F32)
    rb2_sb = sb(att, "rb2_sb", (128, W), F32)
    ln_t = sb(att, "ln_t", (128, W), F32)

    halo = ExitStack()
    x_halo = sb(halo, "x_halo", (128, KD, W), BF16)

    def x_blk(n):
        return [x_res[:, kd, n * W:(n + 1) * W] for kd in range(KD)]

    wplan = {"aps": [], "next": 0, "cnt": {}}

    def wreset(aps):
        wplan["aps"] = aps
        wplan["next"] = 0
        wplan["cnt"] = {}

    def wpump(upto):
        while wplan["next"] < min(upto, len(wplan["aps"])):
            i = wplan["next"]
            slot = i % 3
            g.wait_gate(gate.get(("wqo", slot)))
            dma(wqo_sb[:, slot], wplan["aps"][i], f"wq{slot}")
            wplan["cnt"][i] = cnt[f"wq{slot}"]
            wplan["next"] += 1

    def wwait(i):
        t.wait(f"wq{i % 3}", wplan["cnt"][i])

    def wdone(i, pe_val):
        gate[("wqo", i % 3)] = ("pe", pe_val)
        wpump(i + 4)

    def kv_block(n, x_src, widx0):
        slot = n % 2
        for kvh in range(HK):
            wi = widx0 + kvh
            sp3 = wi % 3
            wwait(wi)
            t.wait_gate(gate.get("ps0"))
            t.wait_gate(gate.get(("qraw_rd", kvh % 2)))
            for kd in range(KD):
                t.op(lambda e, kd=kd, sp3=sp3, xs=x_src: e.matmul(
                    ps[0][:], wqo_sb[:, sp3, kd, :], xs[kd],
                    start=(kd == 0), stop=(kd == KD - 1), skip_group_check=True),
                    inc="pe" if kd == KD - 1 else None)
            pe_k = cnt["pe"]
            wdone(wi, pe_k)
            pp = kvh % 2
            a_ev = s.do(lambda e, pp=pp: e.copy(out=q_raw[:, pp, :], in_=ps[0][:]),
                        extra_waits=[("pe", pe_k), ("dve", cnt["dve"])])
            gate["ps0"] = ("act", a_ev)
            t.wait("act", a_ev)
            t.wait_gate(gate.get("ps1"))
            t.op(lambda e, pp=pp: e.matmul(ps[1][:], pswap[:], q_raw[:, pp, :],
                                           start=True, stop=True, skip_group_check=True), inc="pe")
            pe_sh = cnt["pe"]
            v.do(lambda e, pp=pp: e.tensor_mul(
                rtmp1[:], q_raw[:, pp, :], c2_sb[:]),
                extra_waits=[("act", a_ev), ("p0", cnt["p0"])])
            d2 = v.do(lambda e: e.tensor_mul(
                rtmp2[:], ps[1][:], s2_sb[:]),
                extra_waits=[("pe", pe_sh)])
            gate["ps1"] = ("dve", d2)
            dk = v.do(lambda e, slot=slot, kvh=kvh: e.tensor_add(
                kT[:, slot, kvh, :], rtmp1[:], rtmp2[:]))
            gate[("qraw_rd", pp)] = ("dve", dk)
        for jc in range(4):
            pp = jc % 2
            if pp == 0:
                t.wait_gate(gate.get("ps_sc"))
            for kd in range(KD):
                t.op(lambda e, kd=kd, jc=jc, pp=pp, xs=x_src: e.matmul(
                    ps[2 + pp][:], xs[kd][:, jc * 128:(jc + 1) * 128],
                    wvt_sb[:, kd, :],
                    start=(kd == 0), stop=(kd == KD - 1), skip_group_check=True),
                    inc="pe" if kd == KD - 1 else None)
            pe_v = cnt["pe"]
            a_v = s.do(lambda e, slot=slot, jc=jc, pp=pp: e.copy(
                out=v_tok[:, slot, jc, :], in_=ps[2 + pp][:]),
                extra_waits=[("pe", pe_v)])
            gate["ps_sc"] = ("act", a_v)

    P.phase("P0")
    for kq in range(4):
        dma(x_res[:, kq * 4:(kq + 1) * 4, :], dr["xm"][:, kq * 4:(kq + 1) * 4, :], "p0")
    dma(x_halo[:], dr["xh"], "p0")
    dma(wvt_sb[:], dr["wvt"], "p0")
    dma(c2_sb[:], dr["c2"][:, 0:W], "p0")
    dma(s2_sb[:], dr["s2"][:, 0:W], "p0")
    dma(mask_sb[:], dr["maskA0"], "p0")
    dma(pswap[:], dr["pswap"], "p0")
    p0_total = cnt["p0"]
    v.do(lambda e: e.memset(ones_cb[:], 1.0))
    v.do(lambda e: e.memset(ones_rf[:], 1.0))
    v.do(lambda e: e.memset(ones_rb[:], 1.0))
    t.wait("p0", p0_total)
    t.wait("dve", cnt["dve"])
    full_plan = [dr["wkt"][kvh] for kvh in range(HK)]
    for _n in range(NB):
        full_plan += [dr["wkt"][kvh] for kvh in range(HK)]
        full_plan += [dr["wqt"][h] for h in range(H)]
        full_plan += [dr["wot"][dc] for dc in range(KD)]
    wreset(full_plan)
    wpump(3)
    kv_block(-1, [x_halo[:, kd, :] for kd in range(KD)], 0)

    P.phase("ATT")

    def emit_qproj(n, h, xb, wbase0):
        hp = h % 2
        wi = wbase0 + 4 + h
        sp3 = wi % 3
        wwait(wi)
        t.wait_gate(gate.get("ps0"))
        for kd in range(KD):
            t.op(lambda e, kd=kd, sp3=sp3, xb=xb: e.matmul(
                ps[0][:], wqo_sb[:, sp3, kd, :], xb[kd],
                start=(kd == 0), stop=(kd == KD - 1), skip_group_check=True),
                inc="pe" if kd == KD - 1 else None)
        pe_q = cnt["pe"]
        wdone(wi, pe_q)
        a_ev = s.do(lambda e, hp=hp: e.copy(out=q_raw[:, hp, :], in_=ps[0][:]),
                    extra_waits=[("pe", pe_q), ("dve", cnt["dve"])])
        gate["ps0"] = ("act", a_ev)
        t.wait("act", a_ev)
        t.wait_gate(gate.get("ps1"))
        t.op(lambda e, hp=hp: e.matmul(ps[1][:], pswap[:], q_raw[:, hp, :],
                                       start=True, stop=True, skip_group_check=True), inc="pe")
        pe_sh = cnt["pe"]
        v.do(lambda e, hp=hp: e.tensor_mul(
            rtmp1[:], q_raw[:, hp, :], c2_sb[:]),
            extra_waits=[("act", a_ev), ("p0", cnt["p0"])])
        d2 = v.do(lambda e: e.tensor_mul(
            rtmp2[:], ps[1][:], s2_sb[:]),
            extra_waits=[("pe", pe_sh)])
        gate["ps1"] = ("dve", d2)
        dv_q = v.do(lambda e, hp=hp: e.tensor_add(
            qT[:, hp, :], rtmp1[:], rtmp2[:]),
            extra_waits=[gate[("qT", hp)]] if ("qT", hp) in gate else ())
        gate[("qraw_rd", hp)] = ("dve", dv_q)
        return dv_q

    def emit_scores(n, h, dv_q, msk):
        kvh = h // 4
        hp = h % 2
        slot, pslot = n % 2, (n - 1) % 2
        t.wait("dve", dv_q)
        dms = []
        for jp in range(4):
            jc0 = 2 * jp
            pv = jp % 2
            t.wait_gate(gate.get("ps_sc"))
            for half in range(2):
                jc = jc0 + half
                sl = pslot if jc < 4 else slot
                jj = jc % 4
                t.op(lambda e, sl=sl, kvh=kvh, jj=jj, hp=hp, half=half: e.matmul(
                    ps[2 + half][:], kT[:, sl, kvh, jj * 128:(jj + 1) * 128],
                    qT[:, hp, :], start=True, stop=True, skip_group_check=True),
                    inc="pe" if half == 1 else None)
            pe_sc = cnt["pe"]
            a_t = s.do(lambda e: e.activation(
                out=ttmp[:], in_=ps_sc[:], func=AF.Tanh,
                scale=INV_SQRT_D / CAP), extra_waits=[("pe", pe_sc)])
            gate["ps_sc"] = ("act", a_t)
            a_e = s.do(lambda e, pv=pv: e.activation(
                out=etmp[:, pv, :], in_=ttmp[:], func=AF.Exp,
                scale=CAP), extra_waits=[gate.get(("etmp", pv))])
            d_m = v.do(lambda e, jc0=jc0, hp=hp, pv=pv, msk=msk: e.tensor_mul(
                expT[:, hp, jc0:jc0 + 2, :], etmp[:, pv, :].rearrange("p (a b) -> p a b", a=2),
                msk[:, jc0:jc0 + 2, :]),
                extra_waits=[("act", a_e), ("p0", cnt["p0"])] +
                    ([gate[("expT", hp)]] if jp == 0 and ("expT", hp) in gate
                     else []))
            gate[("etmp", pv)] = ("dve", d_m)
            dms.append((jc0, d_m))
            dms.append((jc0 + 1, d_m))
        return dms

    def emit_sumsout(n, h, dms):
        kvh = h // 4
        hp = h % 2
        slot, pslot = n % 2, (n - 1) % 2
        first = True
        for jc, d_m in dms:
            sl = pslot if jc < 4 else slot
            jj = jc % 4
            t.wait("dve", d_m)
            t.op(lambda e, jc=jc, hp=hp, st=first: e.matmul(
                ps[4][0:1, :], ones_cb[:], expT[:, hp, jc, :],
                start=st, stop=(jc == 7), skip_group_check=True), inc="pe")
            t.op(lambda e, sl=sl, jj=jj, kvh=kvh, jc=jc, hp=hp, st=first: e.matmul(
                ps[5][:], v_tok[:, sl, jj, kvh * 128:(kvh + 1) * 128],
                expT[:, hp, jc, :], start=st, stop=(jc == 7), skip_group_check=True),
                inc="pe" if jc == 7 else None)
            first = False
        pe_att = cnt["pe"]
        gate[("expT", hp)] = ("pe", pe_att)
        gate[("qT", hp)] = ("pe", pe_att)
        d_r = v.do(lambda e, hp=hp: e.reciprocal(
            out=recip_row[:, hp, :], in_=ps[4][0:1, :]),
            extra_waits=[("pe", pe_att)])
        t.wait("dve", d_r)
        t.wait_gate(gate.get("ps0"))
        t.op(lambda e, hp=hp: e.matmul(ps[0][:], ones_rf[:],
                                       recip_row[:, hp, :],
                                       start=True, stop=True, skip_group_check=True), inc="pe")
        pe_rb = cnt["pe"]
        a_rb = s.do(lambda e, hp=hp: e.copy(out=rb_sb[:, hp, :], in_=ps[0][:]),
                    extra_waits=[("pe", pe_rb)])
        gate["ps0"] = ("act", a_rb)
        d_ao = v.do(lambda e, h=h, hp=hp: e.tensor_mul(
            attn_out[:, h, :], ps[5][:], rb_sb[:, hp, :]),
            extra_waits=[("act", a_rb)])
        gate["ps5"] = ("dve", d_ao)
        return pe_att

    for n in range(NB):
        slot, pslot = n % 2, (n - 1) % 2
        tab0 = W + n * W
        xb = x_blk(n)
        g.wait_gate(gate.get("ropetab"))
        dma(c2_sb[:], dr["c2"][:, tab0:tab0 + W], "p0")
        dma(s2_sb[:], dr["s2"][:, tab0:tab0 + W], "p0")
        if n == 1:
            g.wait_gate(gate.get("maskrd"))
            dma(mask_sb[:], dr["maskR"], "p0")
        t.wait_gate(gate.get(("kvslot", slot)))
        wbase = 4 + 36 * n
        kv_block(n, xb, wbase)
        msk = mask_sb

        dv_q = emit_qproj(n, 0, xb, wbase)
        for h in range(H):
            dms = emit_scores(n, h, dv_q, msk)
            if h + 1 < H:
                dv_q = emit_qproj(n, h + 1, xb, wbase)
            pe_att = emit_sumsout(n, h, dms)
        gate[("kvslot", pslot)] = ("pe", pe_att)
        gate["ropetab"] = ("dve", cnt["dve"])
        gate["maskrd"] = ("dve", cnt["dve"])

        dv_attn_done = cnt["dve"]
        first = True
        for dc in range(KD):
            dp = dc % 2
            wi = wbase + 20 + dc
            sp3 = wi % 3
            wwait(wi)
            if dc == 0:
                t.wait("dve", dv_attn_done)
            t.wait_gate(gate.get(f"pso{dp}"))
            for kd in range(KD):
                t.op(lambda e, kd=kd, sp3=sp3, dp=dp: e.matmul(
                    ps[dp][:], wqo_sb[:, sp3, kd, :], attn_out[:, kd, :],
                    start=(kd == 0), stop=(kd == KD - 1), skip_group_check=True),
                    inc="pe" if kd == KD - 1 else None)
            pe_o = cnt["pe"]
            wdone(wi, pe_o)
            a_o = s.do(lambda e, dc=dc, dp=dp: e.copy(
                out=expT[:, dc // 8, dc % 8, :], in_=ps[dp][:]),
                extra_waits=[("pe", pe_o)])
            gate[f"pso{dp}"] = ("act", a_o)
            d_s1 = v.do(lambda e, dc=dc, n=n: e.tensor_add(
                s1_pp[:, dc % 2, :], x_res[:, dc, n * W:(n + 1) * W],
                expT[:, dc // 8, dc % 8, :]), extra_waits=[("act", a_o)])
            a_sq = s.do(lambda e, dc=dc: e.activation(
                out=sq_pp[:, dc % 2, :], in_=s1_pp[:, dc % 2, :], func=AF.Square),
                extra_waits=[("dve", d_s1)])
            t.wait("dve", d_s1)
            if dc == 0:
                t.wait_gate(gate.get("ps_sc"))
            t.op(lambda e, dc=dc, st=first: e.matmul(
                ps[2][0:1, :], ones_cb[:], s1_pp[:, dc % 2, :],
                start=st, stop=(dc == KD - 1), skip_group_check=True), inc="pe")
            t.wait("act", a_sq)
            t.op(lambda e, dc=dc, st=first: e.matmul(
                ps[3][0:1, :], ones_cb[:], sq_pp[:, dc % 2, :],
                start=st, stop=(dc == KD - 1), skip_group_check=True),
                inc="pe" if dc == KD - 1 else None)
            first = False
        pe_stats = cnt["pe"]
        v.do(lambda e: e.tensor_scalar_mul(lnrow[:, 0, :], ps[2][0:1, :], 1.0 / DM),
             extra_waits=[("pe", pe_stats)])
        v.do(lambda e: e.tensor_scalar_mul(lnrow[:, 1, :], ps[3][0:1, :], 1.0 / DM))
        v.do(lambda e: e.tensor_mul(lnrow[:, 2, :], lnrow[:, 0, :], lnrow[:, 0, :]))
        v.do(lambda e: e.tensor_sub(lnrow[:, 1, :], lnrow[:, 1, :],
                                    lnrow[:, 2, :]))
        dv_var = v.do(lambda e: e.tensor_scalar_add(lnrow[:, 1, :],
                                                    lnrow[:, 1, :], EPS))
        gate["ps_sc"] = ("dve", dv_var)
        s.do(lambda e: e.activation(out=lnrow[:, 2, :], in_=lnrow[:, 1, :],
                                    func=AF.Ln),
             extra_waits=[("dve", dv_var)])
        a_rs = s.do(lambda e: e.activation(out=lnrow[:, 1, :], in_=lnrow[:, 2, :],
                                           func=AF.Exp, scale=-0.5))
        t.wait("act", a_rs)
        t.wait_gate(gate.get("ps4"))
        t.op(lambda e: e.matmul(ps[4][:], ones_rf[:], lnrow[:, 0, :],
                                start=True, stop=True, skip_group_check=True), inc="pe")
        t.wait_gate(gate.get("ps5"))
        t.op(lambda e: e.matmul(ps[5][:], ones_rf[:], lnrow[:, 1, :],
                                start=True, stop=True, skip_group_check=True), inc="pe")
        pe_bc = cnt["pe"]
        a_mb = s.do(lambda e: e.copy(out=mb_sb[:], in_=ps[4][:]),
                    extra_waits=[("pe", pe_bc)])
        a_rb2 = s.do(lambda e: e.copy(out=rb2_sb[:], in_=ps[5][:]))
        gate["ps4"] = ("act", a_mb)
        gate["ps5"] = ("act", a_rb2)
        for dc in range(KD):
            v.do(lambda e, dc=dc, n=n: e.tensor_add(
                s1_pp[:, dc % 2, :], x_res[:, dc, n * W:(n + 1) * W],
                expT[:, dc // 8, dc % 8, :]),
                extra_waits=[("act", a_rb2)] if dc == 0 else ())
            v.do(lambda e, dc=dc: e.tensor_sub(ln_t[:], s1_pp[:, dc % 2, :], mb_sb[:]))
            v.do(lambda e, dc=dc, n=n: e.tensor_mul(
                x_res[:, dc, n * W:(n + 1) * W], ln_t[:], rb2_sb[:]))

    halo.close()
    att.close()

    ffn = ExitStack()
    racc = sb(ffn, "racc", (128, KD, T), BF16)
    combb = sb(ffn, "combb", (128, T), BF16)
    crow = sb(ffn, "crow", (1, T), BF16)
    sg_row = sb(ffn, "sg_row", (1, T), BF16)

    rt = ExitStack()
    identf = sb(rt, "identf_t", (128, 128), F32)
    routert_sb = sb(rt, "routert_sb", (128, KD, 16), BF16)
    shgw_sb = sb(rt, "shgw_sb", (128, KD, 1), BF16)
    lg_sb = sb(rt, "lg_sb", (16, 4, 512), F32)
    ltok = sb(rt, "ltok", (128, 16, 16), F32)
    ptmp = sb(rt, "ptmp", (128, 16, 16), F32)
    selacc = sb(rt, "selacc", (128, 16, 16), F32)
    maskt = sb(rt, "maskt", (128, 16, 16), F32)
    comb_tok = sb(rt, "comb_tok", (128, 16, 16), F32)
    mx16 = sb(rt, "mx16", (128, 16), F32)
    ts16 = sb(rt, "ts16", (128, 16), F32)
    comb_d = sb(rt, "comb_d", (16, T), F32)

    P.phase("R")
    dma(identf[:], dr["identf"], "p0")
    dma(routert_sb[:], dr["routert"], "p0")
    dma(shgw_sb[:], dr["shgw"], "p0")
    p0r = cnt["p0"]
    dv_ln_done = cnt["dve"]
    for kq in range(4):
        s.do(lambda e, kq=kq: e.copy(out=racc[:, kq * 4:(kq + 1) * 4, :],
                                     in_=x_res[:, kq * 4:(kq + 1) * 4, :]),
             extra_waits=[("dve", dv_ln_done)] if kq == 0 else ())
    t.wait("p0", p0r)
    for tq in range(4):
        t.wait_gate(gate.get("ps0"))
        for kd in range(KD):
            t.op(lambda e, kd=kd, tq=tq: e.matmul(
                ps[0][0:16, :], routert_sb[:, kd, :],
                x_res[:, kd, tq * 512:(tq + 1) * 512],
                start=(kd == 0), stop=(kd == KD - 1), skip_group_check=True),
                inc="pe" if kd == KD - 1 else None)
        pe_lg = cnt["pe"]
        a_lg = s.do(lambda e, tq=tq: e.copy(out=lg_sb[:, tq, :], in_=ps[0][0:16, :]),
                    extra_waits=[("pe", pe_lg)])
        gate["ps0"] = ("act", a_lg)
        t.wait_gate(gate.get("ps_sc"))
        for kd in range(KD):
            t.op(lambda e, kd=kd, tq=tq: e.matmul(
                ps[3][0:1, :], shgw_sb[:, kd, :],
                x_res[:, kd, tq * 512:(tq + 1) * 512],
                start=(kd == 0), stop=(kd == KD - 1), skip_group_check=True),
                inc="pe" if kd == KD - 1 else None)
        pe_sg = cnt["pe"]
        a_sg = s.do(lambda e, tq=tq: e.activation(
            out=sg_row[:, tq * 512:(tq + 1) * 512], in_=ps[3][0:1, :],
            func=AF.Sigmoid), extra_waits=[("pe", pe_sg)])
        gate["ps_sc"] = ("act", a_sg)
    for tc in range(16):
        t.wait_gate(gate.get("ps1"))
        t.op(lambda e, tc=tc: e.transpose(
            ps[1][:, 0:16], lg_sb[:, tc // 4, (tc % 4) * 128:(tc % 4 + 1) * 128],
            identf[0:16, 0:16]), inc="pe")
        pe_tr = cnt["pe"]
        a_ltk = s.do(lambda e, tc=tc: e.copy(out=ltok[:, tc, :], in_=ps[1][:, 0:16]),
                     extra_waits=[("pe", pe_tr)])
        gate["ps1"] = ("act", a_ltk)
    a_lt = cnt["act"]
    bc16 = lambda ap: ap.unsqueeze(2).broadcast_to([128, 16, 16])
    v.do(lambda e: e.tensor_reduce(mx16[:], ltok[:], AX, ALU.max),
         extra_waits=[("act", a_lt)])
    v.do(lambda e: e.tensor_tensor(ptmp[:], ltok[:], bc16(mx16[:]), ALU.subtract))
    a_ex = s.do(lambda e: e.activation(out=ptmp[:], in_=ptmp[:], func=AF.Exp),
                extra_waits=[("dve", cnt["dve"])])
    v.do(lambda e: e.tensor_reduce(mx16[:], ptmp[:], AX, ALU.add),
         extra_waits=[("act", a_ex)])
    v.do(lambda e: e.reciprocal(out=ts16[:], in_=mx16[:]))
    v.do(lambda e: e.tensor_tensor(ptmp[:], ptmp[:], bc16(ts16[:]), ALU.mult))
    v.do(lambda e: e.memset(selacc[:], 0.0))
    v.do(lambda e: e.memset(ts16[:], 0.0))
    for r in range(4):
        v.do(lambda e: e.tensor_reduce(mx16[:], ptmp[:], AX, ALU.max))
        v.do(lambda e: e.tensor_tensor(maskt[:], ptmp[:], bc16(mx16[:]), ALU.is_ge))
        v.do(lambda e: e.tensor_tensor(maskt[:], maskt[:], ptmp[:], ALU.mult))
        v.do(lambda e: e.tensor_add(selacc[:], selacc[:], maskt[:]))
        v.do(lambda e: e.tensor_add(ts16[:], ts16[:], mx16[:]))
        if r < 3:
            v.do(lambda e: e.tensor_sub(ptmp[:], ptmp[:], maskt[:]))
    v.do(lambda e: e.tensor_scalar_max(ts16[:], ts16[:], 1e-9))
    v.do(lambda e: e.reciprocal(out=ts16[:], in_=ts16[:]))
    dv_cmb = v.do(lambda e: e.tensor_tensor(comb_tok[:], selacc[:],
                                            bc16(ts16[:]), ALU.mult))
    t.wait("dve", dv_cmb)
    for tc in range(16):
        t.wait_gate(gate.get("ps_sc"))
        t.op(lambda e, tc=tc: e.transpose(
            ps[2][0:16, 0:128], comb_tok[:, tc, :], identf[:]), inc="pe")
        pe_ct = cnt["pe"]
        a_cd = s.do(lambda e, tc=tc: e.copy(out=comb_d[:, tc * 128:(tc + 1) * 128],
                                            in_=ps[2][0:16, 0:128]),
                    extra_waits=[("pe", pe_ct)])
        gate["ps_sc"] = ("act", a_cd)
    g.wait("act", cnt["act"])
    dma(comb_dram, comb_d[:], "cmb")

    rt.close()

    fe = ExitStack()
    act_sb = sb(fe, "act_sb", (128, IC, T), BF16)
    wg_sb = sb(fe, "wg_sb", (128, 2, KD, 128), BF16)
    wu_sb = sb(fe, "wu_sb", (128, 2, KD, 128), BF16)
    wd_sb = sb(fe, "wd_sb", (128, 2, IC, 128), BF16)
    sil = sb(fe, "sil", (128, 2, 512), BF16)

    P.phase("F")
    g.wait("cmb", cnt["cmb"])
    for e_i in range(NE):
        ep = e_i % 2
        if e_i < 16:
            g.wait_gate(gate.get("crow"))
            dma(crow[:], comb_dram[e_i:e_i + 1, :], "cr0")
            cr_done = cnt["cr0"]
            src = lambda tq: crow[0:1, tq * 512:(tq + 1) * 512]
            t.wait("cr0", cr_done)
        else:
            src = lambda tq: sg_row[0:1, tq * 512:(tq + 1) * 512]
        t.wait("dve", cnt["dve"])
        if e_i == 0:
            t.wait_gate(gate.get("ps_sc"))
        for tq in range(4):
            cp = tq % 2
            t.wait_gate(gate.get(f"ps{cp}"))
            t.op(lambda e, tq=tq, cp=cp, src=src: e.matmul(
                ps[cp][:], ones_rb[:], src(tq), start=True, stop=True, skip_group_check=True), inc="pe")
            pe_cb = cnt["pe"]
            if e_i < 16:
                gate["crow"] = ("pe", pe_cb)
            a_cb = s.do(lambda e, tq=tq, cp=cp: e.copy(
                out=combb[:, tq * 512:(tq + 1) * 512], in_=ps[cp][:]),
                extra_waits=[("pe", pe_cb)])
            gate[f"ps{cp}"] = ("act", a_cb)
        a_cbl = cnt["act"]

        for i in range(IC):
            ip = i % 2
            gsem, usem = f"wg{ip}", f"wu{ip}"
            g.wait_gate(gate.get(("wg", ip)))
            dma(wg_sb[:, ip], dr["wg_all"][e_i, i], gsem)
            g.wait_gate(gate.get(("wu", ip)))
            dma(wu_sb[:, ip], dr["wu_all"][e_i, i], usem)
            g_done, u_done = cnt[gsem], cnt[usem]
            t.wait(gsem, g_done)
            t.wait(usem, u_done)
            for tq in range(4):
                tp = tq % 2
                t.wait_gate(gate.get(f"ps{tp}"))
                if i == 0 and tq < 2:
                    t.wait("act", a_cbl)
                for kd in range(KD):
                    t.op(lambda e, kd=kd, ip=ip, tq=tq, tp=tp: e.matmul(
                        ps[tp][:], wg_sb[:, ip, kd, :],
                        x_res[:, kd, tq * 512:(tq + 1) * 512],
                        start=(kd == 0), stop=(kd == KD - 1), skip_group_check=True),
                        inc="pe" if kd == KD - 1 else None)
                pe_g = cnt["pe"]
                t.wait_gate(gate.get(("psu", tp)))
                for kd in range(KD):
                    t.op(lambda e, kd=kd, ip=ip, tq=tq, tp=tp: e.matmul(
                        ps[2 + tp][:], wu_sb[:, ip, kd, :],
                        x_res[:, kd, tq * 512:(tq + 1) * 512],
                        start=(kd == 0), stop=(kd == KD - 1), skip_group_check=True),
                        inc="pe" if kd == KD - 1 else None)
                pe_u = cnt["pe"]
                a_si = s.do(lambda e, tp=tp: e.activation(
                    out=sil[:, tp, :], in_=ps[tp][:], func=AF.Silu),
                    extra_waits=[("pe", pe_g), gate.get(("sil", tp))])
                gate[f"ps{tp}"] = ("act", a_si)
                d_u = v.do(lambda e, tp=tp: e.tensor_mul(
                    sil[:, tp, :], sil[:, tp, :], ps[2 + tp][:]),
                    extra_waits=[("act", a_si), ("pe", pe_u)])
                gate[("psu", tp)] = ("dve", d_u)
                d_sc = v.do(lambda e, i=i, tq=tq, tp=tp: e.tensor_mul(
                    act_sb[:, i, tq * 512:(tq + 1) * 512], sil[:, tp, :],
                    combb[:, tq * 512:(tq + 1) * 512]),
                    extra_waits=[("act", a_cbl)] if i == 0 else ())
                gate[("sil", tp)] = ("dve", d_sc)
            gate[("wg", ip)] = ("pe", pe_g)
            gate[("wu", ip)] = ("pe", pe_u)
        dv_A = cnt["dve"]

        first_b = True
        for dc in range(KD):
            dp = dc % 2
            dsem = f"wd{dp}"
            g.wait_gate(gate.get(("wd", dp)))
            dma(wd_sb[:, dp], dr["wd_all"][e_i, dc], dsem)
            d_done = cnt[dsem]
            t.wait(dsem, d_done)
            if first_b:
                t.wait("dve", dv_A)
                first_b = False
            for tq in range(4):
                yp = 4 + ((dc * 4 + tq) % 2)
                t.wait_gate(gate.get(f"ps{yp}"))
                for i in range(IC):
                    t.op(lambda e, i=i, dp=dp, tq=tq, yp=yp: e.matmul(
                        ps[yp][:], wd_sb[:, dp, i, :],
                        act_sb[:, i, tq * 512:(tq + 1) * 512],
                        start=(i == 0), stop=(i == IC - 1), skip_group_check=True),
                        inc="pe" if i == IC - 1 else None)
                pe_y = cnt["pe"]
                d_y = v.do(lambda e, dc=dc, tq=tq, yp=yp: e.tensor_add(
                    racc[:, dc, tq * 512:(tq + 1) * 512],
                    racc[:, dc, tq * 512:(tq + 1) * 512], ps[yp][:]),
                    extra_waits=[("pe", pe_y)])
                gate[f"ps{yp}"] = ("dve", d_y)
            gate[("wd", dp)] = ("pe", pe_y)

    fe.close()

    epi = ExitStack()
    identb = sb(epi, "identb_t", (128, 128), BF16)
    ytok = sb(epi, "ytok", (128, DM), BF16)
    ysq = sb(epi, "ysq", (128, DM), BF16)
    yout = sb(epi, "yout", (128, 2, DM), F32)
    cols = sb(epi, "cols", (128, 8), F32)

    P.phase("EPI")
    dma(identb[:], dr["identb"], "p0")
    dv_racc_done = cnt["dve"]
    t.wait("dve", dv_racc_done)
    t.wait("p0", cnt["p0"])
    last_dv_y = 0
    for tc in range(16):
        op = tc % 2
        osem = f"out{op}"
        for dc in range(KD):
            tp = dc % 2
            t.wait_gate(gate.get(("pst", tp)))
            t.op(lambda e, dc=dc, tc=tc, tp=tp: e.transpose(
                pst[tp][:], racc[:, dc, tc * 128:(tc + 1) * 128], identb[:]),
                inc="pe")
            pe_t = cnt["pe"]
            a_yt = s.do(lambda e, dc=dc, tp=tp: e.copy(
                out=ytok[:, dc * 128:(dc + 1) * 128], in_=pst[tp][:]),
                extra_waits=[("pe", pe_t)] +
                    ([("dve", last_dv_y)] if dc == 0 and tc > 0 else []))
            gate[("pst", tp)] = ("act", a_yt)
        a_ytl = cnt["act"]
        v.do(lambda e: e.tensor_reduce(cols[:, 0:1], ytok[:], AX, ALU.add),
             extra_waits=[("act", a_ytl)])
        v.do(lambda e: e.tensor_scalar_mul(cols[:, 0:1], cols[:, 0:1], 1.0 / DM))
        a_q = s.do(lambda e: e.activation(out=ysq[:], in_=ytok[:], func=AF.Square),
                   extra_waits=[("dve", cnt["dve"])])
        v.do(lambda e: e.tensor_reduce(cols[:, 1:2], ysq[:], AX, ALU.add),
             extra_waits=[("act", a_q)])
        v.do(lambda e: e.tensor_scalar_mul(cols[:, 1:2], cols[:, 1:2], 1.0 / DM))
        v.do(lambda e: e.tensor_mul(cols[:, 2:3], cols[:, 0:1], cols[:, 0:1]))
        v.do(lambda e: e.tensor_sub(cols[:, 3:4], cols[:, 1:2], cols[:, 2:3]))
        dv_v = v.do(lambda e: e.tensor_scalar_add(cols[:, 3:4], cols[:, 3:4], EPS))
        s.do(lambda e: e.activation(out=cols[:, 4:5], in_=cols[:, 3:4],
                                    func=AF.Ln),
             extra_waits=[("dve", dv_v)])
        a_r = s.do(lambda e: e.activation(out=cols[:, 5:6], in_=cols[:, 4:5],
                                          func=AF.Exp, scale=-0.5))
        dv_y = v.do(lambda e, op=op: e.tensor_scalar(
            out=yout[:, op, :], in0=ytok[:], scalar1=cols[:, 0:1],
            scalar2=cols[:, 5:6], op0=ALU.subtract, op1=ALU.mult),
            extra_waits=[("act", a_r), (osem, cnt[osem])])
        last_dv_y = dv_y
        yq.wait("dve", dv_y)
        yq.op(lambda e, tc=tc, op=op: e.dma_start(
            out=y[tc * 128:(tc + 1) * 128, :], in_=yout[:, op, :]),
            inc=osem, n=16)
    yq.wait("out0", cnt["out0"])
    yq.wait("out1", cnt["out1"])

    epi.close()
    ffn.close()
    P.replay()
    es.close()
    return nc


_NC_CACHE = None


def kernel(**inputs):
    global _NC_CACHE
    hp = host_prep(inputs)
    if _NC_CACHE is None:
        _NC_CACHE = build_nc()
    nc = _NC_CACHE
    shared = {k: hp[k] for k in ["wqt", "wkt", "wvt", "wot", "routert", "shgw",
                                 "wg_all", "wu_all", "wd_all", "maskR",
                                 "pswap", "identb", "identf"]}
    in_maps = []
    for c in range(NCORES):
        m = dict(shared)
        m["xm"] = hp["xm"][c]
        m["xh"] = hp["xh"][c]
        m["c2"] = hp["c2"][c]
        m["s2"] = hp["s2"][c]
        m["maskA0"] = hp["maskA0"] if c == 0 else hp["maskR"]
        in_maps.append(m)
    res = run_bass_kernel_spmd(nc, in_maps, core_ids=list(range(NCORES)))
    out = np.concatenate([res.results[c]["y"] for c in range(NCORES)], axis=0)
    return out.reshape(1, NCORES * T, DM)


# version 05
# speedup vs baseline: 1.0346x; 1.0346x over previous
import math
import numpy as np
import ml_dtypes
from contextlib import ExitStack

import concourse.bass as bass
import concourse.mybir as mybir
from concourse.bass_utils import run_bass_kernel_spmd

F32 = mybir.dt.float32
BF16 = mybir.dt.bfloat16
AF = mybir.ActivationFunctionType
ALU = mybir.AluOpType
AX = mybir.AxisListType.X

NCORES = 8
T = 2048
W = 512
NB = 4
D = 128
H = 16
HK = 4
DM = 2048
KD = 16
NE = 20
IC = 11
CAP = 20.0
EPS = 1e-5
INV_SQRT_D = 1.0 / math.sqrt(D)

BF = ml_dtypes.bfloat16



class Prog:

    def __init__(self, nc, es):
        self.nc = nc
        self.cnt = {}
        self.sems = {}
        self.phases = []
        self.cur = None

    def mksem(self, es, name):
        self.sems[name] = es.enter_context(self.nc.semaphore(name))
        self.cnt[name] = 0

    def phase(self, name):
        self.cur = {"name": name, "t": [], "v": [], "s": [], "g": [], "y": []}
        self.phases.append(self.cur)

    def replay(self):
        for ph in self.phases:
            with self.nc.Block() as blk:
                if ph["g"]:
                    @blk.gpsimd
                    def _(g, ops=ph["g"]):
                        for f in ops:
                            f(g)
                if ph["t"]:
                    @blk.tensor
                    def _(t, ops=ph["t"]):
                        for f in ops:
                            f(t)
                if ph["s"]:
                    @blk.scalar
                    def _(s, ops=ph["s"]):
                        for f in ops:
                            f(s)
                if ph["v"]:
                    @blk.vector
                    def _(v, ops=ph["v"]):
                        for f in ops:
                            f(v)
                if ph["y"]:
                    @blk.sync
                    def _(y, ops=ph["y"]):
                        for f in ops:
                            f(y)


class ES_:
    def __init__(self, prog, key, own=None):
        self.p = prog
        self.key = key
        self.own = own

    def raw(self, fn):
        self.p.cur[self.key].append(fn)

    def wait(self, sem, val):
        if val is None or val <= 0:
            return
        self.raw(lambda eng, s=self.p.sems[sem], v=val: eng.wait_ge(s, v))

    def wait_gate(self, gate):
        if gate:
            self.wait(gate[0], gate[1])

    def op(self, fn, inc=None, n=1):
        if inc is None:
            self.raw(lambda eng, f=fn: f(eng))
            return None
        self.p.cnt[inc] += n
        tgt = self.p.cnt[inc]
        self.raw(lambda eng, f=fn, s=self.p.sems[inc], m=n: f(eng).then_inc(s, m))
        return tgt

    def do(self, fn, extra_waits=()):
        for sv in extra_waits:
            if sv:
                self.wait(sv[0], sv[1])
        self.wait(self.own, self.p.cnt[self.own])
        return self.op(fn, inc=self.own)



def _tile_lhsT(wT, oc_chunks):
    din, dout = wT.shape
    assert din == DM and dout == oc_chunks * 128
    return np.ascontiguousarray(
        wT.reshape(KD, 128, oc_chunks, 128).transpose(2, 1, 0, 3))


def host_prep(inputs):
    d = {}
    x = np.asarray(inputs["L0_attn_x"])[0]
    xT = np.ascontiguousarray(x.T)
    xT4 = xT.reshape(KD, 128, NCORES * T)
    xm, xh = [], []
    for c in range(NCORES):
        xm.append(np.ascontiguousarray(
            xT4[:, :, c * T:(c + 1) * T].transpose(1, 0, 2)))
        h0 = c * T - W if c > 0 else 0
        xh.append(np.ascontiguousarray(
            xT4[:, :, h0:h0 + W].transpose(1, 0, 2)))
    d["xm"], d["xh"] = xm, xh

    d["wqt"] = _tile_lhsT(np.ascontiguousarray(np.asarray(inputs["L0_attn_wq"]).T), 16)
    d["wkt"] = _tile_lhsT(np.ascontiguousarray(np.asarray(inputs["L0_attn_wk"]).T), 4)
    d["wot"] = _tile_lhsT(np.ascontiguousarray(np.asarray(inputs["L0_attn_wo"]).T), 16)
    d["wvt"] = np.ascontiguousarray(
        np.ascontiguousarray(np.asarray(inputs["L0_attn_wv"]).T)
        .reshape(KD, 128, 512).transpose(1, 0, 2))
    d["routert"] = np.ascontiguousarray(
        np.ascontiguousarray(np.asarray(inputs["L0_ffn_router_w"]).T)
        .reshape(KD, 128, 16).transpose(1, 0, 2))
    d["shgw"] = np.ascontiguousarray(
        np.asarray(inputs["L0_ffn_sh_gate_w"]).reshape(KD, 128, 1).transpose(1, 0, 2))

    wg = np.asarray(inputs["L0_ffn_w_gate"])
    wu = np.asarray(inputs["L0_ffn_w_up"])
    wd = np.asarray(inputs["L0_ffn_w_down"])
    shg = np.asarray(inputs["L0_ffn_sh_gate"])
    shu = np.asarray(inputs["L0_ffn_sh_up"])
    shd = np.asarray(inputs["L0_ffn_sh_down"])

    def tile_gu(m):
        return m.reshape(KD, 128, IC, 128).transpose(2, 1, 0, 3)

    def tile_dn(m):
        return m.reshape(IC, 128, KD, 128).transpose(2, 1, 0, 3)

    wg_all = np.empty((NE, IC, 128, KD, 128), np.float32)
    wu_all = np.empty((NE, IC, 128, KD, 128), np.float32)
    wd_all = np.empty((NE, KD, 128, IC, 128), np.float32)
    for e in range(16):
        wg_all[e] = tile_gu(wg[e])
        wu_all[e] = tile_gu(wu[e])
        wd_all[e] = tile_dn(wd[e])
    for sx in range(4):
        wg_all[16 + sx] = tile_gu(shg[:, sx * 1408:(sx + 1) * 1408])
        wu_all[16 + sx] = tile_gu(shu[:, sx * 1408:(sx + 1) * 1408])
        wd_all[16 + sx] = tile_dn(shd[sx * 1408:(sx + 1) * 1408])
    d["wg_all"], d["wu_all"], d["wd_all"] = wg_all, wu_all, wd_all

    inv = 10000.0 ** (-np.arange(0, D, 2, dtype=np.float64) / D)
    c2s, s2s = [], []
    for c in range(NCORES):
        pos = np.arange(c * T - W, c * T + T, dtype=np.float64)
        ang = np.outer(pos, inv)
        cosv = np.cos(ang).T.astype(np.float32)
        sinv = np.sin(ang).T.astype(np.float32)
        c2 = np.empty((128, T + W), np.float32)
        s2 = np.empty((128, T + W), np.float32)
        c2[0::2] = cosv
        c2[1::2] = cosv
        s2[0::2] = -sinv
        s2[1::2] = sinv
        c2s.append(c2.astype(BF))
        s2s.append(s2.astype(BF))
    d["c2"], d["s2"] = c2s, s2s

    j = np.arange(1024)[:, None]
    a = np.arange(W)[None, :]
    mR = ((j < W) & (j >= a)) | ((j >= W) & (j - W <= a))
    mA0 = (j >= W) & (j - W <= a)
    d["maskR"] = np.ascontiguousarray(
        mR.astype(BF).reshape(8, 128, W).transpose(1, 0, 2))
    d["maskA0"] = np.ascontiguousarray(
        mA0.astype(BF).reshape(8, 128, W).transpose(1, 0, 2))

    psw = np.zeros((128, 128), np.float32)
    i2 = np.arange(0, 128, 2)
    psw[i2, i2 + 1] = 1.0
    psw[i2 + 1, i2] = 1.0
    d["pswap"] = psw.astype(BF)
    d["identb"] = np.eye(128).astype(BF)
    d["identf"] = np.eye(128, dtype=np.float32)
    return d



def build_nc():
    nc = bass.Bass("TRN2", target_bir_lowering=False)

    dr = {}
    def din(name, shape, dt=F32):
        dr[name] = nc.dram_tensor(name, list(shape), dt, kind="ExternalInput").ap()
    din("xm", (128, KD, T)); din("xh", (128, KD, W))
    din("wqt", (16, 128, KD, 128)); din("wkt", (4, 128, KD, 128))
    din("wvt", (128, KD, 512)); din("wot", (16, 128, KD, 128))
    din("routert", (128, KD, 16)); din("shgw", (128, KD, 1))
    din("wg_all", (NE, IC, 128, KD, 128)); din("wu_all", (NE, IC, 128, KD, 128))
    din("wd_all", (NE, KD, 128, IC, 128))
    din("c2", (128, T + W), BF16); din("s2", (128, T + W), BF16)
    din("maskR", (128, 8, W), BF16); din("maskA0", (128, 8, W), BF16)
    din("pswap", (128, 128), BF16); din("identb", (128, 128), BF16)
    din("identf", (128, 128))
    y = nc.dram_tensor("y", [T, DM], F32, kind="ExternalOutput").ap()
    comb_dram = nc.dram_tensor("comb_scr", [16, T], F32, kind="Internal").ap()

    es = ExitStack()
    P = Prog(nc, es)
    for sname in ["p0", "wq0", "wq1", "wq2", "wg0", "wg1", "wu0", "wu1", "wd0", "wd1",
                  "cr0", "cr1", "cmb", "pe", "act", "dve", "out0", "out1"]:
        P.mksem(es, sname)
    t = ES_(P, "t")
    v = ES_(P, "v", own="dve")
    s = ES_(P, "s", own="act")
    g = ES_(P, "g")
    yq = ES_(P, "y")
    cnt = P.cnt
    gate = {}

    def sb(stack, name, shape, dt):
        return stack.enter_context(nc.sbuf_tensor(name, list(shape), dt))

    x_res = sb(es, "x_res", (128, KD, T), BF16)
    ones_cb = sb(es, "ones_cb", (128, 1), BF16)
    ones_rb = sb(es, "ones_rb", (1, 128), BF16)

    ps = [es.enter_context(nc.psum_tensor(f"ps{i}", [128, 512], F32))
          for i in range(2)]
    ps_sc = es.enter_context(nc.psum_tensor("ps_sc", [128, 1024], F32))
    ps.append(ps_sc.ap()[:, 0:512])
    ps.append(ps_sc.ap()[:, 512:1024])
    ps += [es.enter_context(nc.psum_tensor(f"ps{i}", [128, 512], F32))
           for i in (4, 5)]
    pst = [es.enter_context(nc.psum_tensor(f"pst{i}", [128, 128], BF16))
           for i in range(2)]

    def dma(ap_out, ap_in, sem):
        return g.op(lambda e, o=ap_out, i=ap_in: e.dma_start(out=o, in_=i),
                    inc=sem, n=16)

    att = ExitStack()
    ones_rf = sb(att, "ones_rf", (1, 128), F32)
    pswap = sb(att, "pswap_t", (128, 128), BF16)
    wvt_sb = sb(att, "wvt_sb", (128, KD, 512), BF16)
    kT = sb(att, "kT", (128, 2, HK, W), BF16)
    v_tok = sb(att, "v_tok", (128, 2, 4, W), BF16)
    c2_sb = sb(att, "c2_sb", (128, W), BF16)
    s2_sb = sb(att, "s2_sb", (128, W), BF16)
    mask_sb = sb(att, "mask_sb", (128, 8, W), BF16)
    wqo_sb = sb(att, "wqo_sb", (128, 3, KD, 128), BF16)
    qT = sb(att, "qT", (128, 2, W), BF16)
    q_raw = sb(att, "q_raw", (128, 2, W), BF16)
    rtmp1 = sb(att, "rtmp1", (128, W), F32)
    rtmp2 = sb(att, "rtmp2", (128, W), F32)
    expT = sb(att, "expT", (128, 2, 8, W), BF16)
    etmp = sb(att, "etmp", (128, 2, 2 * W), BF16)
    ttmp = sb(att, "ttmp", (128, 2 * W), F32)
    rb_sb = sb(att, "rb_sb", (128, 2, W), F32)
    recip_row = sb(att, "recip_row", (1, 2, W), F32)
    attn_out = sb(att, "attn_out", (128, KD, W), BF16)
    s1_pp = sb(att, "s1_pp", (128, 2, W), BF16)
    sq_pp = sb(att, "sq_pp", (128, 2, W), BF16)
    lnrow = sb(att, "lnrow", (1, 3, W), F32)
    mb_sb = sb(att, "mb_sb", (128, W), F32)
    rb2_sb = sb(att, "rb2_sb", (128, W), F32)
    ln_t = sb(att, "ln_t", (128, W), F32)

    halo = ExitStack()
    x_halo = sb(halo, "x_halo", (128, KD, W), BF16)

    def x_blk(n):
        return [x_res[:, kd, n * W:(n + 1) * W] for kd in range(KD)]

    wplan = {"aps": [], "next": 0, "cnt": {}}

    def wreset(aps):
        wplan["aps"] = aps
        wplan["next"] = 0
        wplan["cnt"] = {}

    def wpump(upto):
        while wplan["next"] < min(upto, len(wplan["aps"])):
            i = wplan["next"]
            slot = i % 3
            g.wait_gate(gate.get(("wqo", slot)))
            dma(wqo_sb[:, slot], wplan["aps"][i], f"wq{slot}")
            wplan["cnt"][i] = cnt[f"wq{slot}"]
            wplan["next"] += 1

    def wwait(i):
        t.wait(f"wq{i % 3}", wplan["cnt"][i])

    def wdone(i, pe_val):
        gate[("wqo", i % 3)] = ("pe", pe_val)
        wpump(i + 4)

    def kv_block(n, x_src, widx0):
        slot = n % 2
        for kvh in range(HK):
            wi = widx0 + kvh
            sp3 = wi % 3
            wwait(wi)
            t.wait_gate(gate.get("ps0"))
            t.wait_gate(gate.get(("qraw_rd", kvh % 2)))
            for kd in range(KD):
                t.op(lambda e, kd=kd, sp3=sp3, xs=x_src: e.matmul(
                    ps[0][:], wqo_sb[:, sp3, kd, :], xs[kd],
                    start=(kd == 0), stop=(kd == KD - 1), skip_group_check=True),
                    inc="pe" if kd == KD - 1 else None)
            pe_k = cnt["pe"]
            wdone(wi, pe_k)
            pp = kvh % 2
            a_ev = s.do(lambda e, pp=pp: e.copy(out=q_raw[:, pp, :], in_=ps[0][:]),
                        extra_waits=[("pe", pe_k), ("dve", cnt["dve"])])
            gate["ps0"] = ("act", a_ev)
            t.wait("act", a_ev)
            t.wait_gate(gate.get("ps1"))
            t.op(lambda e, pp=pp: e.matmul(ps[1][:], pswap[:], q_raw[:, pp, :],
                                           start=True, stop=True, skip_group_check=True), inc="pe")
            pe_sh = cnt["pe"]
            v.do(lambda e, pp=pp: e.tensor_mul(
                rtmp1[:], q_raw[:, pp, :], c2_sb[:]),
                extra_waits=[("act", a_ev), ("p0", cnt["p0"])])
            d2 = v.do(lambda e: e.tensor_mul(
                rtmp2[:], ps[1][:], s2_sb[:]),
                extra_waits=[("pe", pe_sh)])
            gate["ps1"] = ("dve", d2)
            dk = v.do(lambda e, slot=slot, kvh=kvh: e.tensor_add(
                kT[:, slot, kvh, :], rtmp1[:], rtmp2[:]))
            gate[("qraw_rd", pp)] = ("dve", dk)
        for jc in range(4):
            pp = jc % 2
            if pp == 0:
                t.wait_gate(gate.get("ps_sc"))
            for kd in range(KD):
                t.op(lambda e, kd=kd, jc=jc, pp=pp, xs=x_src: e.matmul(
                    ps[2 + pp][:], xs[kd][:, jc * 128:(jc + 1) * 128],
                    wvt_sb[:, kd, :],
                    start=(kd == 0), stop=(kd == KD - 1), skip_group_check=True),
                    inc="pe" if kd == KD - 1 else None)
            pe_v = cnt["pe"]
            a_v = s.do(lambda e, slot=slot, jc=jc, pp=pp: e.copy(
                out=v_tok[:, slot, jc, :], in_=ps[2 + pp][:]),
                extra_waits=[("pe", pe_v)])
            gate["ps_sc"] = ("act", a_v)

    P.phase("P0")
    for kq in range(4):
        dma(x_res[:, kq * 4:(kq + 1) * 4, :], dr["xm"][:, kq * 4:(kq + 1) * 4, :], "p0")
    dma(x_halo[:], dr["xh"], "p0")
    dma(wvt_sb[:], dr["wvt"], "p0")
    dma(c2_sb[:], dr["c2"][:, 0:W], "p0")
    dma(s2_sb[:], dr["s2"][:, 0:W], "p0")
    dma(mask_sb[:], dr["maskA0"], "p0")
    dma(pswap[:], dr["pswap"], "p0")
    p0_total = cnt["p0"]
    v.do(lambda e: e.memset(ones_cb[:], 1.0))
    v.do(lambda e: e.memset(ones_rf[:], 1.0))
    v.do(lambda e: e.memset(ones_rb[:], 1.0))
    t.wait("p0", p0_total)
    t.wait("dve", cnt["dve"])
    full_plan = [dr["wkt"][kvh] for kvh in range(HK)]
    for _n in range(NB):
        full_plan += [dr["wkt"][kvh] for kvh in range(HK)]
        full_plan += [dr["wqt"][h] for h in range(H)]
        full_plan += [dr["wot"][dc] for dc in range(KD)]
    wreset(full_plan)
    wpump(3)
    kv_block(-1, [x_halo[:, kd, :] for kd in range(KD)], 0)

    P.phase("ATT")

    def emit_qproj(n, h, xb, wbase0):
        hp = h % 2
        wi = wbase0 + 4 + h
        sp3 = wi % 3
        wwait(wi)
        t.wait_gate(gate.get("ps0"))
        for kd in range(KD):
            t.op(lambda e, kd=kd, sp3=sp3, xb=xb: e.matmul(
                ps[0][:], wqo_sb[:, sp3, kd, :], xb[kd],
                start=(kd == 0), stop=(kd == KD - 1), skip_group_check=True),
                inc="pe" if kd == KD - 1 else None)
        pe_q = cnt["pe"]
        wdone(wi, pe_q)
        a_ev = s.do(lambda e, hp=hp: e.copy(out=q_raw[:, hp, :], in_=ps[0][:]),
                    extra_waits=[("pe", pe_q), ("dve", cnt["dve"])])
        gate["ps0"] = ("act", a_ev)
        t.wait("act", a_ev)
        t.wait_gate(gate.get("ps1"))
        t.op(lambda e, hp=hp: e.matmul(ps[1][:], pswap[:], q_raw[:, hp, :],
                                       start=True, stop=True, skip_group_check=True), inc="pe")
        pe_sh = cnt["pe"]
        v.do(lambda e, hp=hp: e.tensor_mul(
            rtmp1[:], q_raw[:, hp, :], c2_sb[:]),
            extra_waits=[("act", a_ev), ("p0", cnt["p0"])])
        d2 = v.do(lambda e: e.tensor_mul(
            rtmp2[:], ps[1][:], s2_sb[:]),
            extra_waits=[("pe", pe_sh)])
        gate["ps1"] = ("dve", d2)
        dv_q = v.do(lambda e, hp=hp: e.tensor_add(
            qT[:, hp, :], rtmp1[:], rtmp2[:]),
            extra_waits=[gate[("qT", hp)]] if ("qT", hp) in gate else ())
        gate[("qraw_rd", hp)] = ("dve", dv_q)
        return dv_q

    def emit_scores(n, h, dv_q, msk):
        kvh = h // 4
        hp = h % 2
        slot, pslot = n % 2, (n - 1) % 2
        t.wait("dve", dv_q)
        dms = []
        for jp in range(4):
            jc0 = 2 * jp
            pv = jp % 2
            t.wait_gate(gate.get("ps_sc"))
            for half in range(2):
                jc = jc0 + half
                sl = pslot if jc < 4 else slot
                jj = jc % 4
                t.op(lambda e, sl=sl, kvh=kvh, jj=jj, hp=hp, half=half: e.matmul(
                    ps[2 + half][:], kT[:, sl, kvh, jj * 128:(jj + 1) * 128],
                    qT[:, hp, :], start=True, stop=True, skip_group_check=True),
                    inc="pe" if half == 1 else None)
            pe_sc = cnt["pe"]
            a_t = s.do(lambda e: e.activation(
                out=ttmp[:], in_=ps_sc[:], func=AF.Tanh,
                scale=INV_SQRT_D / CAP), extra_waits=[("pe", pe_sc)])
            gate["ps_sc"] = ("act", a_t)
            a_e = s.do(lambda e, pv=pv: e.activation(
                out=etmp[:, pv, :], in_=ttmp[:], func=AF.Exp,
                scale=CAP), extra_waits=[gate.get(("etmp", pv))])
            d_m = v.do(lambda e, jc0=jc0, hp=hp, pv=pv, msk=msk: e.tensor_mul(
                expT[:, hp, jc0:jc0 + 2, :], etmp[:, pv, :].rearrange("p (a b) -> p a b", a=2),
                msk[:, jc0:jc0 + 2, :]),
                extra_waits=[("act", a_e), ("p0", cnt["p0"])] +
                    ([gate[("expT", hp)]] if jp == 0 and ("expT", hp) in gate
                     else []))
            gate[("etmp", pv)] = ("dve", d_m)
            dms.append((jc0, d_m))
            dms.append((jc0 + 1, d_m))
        return dms

    def emit_sumsout(n, h, dms):
        kvh = h // 4
        hp = h % 2
        slot, pslot = n % 2, (n - 1) % 2
        first = True
        for jc, d_m in dms:
            sl = pslot if jc < 4 else slot
            jj = jc % 4
            t.wait("dve", d_m)
            t.op(lambda e, jc=jc, hp=hp, st=first: e.matmul(
                ps[4][0:1, :], ones_cb[:], expT[:, hp, jc, :],
                start=st, stop=(jc == 7), skip_group_check=True), inc="pe")
            t.op(lambda e, sl=sl, jj=jj, kvh=kvh, jc=jc, hp=hp, st=first: e.matmul(
                ps[5][:], v_tok[:, sl, jj, kvh * 128:(kvh + 1) * 128],
                expT[:, hp, jc, :], start=st, stop=(jc == 7), skip_group_check=True),
                inc="pe" if jc == 7 else None)
            first = False
        pe_att = cnt["pe"]
        gate[("expT", hp)] = ("pe", pe_att)
        gate[("qT", hp)] = ("pe", pe_att)
        d_r = v.do(lambda e, hp=hp: e.reciprocal(
            out=recip_row[:, hp, :], in_=ps[4][0:1, :]),
            extra_waits=[("pe", pe_att)])
        t.wait("dve", d_r)
        t.wait_gate(gate.get("ps0"))
        t.op(lambda e, hp=hp: e.matmul(ps[0][:], ones_rf[:],
                                       recip_row[:, hp, :],
                                       start=True, stop=True, skip_group_check=True), inc="pe")
        pe_rb = cnt["pe"]
        a_rb = s.do(lambda e, hp=hp: e.copy(out=rb_sb[:, hp, :], in_=ps[0][:]),
                    extra_waits=[("pe", pe_rb)])
        gate["ps0"] = ("act", a_rb)
        d_ao = v.do(lambda e, h=h, hp=hp: e.tensor_mul(
            attn_out[:, h, :], ps[5][:], rb_sb[:, hp, :]),
            extra_waits=[("act", a_rb)])
        gate["ps5"] = ("dve", d_ao)
        return pe_att

    for n in range(NB):
        slot, pslot = n % 2, (n - 1) % 2
        tab0 = W + n * W
        xb = x_blk(n)
        g.wait_gate(gate.get("ropetab"))
        dma(c2_sb[:], dr["c2"][:, tab0:tab0 + W], "p0")
        dma(s2_sb[:], dr["s2"][:, tab0:tab0 + W], "p0")
        if n == 1:
            g.wait_gate(gate.get("maskrd"))
            dma(mask_sb[:], dr["maskR"], "p0")
        t.wait_gate(gate.get(("kvslot", slot)))
        wbase = 4 + 36 * n
        kv_block(n, xb, wbase)
        msk = mask_sb

        dv_q = emit_qproj(n, 0, xb, wbase)
        for h in range(H):
            dms = emit_scores(n, h, dv_q, msk)
            if h + 1 < H:
                dv_q = emit_qproj(n, h + 1, xb, wbase)
            pe_att = emit_sumsout(n, h, dms)
        gate[("kvslot", pslot)] = ("pe", pe_att)
        gate["ropetab"] = ("dve", cnt["dve"])
        gate["maskrd"] = ("dve", cnt["dve"])

        dv_attn_done = cnt["dve"]
        first = True
        for dc in range(KD):
            dp = dc % 2
            wi = wbase + 20 + dc
            sp3 = wi % 3
            wwait(wi)
            if dc == 0:
                t.wait("dve", dv_attn_done)
            t.wait_gate(gate.get(f"pso{dp}"))
            for kd in range(KD):
                t.op(lambda e, kd=kd, sp3=sp3, dp=dp: e.matmul(
                    ps[dp][:], wqo_sb[:, sp3, kd, :], attn_out[:, kd, :],
                    start=(kd == 0), stop=(kd == KD - 1), skip_group_check=True),
                    inc="pe" if kd == KD - 1 else None)
            pe_o = cnt["pe"]
            wdone(wi, pe_o)
            a_o = s.do(lambda e, dc=dc, dp=dp: e.copy(
                out=expT[:, dc // 8, dc % 8, :], in_=ps[dp][:]),
                extra_waits=[("pe", pe_o)])
            gate[f"pso{dp}"] = ("act", a_o)
            d_s1 = v.do(lambda e, dc=dc, n=n: e.tensor_add(
                s1_pp[:, dc % 2, :], x_res[:, dc, n * W:(n + 1) * W],
                expT[:, dc // 8, dc % 8, :]), extra_waits=[("act", a_o)])
            a_sq = s.do(lambda e, dc=dc: e.activation(
                out=sq_pp[:, dc % 2, :], in_=s1_pp[:, dc % 2, :], func=AF.Square),
                extra_waits=[("dve", d_s1)])
            t.wait("dve", d_s1)
            if dc == 0:
                t.wait_gate(gate.get("ps_sc"))
            t.op(lambda e, dc=dc, st=first: e.matmul(
                ps[2][0:1, :], ones_cb[:], s1_pp[:, dc % 2, :],
                start=st, stop=(dc == KD - 1), skip_group_check=True), inc="pe")
            t.wait("act", a_sq)
            t.op(lambda e, dc=dc, st=first: e.matmul(
                ps[3][0:1, :], ones_cb[:], sq_pp[:, dc % 2, :],
                start=st, stop=(dc == KD - 1), skip_group_check=True),
                inc="pe" if dc == KD - 1 else None)
            first = False
        pe_stats = cnt["pe"]
        wpump(wbase + 36 + 3)
        v.do(lambda e: e.tensor_scalar_mul(lnrow[:, 0, :], ps[2][0:1, :], 1.0 / DM),
             extra_waits=[("pe", pe_stats)])
        v.do(lambda e: e.tensor_scalar_mul(lnrow[:, 1, :], ps[3][0:1, :], 1.0 / DM))
        v.do(lambda e: e.tensor_mul(lnrow[:, 2, :], lnrow[:, 0, :], lnrow[:, 0, :]))
        v.do(lambda e: e.tensor_sub(lnrow[:, 1, :], lnrow[:, 1, :],
                                    lnrow[:, 2, :]))
        dv_var = v.do(lambda e: e.tensor_scalar_add(lnrow[:, 1, :],
                                                    lnrow[:, 1, :], EPS))
        gate["ps_sc"] = ("dve", dv_var)
        s.do(lambda e: e.activation(out=lnrow[:, 2, :], in_=lnrow[:, 1, :],
                                    func=AF.Ln),
             extra_waits=[("dve", dv_var)])
        a_rs = s.do(lambda e: e.activation(out=lnrow[:, 1, :], in_=lnrow[:, 2, :],
                                           func=AF.Exp, scale=-0.5))
        t.wait("act", a_rs)
        t.wait_gate(gate.get("ps4"))
        t.op(lambda e: e.matmul(ps[4][:], ones_rf[:], lnrow[:, 0, :],
                                start=True, stop=True, skip_group_check=True), inc="pe")
        t.wait_gate(gate.get("ps5"))
        t.op(lambda e: e.matmul(ps[5][:], ones_rf[:], lnrow[:, 1, :],
                                start=True, stop=True, skip_group_check=True), inc="pe")
        pe_bc = cnt["pe"]
        a_mb = s.do(lambda e: e.copy(out=mb_sb[:], in_=ps[4][:]),
                    extra_waits=[("pe", pe_bc)])
        a_rb2 = s.do(lambda e: e.copy(out=rb2_sb[:], in_=ps[5][:]))
        gate["ps4"] = ("act", a_mb)
        gate["ps5"] = ("act", a_rb2)
        for dc in range(KD):
            v.do(lambda e, dc=dc, n=n: e.tensor_add(
                s1_pp[:, dc % 2, :], x_res[:, dc, n * W:(n + 1) * W],
                expT[:, dc // 8, dc % 8, :]),
                extra_waits=[("act", a_rb2)] if dc == 0 else ())
            v.do(lambda e, dc=dc: e.tensor_sub(ln_t[:], s1_pp[:, dc % 2, :], mb_sb[:]))
            v.do(lambda e, dc=dc, n=n: e.tensor_mul(
                x_res[:, dc, n * W:(n + 1) * W], ln_t[:], rb2_sb[:]))

    halo.close()
    att.close()

    ffn = ExitStack()
    racc = sb(ffn, "racc", (128, KD, T), BF16)
    combb = sb(ffn, "combb", (128, T), BF16)
    crow = sb(ffn, "crow", (1, T), BF16)
    sg_row = sb(ffn, "sg_row", (1, T), BF16)

    rt = ExitStack()
    identf = sb(rt, "identf_t", (128, 128), F32)
    routert_sb = sb(rt, "routert_sb", (128, KD, 16), BF16)
    shgw_sb = sb(rt, "shgw_sb", (128, KD, 1), BF16)
    lg_sb = sb(rt, "lg_sb", (16, 4, 512), F32)
    ltok = sb(rt, "ltok", (128, 16, 16), F32)
    ptmp = sb(rt, "ptmp", (128, 16, 16), F32)
    selacc = sb(rt, "selacc", (128, 16, 16), F32)
    maskt = sb(rt, "maskt", (128, 16, 16), F32)
    comb_tok = sb(rt, "comb_tok", (128, 16, 16), F32)
    mx16 = sb(rt, "mx16", (128, 16), F32)
    ts16 = sb(rt, "ts16", (128, 16), F32)
    comb_d = sb(rt, "comb_d", (16, T), F32)

    P.phase("R")
    dma(identf[:], dr["identf"], "p0")
    dma(routert_sb[:], dr["routert"], "p0")
    dma(shgw_sb[:], dr["shgw"], "p0")
    p0r = cnt["p0"]
    dv_ln_done = cnt["dve"]
    for kq in range(4):
        s.do(lambda e, kq=kq: e.copy(out=racc[:, kq * 4:(kq + 1) * 4, :],
                                     in_=x_res[:, kq * 4:(kq + 1) * 4, :]),
             extra_waits=[("dve", dv_ln_done)] if kq == 0 else ())
    t.wait("p0", p0r)
    for tq in range(4):
        t.wait_gate(gate.get("ps0"))
        for kd in range(KD):
            t.op(lambda e, kd=kd, tq=tq: e.matmul(
                ps[0][0:16, :], routert_sb[:, kd, :],
                x_res[:, kd, tq * 512:(tq + 1) * 512],
                start=(kd == 0), stop=(kd == KD - 1), skip_group_check=True),
                inc="pe" if kd == KD - 1 else None)
        pe_lg = cnt["pe"]
        a_lg = s.do(lambda e, tq=tq: e.copy(out=lg_sb[:, tq, :], in_=ps[0][0:16, :]),
                    extra_waits=[("pe", pe_lg)])
        gate["ps0"] = ("act", a_lg)
        t.wait_gate(gate.get("ps_sc"))
        for kd in range(KD):
            t.op(lambda e, kd=kd, tq=tq: e.matmul(
                ps[3][0:1, :], shgw_sb[:, kd, :],
                x_res[:, kd, tq * 512:(tq + 1) * 512],
                start=(kd == 0), stop=(kd == KD - 1), skip_group_check=True),
                inc="pe" if kd == KD - 1 else None)
        pe_sg = cnt["pe"]
        a_sg = s.do(lambda e, tq=tq: e.activation(
            out=sg_row[:, tq * 512:(tq + 1) * 512], in_=ps[3][0:1, :],
            func=AF.Sigmoid), extra_waits=[("pe", pe_sg)])
        gate["ps_sc"] = ("act", a_sg)
    for tc in range(16):
        t.wait_gate(gate.get("ps1"))
        t.op(lambda e, tc=tc: e.transpose(
            ps[1][:, 0:16], lg_sb[:, tc // 4, (tc % 4) * 128:(tc % 4 + 1) * 128],
            identf[0:16, 0:16]), inc="pe")
        pe_tr = cnt["pe"]
        a_ltk = s.do(lambda e, tc=tc: e.copy(out=ltok[:, tc, :], in_=ps[1][:, 0:16]),
                     extra_waits=[("pe", pe_tr)])
        gate["ps1"] = ("act", a_ltk)
    a_lt = cnt["act"]
    bc16 = lambda ap: ap.unsqueeze(2).broadcast_to([128, 16, 16])
    v.do(lambda e: e.tensor_reduce(mx16[:], ltok[:], AX, ALU.max),
         extra_waits=[("act", a_lt)])
    v.do(lambda e: e.tensor_tensor(ptmp[:], ltok[:], bc16(mx16[:]), ALU.subtract))
    a_ex = s.do(lambda e: e.activation(out=ptmp[:], in_=ptmp[:], func=AF.Exp),
                extra_waits=[("dve", cnt["dve"])])
    v.do(lambda e: e.tensor_reduce(mx16[:], ptmp[:], AX, ALU.add),
         extra_waits=[("act", a_ex)])
    v.do(lambda e: e.reciprocal(out=ts16[:], in_=mx16[:]))
    v.do(lambda e: e.tensor_tensor(ptmp[:], ptmp[:], bc16(ts16[:]), ALU.mult))
    v.do(lambda e: e.memset(selacc[:], 0.0))
    v.do(lambda e: e.memset(ts16[:], 0.0))
    for r in range(4):
        v.do(lambda e: e.tensor_reduce(mx16[:], ptmp[:], AX, ALU.max))
        v.do(lambda e: e.tensor_tensor(maskt[:], ptmp[:], bc16(mx16[:]), ALU.is_ge))
        v.do(lambda e: e.tensor_tensor(maskt[:], maskt[:], ptmp[:], ALU.mult))
        v.do(lambda e: e.tensor_add(selacc[:], selacc[:], maskt[:]))
        v.do(lambda e: e.tensor_add(ts16[:], ts16[:], mx16[:]))
        if r < 3:
            v.do(lambda e: e.tensor_sub(ptmp[:], ptmp[:], maskt[:]))
    v.do(lambda e: e.tensor_scalar_max(ts16[:], ts16[:], 1e-9))
    v.do(lambda e: e.reciprocal(out=ts16[:], in_=ts16[:]))
    dv_cmb = v.do(lambda e: e.tensor_tensor(comb_tok[:], selacc[:],
                                            bc16(ts16[:]), ALU.mult))
    t.wait("dve", dv_cmb)
    for tc in range(16):
        t.wait_gate(gate.get("ps_sc"))
        t.op(lambda e, tc=tc: e.transpose(
            ps[2][0:16, 0:128], comb_tok[:, tc, :], identf[:]), inc="pe")
        pe_ct = cnt["pe"]
        a_cd = s.do(lambda e, tc=tc: e.copy(out=comb_d[:, tc * 128:(tc + 1) * 128],
                                            in_=ps[2][0:16, 0:128]),
                    extra_waits=[("pe", pe_ct)])
        gate["ps_sc"] = ("act", a_cd)
    g.wait("act", cnt["act"])
    dma(comb_dram, comb_d[:], "cmb")

    rt.close()

    fe = ExitStack()
    act_sb = sb(fe, "act_sb", (128, IC, T), BF16)
    wg_sb = sb(fe, "wg_sb", (128, 2, KD, 128), BF16)
    wu_sb = sb(fe, "wu_sb", (128, 2, KD, 128), BF16)
    wd_sb = sb(fe, "wd_sb", (128, 2, IC, 128), BF16)
    sil = sb(fe, "sil", (128, 2, 512), BF16)

    P.phase("F")
    g.wait("cmb", cnt["cmb"])
    for e_i in range(NE):
        ep = e_i % 2
        if e_i < 16:
            g.wait_gate(gate.get("crow"))
            dma(crow[:], comb_dram[e_i:e_i + 1, :], "cr0")
            cr_done = cnt["cr0"]
            src = lambda tq: crow[0:1, tq * 512:(tq + 1) * 512]
            t.wait("cr0", cr_done)
        else:
            src = lambda tq: sg_row[0:1, tq * 512:(tq + 1) * 512]
        t.wait("dve", cnt["dve"])
        if e_i == 0:
            t.wait_gate(gate.get("ps_sc"))
        for tq in range(4):
            cp = tq % 2
            t.wait_gate(gate.get(f"ps{cp}"))
            t.op(lambda e, tq=tq, cp=cp, src=src: e.matmul(
                ps[cp][:], ones_rb[:], src(tq), start=True, stop=True, skip_group_check=True), inc="pe")
            pe_cb = cnt["pe"]
            if e_i < 16:
                gate["crow"] = ("pe", pe_cb)
            a_cb = s.do(lambda e, tq=tq, cp=cp: e.copy(
                out=combb[:, tq * 512:(tq + 1) * 512], in_=ps[cp][:]),
                extra_waits=[("pe", pe_cb)])
            gate[f"ps{cp}"] = ("act", a_cb)
        a_cbl = cnt["act"]

        for i in range(IC):
            ip = i % 2
            gsem, usem = f"wg{ip}", f"wu{ip}"
            g.wait_gate(gate.get(("wg", ip)))
            dma(wg_sb[:, ip], dr["wg_all"][e_i, i], gsem)
            g.wait_gate(gate.get(("wu", ip)))
            dma(wu_sb[:, ip], dr["wu_all"][e_i, i], usem)
            g_done, u_done = cnt[gsem], cnt[usem]
            t.wait(gsem, g_done)
            t.wait(usem, u_done)
            for tq in range(4):
                tp = tq % 2
                t.wait_gate(gate.get(f"ps{tp}"))
                if i == 0 and tq < 2:
                    t.wait("act", a_cbl)
                for kd in range(KD):
                    t.op(lambda e, kd=kd, ip=ip, tq=tq, tp=tp: e.matmul(
                        ps[tp][:], wg_sb[:, ip, kd, :],
                        x_res[:, kd, tq * 512:(tq + 1) * 512],
                        start=(kd == 0), stop=(kd == KD - 1), skip_group_check=True),
                        inc="pe" if kd == KD - 1 else None)
                pe_g = cnt["pe"]
                t.wait_gate(gate.get(("psu", tp)))
                for kd in range(KD):
                    t.op(lambda e, kd=kd, ip=ip, tq=tq, tp=tp: e.matmul(
                        ps[2 + tp][:], wu_sb[:, ip, kd, :],
                        x_res[:, kd, tq * 512:(tq + 1) * 512],
                        start=(kd == 0), stop=(kd == KD - 1), skip_group_check=True),
                        inc="pe" if kd == KD - 1 else None)
                pe_u = cnt["pe"]
                a_si = s.do(lambda e, tp=tp: e.activation(
                    out=sil[:, tp, :], in_=ps[tp][:], func=AF.Silu),
                    extra_waits=[("pe", pe_g), gate.get(("sil", tp))])
                gate[f"ps{tp}"] = ("act", a_si)
                d_u = v.do(lambda e, tp=tp: e.tensor_mul(
                    sil[:, tp, :], sil[:, tp, :], ps[2 + tp][:]),
                    extra_waits=[("act", a_si), ("pe", pe_u)])
                gate[("psu", tp)] = ("dve", d_u)
                d_sc = v.do(lambda e, i=i, tq=tq, tp=tp: e.tensor_mul(
                    act_sb[:, i, tq * 512:(tq + 1) * 512], sil[:, tp, :],
                    combb[:, tq * 512:(tq + 1) * 512]),
                    extra_waits=[("act", a_cbl)] if i == 0 else ())
                gate[("sil", tp)] = ("dve", d_sc)
            gate[("wg", ip)] = ("pe", pe_g)
            gate[("wu", ip)] = ("pe", pe_u)
        dv_A = cnt["dve"]

        first_b = True
        for dc in range(KD):
            dp = dc % 2
            dsem = f"wd{dp}"
            g.wait_gate(gate.get(("wd", dp)))
            dma(wd_sb[:, dp], dr["wd_all"][e_i, dc], dsem)
            d_done = cnt[dsem]
            t.wait(dsem, d_done)
            if first_b:
                t.wait("dve", dv_A)
                first_b = False
            for tq in range(4):
                yp = 4 + ((dc * 4 + tq) % 2)
                t.wait_gate(gate.get(f"ps{yp}"))
                for i in range(IC):
                    t.op(lambda e, i=i, dp=dp, tq=tq, yp=yp: e.matmul(
                        ps[yp][:], wd_sb[:, dp, i, :],
                        act_sb[:, i, tq * 512:(tq + 1) * 512],
                        start=(i == 0), stop=(i == IC - 1), skip_group_check=True),
                        inc="pe" if i == IC - 1 else None)
                pe_y = cnt["pe"]
                d_y = v.do(lambda e, dc=dc, tq=tq, yp=yp: e.tensor_add(
                    racc[:, dc, tq * 512:(tq + 1) * 512],
                    racc[:, dc, tq * 512:(tq + 1) * 512], ps[yp][:]),
                    extra_waits=[("pe", pe_y)])
                gate[f"ps{yp}"] = ("dve", d_y)
            gate[("wd", dp)] = ("pe", pe_y)

    fe.close()

    epi = ExitStack()
    identb = sb(epi, "identb_t", (128, 128), BF16)
    ytok = sb(epi, "ytok", (128, 2, DM), BF16)
    ysq = sb(epi, "ysq", (128, DM), BF16)
    yout = sb(epi, "yout", (128, 2, DM), F32)
    cols = sb(epi, "cols", (128, 8), F32)

    P.phase("EPI")
    dma(identb[:], dr["identb"], "p0")
    dv_racc_done = cnt["dve"]
    t.wait("dve", dv_racc_done)
    t.wait("p0", cnt["p0"])
    last_dv_y = 0
    stats_q = []

    def emit_transposes(tc):
        yp = tc % 2
        for dc in range(KD):
            tp = dc % 2
            t.wait_gate(gate.get(("pst", tp)))
            t.op(lambda e, dc=dc, tc=tc, tp=tp: e.transpose(
                pst[tp][:], racc[:, dc, tc * 128:(tc + 1) * 128], identb[:]),
                inc="pe")
            pe_t = cnt["pe"]
            if dc % 2 == 0:
                a_yt = s.do(lambda e, dc=dc, tp=tp, yp=yp: e.copy(
                    out=ytok[:, yp, dc * 128:(dc + 1) * 128], in_=pst[tp][:]),
                    extra_waits=[("pe", pe_t)] +
                        ([("dve", last_dv_y)] if dc == 0 and tc > 1 else []))
                gate[("pst", tp)] = ("act", a_yt)
            else:
                d_yt = v.do(lambda e, dc=dc, tp=tp, yp=yp: e.tensor_copy(
                    out=ytok[:, yp, dc * 128:(dc + 1) * 128], in_=pst[tp][:]),
                    extra_waits=[("pe", pe_t)])
                gate[("pst", tp)] = ("dve", d_yt)
        return cnt["act"], cnt["dve"]

    def emit_stats(tc, a_ytl, d_ytl):
        global_dummy = None
        yp = tc % 2
        op = tc % 2
        osem = f"out{op}"
        v.do(lambda e, yp=yp: e.tensor_reduce(cols[:, 0:1], ytok[:, yp, :], AX, ALU.add),
             extra_waits=[("act", a_ytl)])
        v.do(lambda e: e.tensor_scalar_mul(cols[:, 0:1], cols[:, 0:1], 1.0 / DM))
        a_q = s.do(lambda e, yp=yp: e.activation(out=ysq[:], in_=ytok[:, yp, :], func=AF.Square),
                   extra_waits=[("dve", d_ytl)])
        v.do(lambda e: e.tensor_reduce(cols[:, 1:2], ysq[:], AX, ALU.add),
             extra_waits=[("act", a_q)])
        v.do(lambda e: e.tensor_scalar_mul(cols[:, 1:2], cols[:, 1:2], 1.0 / DM))
        v.do(lambda e: e.tensor_mul(cols[:, 2:3], cols[:, 0:1], cols[:, 0:1]))
        v.do(lambda e: e.tensor_sub(cols[:, 3:4], cols[:, 1:2], cols[:, 2:3]))
        dv_v = v.do(lambda e: e.tensor_scalar_add(cols[:, 3:4], cols[:, 3:4], EPS))
        s.do(lambda e: e.activation(out=cols[:, 4:5], in_=cols[:, 3:4],
                                    func=AF.Ln),
             extra_waits=[("dve", dv_v)])
        a_r = s.do(lambda e: e.activation(out=cols[:, 5:6], in_=cols[:, 4:5],
                                          func=AF.Exp, scale=-0.5))
        dv_y = v.do(lambda e, op=op, yp=yp: e.tensor_scalar(
            out=yout[:, op, :], in0=ytok[:, yp, :], scalar1=cols[:, 0:1],
            scalar2=cols[:, 5:6], op0=ALU.subtract, op1=ALU.mult),
            extra_waits=[("act", a_r), (osem, cnt[osem])])
        yq.wait("dve", dv_y)
        yq.op(lambda e, tc=tc, op=op: e.dma_start(
            out=y[tc * 128:(tc + 1) * 128, :], in_=yout[:, op, :]),
            inc=osem, n=16)
        return dv_y

    pend = None
    for tc in range(KD + 1):
        if tc < KD:
            marks = emit_transposes(tc)
        if pend is not None:
            last_dv_y = emit_stats(pend[0], pend[1][0], pend[1][1])
        pend = (tc, marks) if tc < KD else None
    yq.wait("out0", cnt["out0"])
    yq.wait("out1", cnt["out1"])

    epi.close()
    ffn.close()
    P.replay()
    es.close()
    return nc


_NC_CACHE = None


def kernel(**inputs):
    global _NC_CACHE
    hp = host_prep(inputs)
    if _NC_CACHE is None:
        _NC_CACHE = build_nc()
    nc = _NC_CACHE
    shared = {k: hp[k] for k in ["wqt", "wkt", "wvt", "wot", "routert", "shgw",
                                 "wg_all", "wu_all", "wd_all", "maskR",
                                 "pswap", "identb", "identf"]}
    in_maps = []
    for c in range(NCORES):
        m = dict(shared)
        m["xm"] = hp["xm"][c]
        m["xh"] = hp["xh"][c]
        m["c2"] = hp["c2"][c]
        m["s2"] = hp["s2"][c]
        m["maskA0"] = hp["maskA0"] if c == 0 else hp["maskR"]
        in_maps.append(m)
    res = run_bass_kernel_spmd(nc, in_maps, core_ids=list(range(NCORES)))
    out = np.concatenate([res.results[c]["y"] for c in range(NCORES)], axis=0)
    return out.reshape(1, NCORES * T, DM)


# version 07
# speedup vs baseline: 1.1126x; 1.0754x over previous
import math
import numpy as np
import ml_dtypes
from contextlib import ExitStack

import concourse.bass as bass
import concourse.mybir as mybir
from concourse.bass_utils import run_bass_kernel_spmd
from concourse.bass import IndirectOffsetOnAxis

F32 = mybir.dt.float32
BF16 = mybir.dt.bfloat16
AF = mybir.ActivationFunctionType
ALU = mybir.AluOpType
AX = mybir.AxisListType.X

NCORES = 8
T = 2048
W = 512
NB = 4
D = 128
H = 16
HK = 4
DM = 2048
KD = 16
NE = 20
IC = 11
CEXP = 768
CAP = 20.0
EPS = 1e-5
INV_SQRT_D = 1.0 / math.sqrt(D)

BF = ml_dtypes.bfloat16



class Prog:

    def __init__(self, nc, es):
        self.nc = nc
        self.cnt = {}
        self.sems = {}
        self.phases = []
        self.cur = None

    def mksem(self, es, name):
        self.sems[name] = es.enter_context(self.nc.semaphore(name))
        self.cnt[name] = 0

    def phase(self, name):
        self.cur = {"name": name, "t": [], "v": [], "s": [], "g": [], "y": []}
        self.phases.append(self.cur)

    def replay(self):
        for ph in self.phases:
            with self.nc.Block() as blk:
                if ph["g"]:
                    @blk.gpsimd
                    def _(g, ops=ph["g"]):
                        for f in ops:
                            f(g)
                if ph["t"]:
                    @blk.tensor
                    def _(t, ops=ph["t"]):
                        for f in ops:
                            f(t)
                if ph["s"]:
                    @blk.scalar
                    def _(s, ops=ph["s"]):
                        for f in ops:
                            f(s)
                if ph["v"]:
                    @blk.vector
                    def _(v, ops=ph["v"]):
                        for f in ops:
                            f(v)
                if ph["y"]:
                    @blk.sync
                    def _(y, ops=ph["y"]):
                        for f in ops:
                            f(y)


class ES_:
    def __init__(self, prog, key, own=None):
        self.p = prog
        self.key = key
        self.own = own

    def raw(self, fn):
        self.p.cur[self.key].append(fn)

    def wait(self, sem, val):
        if val is None or val <= 0:
            return
        self.raw(lambda eng, s=self.p.sems[sem], v=val: eng.wait_ge(s, v))

    def wait_gate(self, gate):
        if gate:
            self.wait(gate[0], gate[1])

    def op(self, fn, inc=None, n=1):
        if inc is None:
            self.raw(lambda eng, f=fn: f(eng))
            return None
        self.p.cnt[inc] += n
        tgt = self.p.cnt[inc]
        self.raw(lambda eng, f=fn, s=self.p.sems[inc], m=n: f(eng).then_inc(s, m))
        return tgt

    def do(self, fn, extra_waits=()):
        for sv in extra_waits:
            if sv:
                self.wait(sv[0], sv[1])
        self.wait(self.own, self.p.cnt[self.own])
        return self.op(fn, inc=self.own)



def _tile_lhsT(wT, oc_chunks):
    din, dout = wT.shape
    assert din == DM and dout == oc_chunks * 128
    return np.ascontiguousarray(
        wT.reshape(KD, 128, oc_chunks, 128).transpose(2, 1, 0, 3))


def host_prep(inputs):
    d = {}
    x = np.asarray(inputs["L0_attn_x"])[0]
    xT = np.ascontiguousarray(x.T)
    xT4 = xT.reshape(KD, 128, NCORES * T)
    xm, xh = [], []
    for c in range(NCORES):
        xm.append(np.ascontiguousarray(
            xT4[:, :, c * T:(c + 1) * T].transpose(1, 0, 2)))
        h0 = c * T - W if c > 0 else 0
        xh.append(np.ascontiguousarray(
            xT4[:, :, h0:h0 + W].transpose(1, 0, 2)))
    d["xm"], d["xh"] = xm, xh

    d["wqt"] = _tile_lhsT(np.ascontiguousarray(np.asarray(inputs["L0_attn_wq"]).T), 16)
    d["wkt"] = _tile_lhsT(np.ascontiguousarray(np.asarray(inputs["L0_attn_wk"]).T), 4)
    d["wot"] = _tile_lhsT(np.ascontiguousarray(np.asarray(inputs["L0_attn_wo"]).T), 16)
    d["wvt"] = np.ascontiguousarray(
        np.ascontiguousarray(np.asarray(inputs["L0_attn_wv"]).T)
        .reshape(KD, 128, 512).transpose(1, 0, 2))
    d["routert"] = np.ascontiguousarray(
        np.ascontiguousarray(np.asarray(inputs["L0_ffn_router_w"]).T)
        .reshape(KD, 128, 16).transpose(1, 0, 2))
    d["shgw"] = np.ascontiguousarray(
        np.asarray(inputs["L0_ffn_sh_gate_w"]).reshape(KD, 128, 1).transpose(1, 0, 2))

    wg = np.asarray(inputs["L0_ffn_w_gate"])
    wu = np.asarray(inputs["L0_ffn_w_up"])
    wd = np.asarray(inputs["L0_ffn_w_down"])
    shg = np.asarray(inputs["L0_ffn_sh_gate"])
    shu = np.asarray(inputs["L0_ffn_sh_up"])
    shd = np.asarray(inputs["L0_ffn_sh_down"])

    def tile_gu(m):
        return m.reshape(KD, 128, IC, 128).transpose(2, 1, 0, 3)

    def tile_dn(m):
        return m.reshape(IC, 128, KD, 128).transpose(2, 1, 0, 3)

    wg_all = np.empty((NE, IC, 128, KD, 128), np.float32)
    wu_all = np.empty((NE, IC, 128, KD, 128), np.float32)
    wd_all = np.empty((NE, KD, 128, IC, 128), np.float32)
    for e in range(16):
        wg_all[e] = tile_gu(wg[e])
        wu_all[e] = tile_gu(wu[e])
        wd_all[e] = tile_dn(wd[e])
    for sx in range(4):
        wg_all[16 + sx] = tile_gu(shg[:, sx * 1408:(sx + 1) * 1408])
        wu_all[16 + sx] = tile_gu(shu[:, sx * 1408:(sx + 1) * 1408])
        wd_all[16 + sx] = tile_dn(shd[sx * 1408:(sx + 1) * 1408])
    d["wg_all"], d["wu_all"], d["wd_all"] = wg_all, wu_all, wd_all

    inv = 10000.0 ** (-np.arange(0, D, 2, dtype=np.float64) / D)
    c2s, s2s = [], []
    for c in range(NCORES):
        pos = np.arange(c * T - W, c * T + T, dtype=np.float64)
        ang = np.outer(pos, inv)
        cosv = np.cos(ang).T.astype(np.float32)
        sinv = np.sin(ang).T.astype(np.float32)
        c2 = np.empty((128, T + W), np.float32)
        s2 = np.empty((128, T + W), np.float32)
        c2[0::2] = cosv
        c2[1::2] = cosv
        s2[0::2] = -sinv
        s2[1::2] = sinv
        c2s.append(c2.astype(BF))
        s2s.append(s2.astype(BF))
    d["c2"], d["s2"] = c2s, s2s

    j = np.arange(1024)[:, None]
    a = np.arange(W)[None, :]
    mR = ((j < W) & (j >= a)) | ((j >= W) & (j - W <= a))
    mA0 = (j >= W) & (j - W <= a)
    d["maskR"] = np.ascontiguousarray(
        mR.astype(BF).reshape(8, 128, W).transpose(1, 0, 2))
    d["maskA0"] = np.ascontiguousarray(
        mA0.astype(BF).reshape(8, 128, W).transpose(1, 0, 2))

    d["iota16"] = np.tile(np.arange(16, dtype=np.float32), (128, 1))

    psw = np.zeros((128, 128), np.float32)
    i2 = np.arange(0, 128, 2)
    psw[i2, i2 + 1] = 1.0
    psw[i2 + 1, i2] = 1.0
    d["pswap"] = psw.astype(BF)
    d["identb"] = np.eye(128).astype(BF)
    d["identf"] = np.eye(128, dtype=np.float32)
    return d



def build_nc():
    nc = bass.Bass("TRN2", target_bir_lowering=False)

    dr = {}
    def din(name, shape, dt=F32):
        dr[name] = nc.dram_tensor(name, list(shape), dt, kind="ExternalInput").ap()
    din("xm", (128, KD, T)); din("xh", (128, KD, W))
    din("wqt", (16, 128, KD, 128)); din("wkt", (4, 128, KD, 128))
    din("wvt", (128, KD, 512)); din("wot", (16, 128, KD, 128))
    din("routert", (128, KD, 16)); din("shgw", (128, KD, 1))
    din("wg_all", (NE, IC, 128, KD, 128)); din("wu_all", (NE, IC, 128, KD, 128))
    din("wd_all", (NE, KD, 128, IC, 128))
    din("c2", (128, T + W), BF16); din("s2", (128, T + W), BF16)
    din("maskR", (128, 8, W), BF16); din("maskA0", (128, 8, W), BF16)
    din("pswap", (128, 128), BF16); din("identb", (128, 128), BF16)
    din("identf", (128, 128))
    y = nc.dram_tensor("y", [T, DM], F32, kind="ExternalOutput").ap()
    din("iota16", (128, 16))
    xg_dram = nc.dram_tensor("xg_scr", [16 * CEXP, DM], BF16, kind="Internal").ap()
    cg_dram = nc.dram_tensor("cg_scr", [16 * CEXP, 2], F32, kind="Internal").ap()
    y_all_dram = nc.dram_tensor("yall_scr", [16 * CEXP, DM], BF16, kind="Internal").ap()

    es = ExitStack()
    P = Prog(nc, es)
    for sname in ["p0", "wq0", "wq1", "wq2", "wg0", "wg1", "wu0", "wu1", "wd0", "wd1",
                  "cr0", "cr1", "cmb", "pe", "act", "dve", "out0", "out1",
                  "xg0", "xg1", "ya", "sca0", "sca1", "yg0", "yg1"]:
        P.mksem(es, sname)
    t = ES_(P, "t")
    v = ES_(P, "v", own="dve")
    s = ES_(P, "s", own="act")
    g = ES_(P, "g")
    yq = ES_(P, "y")
    cnt = P.cnt
    gate = {}

    def sb(stack, name, shape, dt):
        return stack.enter_context(nc.sbuf_tensor(name, list(shape), dt))

    ones_cb = sb(es, "ones_cb", (128, 1), BF16)
    ones_rb = sb(es, "ones_rb", (1, 128), BF16)
    ones_rf = sb(es, "ones_rf", (1, 128), F32)
    identb = sb(es, "identb_t", (128, 128), BF16)
    xsc = ExitStack()
    x_res = sb(xsc, "x_res", (128, KD, T), BF16)

    ps = [es.enter_context(nc.psum_tensor(f"ps{i}", [128, 512], F32))
          for i in range(2)]
    ps_sc = es.enter_context(nc.psum_tensor("ps_sc", [128, 1024], F32))
    ps.append(ps_sc.ap()[:, 0:512])
    ps.append(ps_sc.ap()[:, 512:1024])
    ps += [es.enter_context(nc.psum_tensor(f"ps{i}", [128, 512], F32))
           for i in (4, 5)]
    pst = [es.enter_context(nc.psum_tensor(f"pst{i}", [128, 128], BF16))
           for i in range(2)]

    def dma(ap_out, ap_in, sem):
        return g.op(lambda e, o=ap_out, i=ap_in: e.dma_start(out=o, in_=i),
                    inc=sem, n=16)

    att = ExitStack()
    pswap = sb(att, "pswap_t", (128, 128), BF16)
    wvt_sb = sb(att, "wvt_sb", (128, KD, 512), BF16)
    kT = sb(att, "kT", (128, 2, HK, W), BF16)
    v_tok = sb(att, "v_tok", (128, 2, 4, W), BF16)
    c2_sb = sb(att, "c2_sb", (128, W), BF16)
    s2_sb = sb(att, "s2_sb", (128, W), BF16)
    mask_sb = sb(att, "mask_sb", (128, 8, W), BF16)
    wqo_sb = sb(att, "wqo_sb", (128, 3, KD, 128), BF16)
    qT = sb(att, "qT", (128, 2, W), BF16)
    q_raw = sb(att, "q_raw", (128, 2, W), BF16)
    rtmp1 = sb(att, "rtmp1", (128, W), F32)
    rtmp2 = sb(att, "rtmp2", (128, W), F32)
    expT = sb(att, "expT", (128, 2, 8, W), BF16)
    etmp = sb(att, "etmp", (128, 2, 2 * W), BF16)
    ttmp = sb(att, "ttmp", (128, 2 * W), F32)
    rb_sb = sb(att, "rb_sb", (128, 2, W), F32)
    recip_row = sb(att, "recip_row", (1, 2, W), F32)
    attn_out = sb(att, "attn_out", (128, KD, W), BF16)
    s1_pp = sb(att, "s1_pp", (128, 2, W), BF16)
    sq_pp = sb(att, "sq_pp", (128, 2, W), BF16)
    lnrow = sb(att, "lnrow", (1, 3, W), F32)
    mb_sb = sb(att, "mb_sb", (128, W), F32)
    rb2_sb = sb(att, "rb2_sb", (128, W), F32)
    ln_t = sb(att, "ln_t", (128, W), F32)

    halo = ExitStack()
    x_halo = sb(halo, "x_halo", (128, KD, W), BF16)

    def x_blk(n):
        return [x_res[:, kd, n * W:(n + 1) * W] for kd in range(KD)]

    wplan = {"aps": [], "next": 0, "cnt": {}}

    def wreset(aps):
        wplan["aps"] = aps
        wplan["next"] = 0
        wplan["cnt"] = {}

    def wpump(upto):
        while wplan["next"] < min(upto, len(wplan["aps"])):
            i = wplan["next"]
            slot = i % 3
            g.wait_gate(gate.get(("wqo", slot)))
            dma(wqo_sb[:, slot], wplan["aps"][i], f"wq{slot}")
            wplan["cnt"][i] = cnt[f"wq{slot}"]
            wplan["next"] += 1

    def wwait(i):
        t.wait(f"wq{i % 3}", wplan["cnt"][i])

    def wdone(i, pe_val):
        gate[("wqo", i % 3)] = ("pe", pe_val)
        wpump(i + 4)

    def kv_block(n, x_src, widx0):
        slot = n % 2
        for kvh in range(HK):
            wi = widx0 + kvh
            sp3 = wi % 3
            wwait(wi)
            t.wait_gate(gate.get("ps0"))
            t.wait_gate(gate.get(("qraw_rd", kvh % 2)))
            for kd in range(KD):
                t.op(lambda e, kd=kd, sp3=sp3, xs=x_src: e.matmul(
                    ps[0][:], wqo_sb[:, sp3, kd, :], xs[kd],
                    start=(kd == 0), stop=(kd == KD - 1), skip_group_check=True),
                    inc="pe" if kd == KD - 1 else None)
            pe_k = cnt["pe"]
            wdone(wi, pe_k)
            pp = kvh % 2
            a_ev = s.do(lambda e, pp=pp: e.copy(out=q_raw[:, pp, :], in_=ps[0][:]),
                        extra_waits=[("pe", pe_k), ("dve", cnt["dve"])])
            gate["ps0"] = ("act", a_ev)
            t.wait("act", a_ev)
            t.wait_gate(gate.get("ps1"))
            t.op(lambda e, pp=pp: e.matmul(ps[1][:], pswap[:], q_raw[:, pp, :],
                                           start=True, stop=True, skip_group_check=True), inc="pe")
            pe_sh = cnt["pe"]
            v.do(lambda e, pp=pp: e.tensor_mul(
                rtmp1[:], q_raw[:, pp, :], c2_sb[:]),
                extra_waits=[("act", a_ev), ("p0", cnt["p0"])])
            d2 = v.do(lambda e: e.tensor_mul(
                rtmp2[:], ps[1][:], s2_sb[:]),
                extra_waits=[("pe", pe_sh)])
            gate["ps1"] = ("dve", d2)
            dk = v.do(lambda e, slot=slot, kvh=kvh: e.tensor_add(
                kT[:, slot, kvh, :], rtmp1[:], rtmp2[:]))
            gate[("qraw_rd", pp)] = ("dve", dk)
        for jc in range(4):
            pp = jc % 2
            if pp == 0:
                t.wait_gate(gate.get("ps_sc"))
            for kd in range(KD):
                t.op(lambda e, kd=kd, jc=jc, pp=pp, xs=x_src: e.matmul(
                    ps[2 + pp][:], xs[kd][:, jc * 128:(jc + 1) * 128],
                    wvt_sb[:, kd, :],
                    start=(kd == 0), stop=(kd == KD - 1), skip_group_check=True),
                    inc="pe" if kd == KD - 1 else None)
            pe_v = cnt["pe"]
            a_v = s.do(lambda e, slot=slot, jc=jc, pp=pp: e.copy(
                out=v_tok[:, slot, jc, :], in_=ps[2 + pp][:]),
                extra_waits=[("pe", pe_v)])
            gate["ps_sc"] = ("act", a_v)

    P.phase("P0")
    for kq in range(4):
        dma(x_res[:, kq * 4:(kq + 1) * 4, :], dr["xm"][:, kq * 4:(kq + 1) * 4, :], "p0")
    dma(x_halo[:], dr["xh"], "p0")
    dma(wvt_sb[:], dr["wvt"], "p0")
    dma(c2_sb[:], dr["c2"][:, 0:W], "p0")
    dma(s2_sb[:], dr["s2"][:, 0:W], "p0")
    dma(mask_sb[:], dr["maskA0"], "p0")
    dma(pswap[:], dr["pswap"], "p0")
    dma(identb[:], dr["identb"], "p0")
    p0_total = cnt["p0"]
    v.do(lambda e: e.memset(ones_cb[:], 1.0))
    v.do(lambda e: e.memset(ones_rf[:], 1.0))
    v.do(lambda e: e.memset(ones_rb[:], 1.0))
    t.wait("p0", p0_total)
    t.wait("dve", cnt["dve"])
    full_plan = [dr["wkt"][kvh] for kvh in range(HK)]
    for _n in range(NB):
        full_plan += [dr["wkt"][kvh] for kvh in range(HK)]
        full_plan += [dr["wqt"][h] for h in range(H)]
        full_plan += [dr["wot"][dc] for dc in range(KD)]
    wreset(full_plan)
    wpump(3)
    kv_block(-1, [x_halo[:, kd, :] for kd in range(KD)], 0)

    P.phase("ATT")
    zrow = x_halo[:, 0:4, :]
    dv_z = v.do(lambda e: e.memset(zrow, 0.0))
    g.wait("dve", dv_z)
    g.op(lambda e: e.dma_start(
        out=cg_dram[:].rearrange("a b -> (a b)").rearrange("(p n) -> p n", p=128),
        in_=zrow[:, 0, 0:192]), inc="p0", n=16)
    yq.wait("dve", dv_z)
    for zi in range(16 * CEXP // 128):
        yq.op(lambda e, zi=zi: e.dma_start(
            out=xg_dram[zi * 128:(zi + 1) * 128, :],
            in_=zrow.rearrange("p a b -> p (a b)")), inc="out0", n=16)
    zx_total = cnt["out0"]

    def emit_qproj(n, h, xb, wbase0):
        hp = h % 2
        wi = wbase0 + 4 + h
        sp3 = wi % 3
        wwait(wi)
        t.wait_gate(gate.get("ps0"))
        for kd in range(KD):
            t.op(lambda e, kd=kd, sp3=sp3, xb=xb: e.matmul(
                ps[0][:], wqo_sb[:, sp3, kd, :], xb[kd],
                start=(kd == 0), stop=(kd == KD - 1), skip_group_check=True),
                inc="pe" if kd == KD - 1 else None)
        pe_q = cnt["pe"]
        wdone(wi, pe_q)
        a_ev = s.do(lambda e, hp=hp: e.copy(out=q_raw[:, hp, :], in_=ps[0][:]),
                    extra_waits=[("pe", pe_q), ("dve", cnt["dve"])])
        gate["ps0"] = ("act", a_ev)
        t.wait("act", a_ev)
        t.wait_gate(gate.get("ps1"))
        t.op(lambda e, hp=hp: e.matmul(ps[1][:], pswap[:], q_raw[:, hp, :],
                                       start=True, stop=True, skip_group_check=True), inc="pe")
        pe_sh = cnt["pe"]
        v.do(lambda e, hp=hp: e.tensor_mul(
            rtmp1[:], q_raw[:, hp, :], c2_sb[:]),
            extra_waits=[("act", a_ev), ("p0", cnt["p0"])])
        d2 = v.do(lambda e: e.tensor_mul(
            rtmp2[:], ps[1][:], s2_sb[:]),
            extra_waits=[("pe", pe_sh)])
        gate["ps1"] = ("dve", d2)
        dv_q = v.do(lambda e, hp=hp: e.tensor_add(
            qT[:, hp, :], rtmp1[:], rtmp2[:]),
            extra_waits=[gate[("qT", hp)]] if ("qT", hp) in gate else ())
        gate[("qraw_rd", hp)] = ("dve", dv_q)
        return dv_q

    def emit_scores(n, h, dv_q, msk):
        kvh = h // 4
        hp = h % 2
        slot, pslot = n % 2, (n - 1) % 2
        t.wait("dve", dv_q)
        dms = []
        for jp in range(4):
            jc0 = 2 * jp
            pv = jp % 2
            t.wait_gate(gate.get("ps_sc"))
            for half in range(2):
                jc = jc0 + half
                sl = pslot if jc < 4 else slot
                jj = jc % 4
                t.op(lambda e, sl=sl, kvh=kvh, jj=jj, hp=hp, half=half: e.matmul(
                    ps[2 + half][:], kT[:, sl, kvh, jj * 128:(jj + 1) * 128],
                    qT[:, hp, :], start=True, stop=True, skip_group_check=True),
                    inc="pe" if half == 1 else None)
            pe_sc = cnt["pe"]
            a_t = s.do(lambda e: e.activation(
                out=ttmp[:], in_=ps_sc[:], func=AF.Tanh,
                scale=INV_SQRT_D / CAP), extra_waits=[("pe", pe_sc)])
            gate["ps_sc"] = ("act", a_t)
            a_e = s.do(lambda e, pv=pv: e.activation(
                out=etmp[:, pv, :], in_=ttmp[:], func=AF.Exp,
                scale=CAP), extra_waits=[gate.get(("etmp", pv))])
            d_m = v.do(lambda e, jc0=jc0, hp=hp, pv=pv, msk=msk: e.tensor_mul(
                expT[:, hp, jc0:jc0 + 2, :], etmp[:, pv, :].rearrange("p (a b) -> p a b", a=2),
                msk[:, jc0:jc0 + 2, :]),
                extra_waits=[("act", a_e), ("p0", cnt["p0"])] +
                    ([gate[("expT", hp)]] if jp == 0 and ("expT", hp) in gate
                     else []))
            gate[("etmp", pv)] = ("dve", d_m)
            dms.append((jc0, d_m))
            dms.append((jc0 + 1, d_m))
        return dms

    def emit_sumsout(n, h, dms):
        kvh = h // 4
        hp = h % 2
        slot, pslot = n % 2, (n - 1) % 2
        first = True
        for jc, d_m in dms:
            sl = pslot if jc < 4 else slot
            jj = jc % 4
            t.wait("dve", d_m)
            t.op(lambda e, jc=jc, hp=hp, st=first: e.matmul(
                ps[4][0:1, :], ones_cb[:], expT[:, hp, jc, :],
                start=st, stop=(jc == 7), skip_group_check=True), inc="pe")
            t.op(lambda e, sl=sl, jj=jj, kvh=kvh, jc=jc, hp=hp, st=first: e.matmul(
                ps[5][:], v_tok[:, sl, jj, kvh * 128:(kvh + 1) * 128],
                expT[:, hp, jc, :], start=st, stop=(jc == 7), skip_group_check=True),
                inc="pe" if jc == 7 else None)
            first = False
        pe_att = cnt["pe"]
        gate[("expT", hp)] = ("pe", pe_att)
        gate[("qT", hp)] = ("pe", pe_att)
        d_r = v.do(lambda e, hp=hp: e.reciprocal(
            out=recip_row[:, hp, :], in_=ps[4][0:1, :]),
            extra_waits=[("pe", pe_att)])
        t.wait("dve", d_r)
        t.wait_gate(gate.get("ps0"))
        t.op(lambda e, hp=hp: e.matmul(ps[0][:], ones_rf[:],
                                       recip_row[:, hp, :],
                                       start=True, stop=True, skip_group_check=True), inc="pe")
        pe_rb = cnt["pe"]
        a_rb = s.do(lambda e, hp=hp: e.copy(out=rb_sb[:, hp, :], in_=ps[0][:]),
                    extra_waits=[("pe", pe_rb)])
        gate["ps0"] = ("act", a_rb)
        d_ao = v.do(lambda e, h=h, hp=hp: e.tensor_mul(
            attn_out[:, h, :], ps[5][:], rb_sb[:, hp, :]),
            extra_waits=[("act", a_rb)])
        gate["ps5"] = ("dve", d_ao)
        return pe_att

    for n in range(NB):
        slot, pslot = n % 2, (n - 1) % 2
        tab0 = W + n * W
        xb = x_blk(n)
        g.wait_gate(gate.get("ropetab"))
        dma(c2_sb[:], dr["c2"][:, tab0:tab0 + W], "p0")
        dma(s2_sb[:], dr["s2"][:, tab0:tab0 + W], "p0")
        if n == 1:
            g.wait_gate(gate.get("maskrd"))
            dma(mask_sb[:], dr["maskR"], "p0")
        t.wait_gate(gate.get(("kvslot", slot)))
        wbase = 4 + 36 * n
        kv_block(n, xb, wbase)
        msk = mask_sb

        dv_q = emit_qproj(n, 0, xb, wbase)
        for h in range(H):
            dms = emit_scores(n, h, dv_q, msk)
            if h + 1 < H:
                dv_q = emit_qproj(n, h + 1, xb, wbase)
            pe_att = emit_sumsout(n, h, dms)
        gate[("kvslot", pslot)] = ("pe", pe_att)
        gate["ropetab"] = ("dve", cnt["dve"])
        gate["maskrd"] = ("dve", cnt["dve"])

        dv_attn_done = cnt["dve"]
        first = True
        for dc in range(KD):
            dp = dc % 2
            wi = wbase + 20 + dc
            sp3 = wi % 3
            wwait(wi)
            if dc == 0:
                t.wait("dve", dv_attn_done)
            t.wait_gate(gate.get(f"pso{dp}"))
            for kd in range(KD):
                t.op(lambda e, kd=kd, sp3=sp3, dp=dp: e.matmul(
                    ps[dp][:], wqo_sb[:, sp3, kd, :], attn_out[:, kd, :],
                    start=(kd == 0), stop=(kd == KD - 1), skip_group_check=True),
                    inc="pe" if kd == KD - 1 else None)
            pe_o = cnt["pe"]
            wdone(wi, pe_o)
            a_o = s.do(lambda e, dc=dc, dp=dp: e.copy(
                out=expT[:, dc // 8, dc % 8, :], in_=ps[dp][:]),
                extra_waits=[("pe", pe_o)])
            gate[f"pso{dp}"] = ("act", a_o)
            d_s1 = v.do(lambda e, dc=dc, n=n: e.tensor_add(
                s1_pp[:, dc % 2, :], x_res[:, dc, n * W:(n + 1) * W],
                expT[:, dc // 8, dc % 8, :]), extra_waits=[("act", a_o)])
            a_sq = s.do(lambda e, dc=dc: e.activation(
                out=sq_pp[:, dc % 2, :], in_=s1_pp[:, dc % 2, :], func=AF.Square),
                extra_waits=[("dve", d_s1)])
            t.wait("dve", d_s1)
            if dc == 0:
                t.wait_gate(gate.get("ps_sc"))
            t.op(lambda e, dc=dc, st=first: e.matmul(
                ps[2][0:1, :], ones_cb[:], s1_pp[:, dc % 2, :],
                start=st, stop=(dc == KD - 1), skip_group_check=True), inc="pe")
            t.wait("act", a_sq)
            t.op(lambda e, dc=dc, st=first: e.matmul(
                ps[3][0:1, :], ones_cb[:], sq_pp[:, dc % 2, :],
                start=st, stop=(dc == KD - 1), skip_group_check=True),
                inc="pe" if dc == KD - 1 else None)
            first = False
        pe_stats = cnt["pe"]
        wpump(wbase + 36 + 3)
        v.do(lambda e: e.tensor_scalar_mul(lnrow[:, 0, :], ps[2][0:1, :], 1.0 / DM),
             extra_waits=[("pe", pe_stats)])
        v.do(lambda e: e.tensor_scalar_mul(lnrow[:, 1, :], ps[3][0:1, :], 1.0 / DM))
        v.do(lambda e: e.tensor_mul(lnrow[:, 2, :], lnrow[:, 0, :], lnrow[:, 0, :]))
        v.do(lambda e: e.tensor_sub(lnrow[:, 1, :], lnrow[:, 1, :],
                                    lnrow[:, 2, :]))
        dv_var = v.do(lambda e: e.tensor_scalar_add(lnrow[:, 1, :],
                                                    lnrow[:, 1, :], EPS))
        gate["ps_sc"] = ("dve", dv_var)
        s.do(lambda e: e.activation(out=lnrow[:, 2, :], in_=lnrow[:, 1, :],
                                    func=AF.Ln),
             extra_waits=[("dve", dv_var)])
        a_rs = s.do(lambda e: e.activation(out=lnrow[:, 1, :], in_=lnrow[:, 2, :],
                                           func=AF.Exp, scale=-0.5))
        t.wait("act", a_rs)
        t.wait_gate(gate.get("ps4"))
        t.op(lambda e: e.matmul(ps[4][:], ones_rf[:], lnrow[:, 0, :],
                                start=True, stop=True, skip_group_check=True), inc="pe")
        t.wait_gate(gate.get("ps5"))
        t.op(lambda e: e.matmul(ps[5][:], ones_rf[:], lnrow[:, 1, :],
                                start=True, stop=True, skip_group_check=True), inc="pe")
        pe_bc = cnt["pe"]
        a_mb = s.do(lambda e: e.copy(out=mb_sb[:], in_=ps[4][:]),
                    extra_waits=[("pe", pe_bc)])
        a_rb2 = s.do(lambda e: e.copy(out=rb2_sb[:], in_=ps[5][:]))
        gate["ps4"] = ("act", a_mb)
        gate["ps5"] = ("act", a_rb2)
        for dc in range(KD):
            v.do(lambda e, dc=dc, n=n: e.tensor_add(
                s1_pp[:, dc % 2, :], x_res[:, dc, n * W:(n + 1) * W],
                expT[:, dc // 8, dc % 8, :]),
                extra_waits=[("act", a_rb2)] if dc == 0 else ())
            v.do(lambda e, dc=dc: e.tensor_sub(ln_t[:], s1_pp[:, dc % 2, :], mb_sb[:]))
            v.do(lambda e, dc=dc, n=n: e.tensor_mul(
                x_res[:, dc, n * W:(n + 1) * W], ln_t[:], rb2_sb[:]))

    halo.close()
    att.close()

    R = "right"
    def sbr_(stack, name, shape, dt):
        return stack.enter_context(nc.sbuf_tensor(name, list(shape), dt, side=R))

    ffn = ExitStack()
    racc = sbr_(ffn, "racc", (128, KD, T), BF16)
    sg_row = sbr_(ffn, "sg_row", (1, T), BF16)
    sidx = sbr_(ffn, "sidx", (128, 16, 4), mybir.dt.int32)
    wg_sb = sbr_(ffn, "wg_sb", (128, 2, KD, 128), BF16)
    wu_sb = sbr_(ffn, "wu_sb", (128, 2, KD, 128), BF16)
    wd_sb = sbr_(ffn, "wd_sb", (128, 2, IC, 128), BF16)

    rt = ExitStack()
    identf = sbr_(rt, "identf_t", (128, 128), F32)
    routert_sb = sbr_(rt, "routert_sb", (128, KD, 16), BF16)
    shgw_sb = sbr_(rt, "shgw_sb", (128, KD, 1), BF16)
    iota16_sb = sbr_(rt, "iota16_sb", (128, 16), F32)
    lg_sb = sbr_(rt, "lg_sb", (16, 4, 512), F32)
    ltok = sbr_(rt, "ltok", (128, 16, 16), F32)
    ptmp = sbr_(rt, "ptmp", (128, 16, 16), F32)
    selacc = sbr_(rt, "selacc", (128, 16, 16), F32)
    maskt = sbr_(rt, "maskt", (128, 16, 16), F32)
    comb_tok = sbr_(rt, "comb_tok", (128, 16, 16), F32)
    mx16 = sbr_(rt, "mx16", (128, 16), F32)
    ts16 = sbr_(rt, "ts16", (128, 16), F32)
    comb_d = sbr_(rt, "comb_d", (16, T), F32)
    maskr4 = sbr_(rt, "maskr4", (128, 16, 16, 4), F32)
    eidr4 = sbr_(rt, "eidr4", (128, 16, 4), F32)
    valr4 = sbr_(rt, "valr4", (128, 16, 4), F32)
    sidxf = sbr_(rt, "sidxf", (128, 16, 4), F32)
    vst2 = sbr_(rt, "vst2", (128, 16, 4, 2), F32)
    postok = sbr_(rt, "postok", (128, 16, 16), F32)
    cum16 = sbr_(rt, "cum16", (16, T), F32)
    tmp1616 = sbr_(rt, "tmp1616", (128, 16, 16), F32)
    x1tok = sbr_(rt, "x1tok", (128, 2, DM), BF16)

    P.phase("R")
    t.wait("out0", zx_total)
    s.wait("out0", zx_total)
    v.wait("out0", zx_total)
    dma(identf[:], dr["identf"], "p0")
    dma(routert_sb[:], dr["routert"], "p0")
    dma(shgw_sb[:], dr["shgw"], "p0")
    dma(iota16_sb[:], dr["iota16"], "p0")
    p0r = cnt["p0"]
    dv_ln_done = cnt["dve"]
    for kq in range(4):
        s.do(lambda e, kq=kq: e.copy(out=racc[:, kq * 4:(kq + 1) * 4, :],
                                     in_=x_res[:, kq * 4:(kq + 1) * 4, :]),
             extra_waits=[("dve", dv_ln_done)] if kq == 0 else ())
    t.wait("p0", p0r)
    for tq in range(4):
        t.wait_gate(gate.get("ps0"))
        for kd in range(KD):
            t.op(lambda e, kd=kd, tq=tq: e.matmul(
                ps[0][0:16, :], routert_sb[:, kd, :],
                x_res[:, kd, tq * 512:(tq + 1) * 512],
                start=(kd == 0), stop=(kd == KD - 1), skip_group_check=True),
                inc="pe" if kd == KD - 1 else None)
        pe_lg = cnt["pe"]
        a_lg = s.do(lambda e, tq=tq: e.copy(out=lg_sb[:, tq, :], in_=ps[0][0:16, :]),
                    extra_waits=[("pe", pe_lg)])
        gate["ps0"] = ("act", a_lg)
        t.wait_gate(gate.get("ps_sc"))
        for kd in range(KD):
            t.op(lambda e, kd=kd, tq=tq: e.matmul(
                ps[3][0:1, :], shgw_sb[:, kd, :],
                x_res[:, kd, tq * 512:(tq + 1) * 512],
                start=(kd == 0), stop=(kd == KD - 1), skip_group_check=True),
                inc="pe" if kd == KD - 1 else None)
        pe_sg = cnt["pe"]
        a_sg = s.do(lambda e, tq=tq: e.activation(
            out=sg_row[:, tq * 512:(tq + 1) * 512], in_=ps[3][0:1, :],
            func=AF.Sigmoid), extra_waits=[("pe", pe_sg)])
        gate["ps_sc"] = ("act", a_sg)
    for tc in range(16):
        t.wait_gate(gate.get("ps1"))
        t.op(lambda e, tc=tc: e.transpose(
            ps[1][:, 0:16], lg_sb[:, tc // 4, (tc % 4) * 128:(tc % 4 + 1) * 128],
            identf[0:16, 0:16]), inc="pe")
        pe_tr = cnt["pe"]
        a_ltk = s.do(lambda e, tc=tc: e.copy(out=ltok[:, tc, :], in_=ps[1][:, 0:16]),
                     extra_waits=[("pe", pe_tr)])
        gate["ps1"] = ("act", a_ltk)
    a_lt = cnt["act"]
    bc16 = lambda ap: ap.unsqueeze(2).broadcast_to([128, 16, 16])
    v.do(lambda e: e.tensor_reduce(mx16[:], ltok[:], AX, ALU.max),
         extra_waits=[("act", a_lt)])
    v.do(lambda e: e.tensor_tensor(ptmp[:], ltok[:], bc16(mx16[:]), ALU.subtract))
    a_ex = s.do(lambda e: e.activation(out=ptmp[:], in_=ptmp[:], func=AF.Exp),
                extra_waits=[("dve", cnt["dve"])])
    v.do(lambda e: e.tensor_reduce(mx16[:], ptmp[:], AX, ALU.add),
         extra_waits=[("act", a_ex)])
    v.do(lambda e: e.reciprocal(out=ts16[:], in_=mx16[:]))
    v.do(lambda e: e.tensor_tensor(ptmp[:], ptmp[:], bc16(ts16[:]), ALU.mult))
    v.do(lambda e: e.memset(selacc[:], 0.0))
    v.do(lambda e: e.memset(ts16[:], 0.0))
    for r in range(4):
        v.do(lambda e: e.tensor_reduce(mx16[:], ptmp[:], AX, ALU.max))
        v.do(lambda e: e.tensor_tensor(maskt[:], ptmp[:], bc16(mx16[:]), ALU.is_ge))
        v.do(lambda e, r=r: e.tensor_copy(out=maskr4[:, :, :, r], in_=maskt[:]))
        v.do(lambda e: e.tensor_tensor(
            tmp1616[:], maskt[:], iota16_sb[:].unsqueeze(1).broadcast_to([128, 16, 16]),
            ALU.mult))
        v.do(lambda e, r=r: e.tensor_reduce(eidr4[:, :, r], tmp1616[:], AX, ALU.add))
        v.do(lambda e, r=r: e.tensor_copy(out=valr4[:, :, r], in_=mx16[:]))
        v.do(lambda e: e.tensor_tensor(maskt[:], maskt[:], ptmp[:], ALU.mult))
        v.do(lambda e: e.tensor_add(selacc[:], selacc[:], maskt[:]))
        v.do(lambda e: e.tensor_add(ts16[:], ts16[:], mx16[:]))
        if r < 3:
            v.do(lambda e: e.tensor_sub(ptmp[:], ptmp[:], maskt[:]))
    v.do(lambda e: e.tensor_scalar_max(ts16[:], ts16[:], 1e-9))
    v.do(lambda e: e.reciprocal(out=ts16[:], in_=ts16[:]))
    for r in range(4):
        v.do(lambda e, r=r: e.tensor_tensor(valr4[:, :, r], valr4[:, :, r],
                                            ts16[:], ALU.mult))
    dv_cmb = v.do(lambda e: e.tensor_tensor(comb_tok[:], selacc[:],
                                            bc16(ts16[:]), ALU.mult))
    t.wait("dve", dv_cmb)
    for tc in range(16):
        t.wait_gate(gate.get("ps_sc"))
        t.op(lambda e, tc=tc: e.transpose(
            ps[2][0:16, 0:128], comb_tok[:, tc, :], identf[:]), inc="pe")
        pe_ct = cnt["pe"]
        a_cd = s.do(lambda e, tc=tc: e.copy(out=comb_d[:, tc * 128:(tc + 1) * 128],
                                            in_=ps[2][0:16, 0:128]),
                    extra_waits=[("pe", pe_ct)])
        gate["ps_sc"] = ("act", a_cd)
    a_cdall = cnt["act"]
    lg_flat = lg_sb[:].rearrange("e a b -> e (a b)")
    v.do(lambda e: e.tensor_single_scalar(out=comb_d[:], in_=comb_d[:], scalar=0.0,
                                          op=ALU.is_gt),
         extra_waits=[("act", a_cdall)])
    v.do(lambda e: e.tensor_tensor_scan(out=cum16[:], data0=comb_d[:],
                                        data1=comb_d[:], initial=0.0,
                                        op0=ALU.add, op1=ALU.max))
    v.do(lambda e: e.tensor_scalar_add(cum16[:], cum16[:], -1.0))
    v.do(lambda e: e.tensor_single_scalar(out=lg_flat, in_=cum16[:],
                                          scalar=float(CEXP), op=ALU.is_lt))
    v.do(lambda e: e.tensor_mul(lg_flat, lg_flat, comb_d[:]))
    v.do(lambda e: e.tensor_scalar(out=comb_d[:], in0=lg_flat, scalar1=-1.0,
                                   scalar2=-1.0e9, op0=ALU.add, op1=ALU.mult))
    v.do(lambda e: e.tensor_mul(cum16[:], cum16[:], lg_flat))
    dv_pos = v.do(lambda e: e.tensor_add(cum16[:], cum16[:], comb_d[:]))
    t.wait("dve", dv_pos)
    for tc in range(16):
        t.wait_gate(gate.get("ps1"))
        t.op(lambda e, tc=tc: e.transpose(
            ps[1][:, 0:16], cum16[0:16, tc * 128:(tc + 1) * 128],
            identf[0:16, 0:16]), inc="pe")
        pe_pt = cnt["pe"]
        a_pt = s.do(lambda e, tc=tc: e.copy(out=postok[:, tc, :], in_=ps[1][:, 0:16]),
                    extra_waits=[("pe", pe_pt)])
        gate["ps1"] = ("act", a_pt)
    a_ptall = cnt["act"]
    v.wait("act", a_ptall)
    for r in range(4):
        v.do(lambda e, r=r: e.tensor_tensor(
            tmp1616[:], maskr4[:, :, :, r], postok[:], ALU.mult))
        v.do(lambda e, r=r: e.tensor_reduce(sidxf[:, :, r], tmp1616[:], AX, ALU.add))
        v.do(lambda e, r=r: e.scalar_tensor_tensor(
            out=sidxf[:, :, r], in0=eidr4[:, :, r], scalar=float(CEXP),
            in1=sidxf[:, :, r], op0=ALU.mult, op1=ALU.add))
    dv_sidx = v.do(lambda e: e.tensor_copy(out=sidx[:], in_=sidxf[:]))
    v.do(lambda e: e.tensor_copy(out=vst2[:, :, :, 0], in_=valr4[:]))
    dv_vst = v.do(lambda e: e.tensor_copy(out=vst2[:, :, :, 1], in_=valr4[:]))

    g.wait("dve", dv_vst)
    g.wait("out0", zx_total)
    regbox = {}
    g.op(lambda e: regbox.__setitem__("b", e.to_reg(16 * CEXP - 1)) or e.engine_nop())
    for tc in range(16):
        yp = tc % 2
        for dc in range(KD):
            tp = dc % 2
            t.wait_gate(gate.get(("pst", tp)))
            t.op(lambda e, dc=dc, tc=tc, tp=tp: e.transpose(
                pst[tp][:], x_res[:, dc, tc * 128:(tc + 1) * 128], identb[:]),
                inc="pe")
            pe_t2 = cnt["pe"]
            if dc % 2 == 0:
                a_x = s.do(lambda e, dc=dc, tp=tp, yp=yp: e.copy(
                    out=x1tok[:, yp, dc * 128:(dc + 1) * 128], in_=pst[tp][:]),
                    extra_waits=[("pe", pe_t2)] +
                        ([gate[("x1tok", yp)]] if dc == 0 and ("x1tok", yp) in gate
                         else []))
                gate[("pst", tp)] = ("act", a_x)
            else:
                d_x = v.do(lambda e, dc=dc, tp=tp, yp=yp: e.tensor_copy(
                    out=x1tok[:, yp, dc * 128:(dc + 1) * 128], in_=pst[tp][:]),
                    extra_waits=[("pe", pe_t2)] +
                        ([gate[("x1tok", yp)]] if dc == 1 and ("x1tok", yp) in gate
                         else []))
                gate[("pst", tp)] = ("dve", d_x)
        g.wait("act", cnt["act"])
        g.wait("dve", cnt["dve"])
        ssem = f"sca{yp}"
        for r in range(4):
            g.op(lambda e, tc=tc, r=r, yp=yp: e.indirect_dma_start(
                out=xg_dram,
                out_offset=IndirectOffsetOnAxis(sidx[:, tc, r:r + 1], 0),
                in_=x1tok[:, yp, :], in_offset=None,
                bounds_check=regbox["b"], oob_is_err=False),
                inc=ssem, n=16)
            g.op(lambda e, tc=tc, r=r: e.indirect_dma_start(
                out=cg_dram,
                out_offset=IndirectOffsetOnAxis(sidx[:, tc, r:r + 1], 0),
                in_=vst2[:, tc, r, :], in_offset=None,
                bounds_check=regbox["b"], oob_is_err=False),
                inc=ssem, n=16)
        gate[("x1tok", yp)] = (ssem, cnt[ssem])
    sca_tot0, sca_tot1 = cnt["sca0"], cnt["sca1"]

    rt.close()

    rt2 = ExitStack()
    act_sh = sbr_(rt2, "act_sh", (128, IC, T), BF16)
    sgb = sbr_(rt2, "sgb", (128, T), BF16)
    silsh = sbr_(rt2, "silsh", (128, 2, 512), BF16)

    P.phase("F")
    t.wait("sca0", sca_tot0); t.wait("sca1", sca_tot1)
    s.wait("sca0", sca_tot0); s.wait("sca1", sca_tot1)
    v.wait("sca0", sca_tot0); v.wait("sca1", sca_tot1)
    for e_i in range(16, NE):
        src_row = lambda tq: sg_row[0:1, tq * 512:(tq + 1) * 512]
        t.wait("dve", cnt["dve"])
        if e_i == 16:
            t.wait_gate(gate.get("ps_sc"))
        for tq in range(4):
            cp = tq % 2
            t.wait_gate(gate.get(f"ps{cp}"))
            t.op(lambda e, tq=tq, cp=cp, src=src_row: e.matmul(
                ps[cp][:], ones_rb[:], src(tq), start=True, stop=True, skip_group_check=True), inc="pe")
            pe_cb = cnt["pe"]
            a_cb = s.do(lambda e, tq=tq, cp=cp: e.copy(
                out=sgb[:, tq * 512:(tq + 1) * 512], in_=ps[cp][:]),
                extra_waits=[("pe", pe_cb)])
            gate[f"ps{cp}"] = ("act", a_cb)
        a_cbl = cnt["act"]
        for i in range(IC):
            ip = i % 2
            gsem, usem = f"wg{ip}", f"wu{ip}"
            g.wait_gate(gate.get(("wg", ip)))
            dma(wg_sb[:, ip], dr["wg_all"][e_i, i], gsem)
            g.wait_gate(gate.get(("wu", ip)))
            dma(wu_sb[:, ip], dr["wu_all"][e_i, i], usem)
            g_done, u_done = cnt[gsem], cnt[usem]
            t.wait(gsem, g_done)
            t.wait(usem, u_done)
            for tq in range(4):
                tp = tq % 2
                t.wait_gate(gate.get(f"ps{tp}"))
                if i == 0 and tq < 2:
                    t.wait("act", a_cbl)
                for kd in range(KD):
                    t.op(lambda e, kd=kd, ip=ip, tq=tq, tp=tp: e.matmul(
                        ps[tp][:], wg_sb[:, ip, kd, :],
                        x_res[:, kd, tq * 512:(tq + 1) * 512],
                        start=(kd == 0), stop=(kd == KD - 1), skip_group_check=True),
                        inc="pe" if kd == KD - 1 else None)
                pe_g = cnt["pe"]
                t.wait_gate(gate.get(("psu", tp)))
                for kd in range(KD):
                    t.op(lambda e, kd=kd, ip=ip, tq=tq, tp=tp: e.matmul(
                        ps[2 + tp][:], wu_sb[:, ip, kd, :],
                        x_res[:, kd, tq * 512:(tq + 1) * 512],
                        start=(kd == 0), stop=(kd == KD - 1), skip_group_check=True),
                        inc="pe" if kd == KD - 1 else None)
                pe_u = cnt["pe"]
                a_si = s.do(lambda e, tp=tp: e.activation(
                    out=silsh[:, tp, :], in_=ps[tp][:], func=AF.Silu),
                    extra_waits=[("pe", pe_g), gate.get(("sil", tp))])
                gate[f"ps{tp}"] = ("act", a_si)
                d_u = v.do(lambda e, tp=tp: e.tensor_mul(
                    silsh[:, tp, :], silsh[:, tp, :], ps[2 + tp][:]),
                    extra_waits=[("act", a_si), ("pe", pe_u)])
                gate[("psu", tp)] = ("dve", d_u)
                d_sc = v.do(lambda e, i=i, tq=tq, tp=tp: e.tensor_mul(
                    act_sh[:, i, tq * 512:(tq + 1) * 512], silsh[:, tp, :],
                    sgb[:, tq * 512:(tq + 1) * 512]),
                    extra_waits=[("act", a_cbl)] if i == 0 else ())
                gate[("sil", tp)] = ("dve", d_sc)
            gate[("wg", ip)] = ("pe", pe_g)
            gate[("wu", ip)] = ("pe", pe_u)
        dv_A = cnt["dve"]
        first_b = True
        for dc in range(KD):
            dp = dc % 2
            dsem = f"wd{dp}"
            g.wait_gate(gate.get(("wd", dp)))
            dma(wd_sb[:, dp], dr["wd_all"][e_i, dc], dsem)
            d_done = cnt[dsem]
            t.wait(dsem, d_done)
            if first_b:
                t.wait("dve", dv_A)
                first_b = False
            for tq in range(4):
                yp2 = 4 + ((dc * 4 + tq) % 2)
                t.wait_gate(gate.get(f"ps{yp2}"))
                for i in range(IC):
                    t.op(lambda e, i=i, dp=dp, tq=tq, yp2=yp2: e.matmul(
                        ps[yp2][:], wd_sb[:, dp, i, :],
                        act_sh[:, i, tq * 512:(tq + 1) * 512],
                        start=(i == 0), stop=(i == IC - 1), skip_group_check=True),
                        inc="pe" if i == IC - 1 else None)
                pe_y = cnt["pe"]
                d_y = v.do(lambda e, dc=dc, tq=tq, yp2=yp2: e.tensor_add(
                    racc[:, dc, tq * 512:(tq + 1) * 512],
                    racc[:, dc, tq * 512:(tq + 1) * 512], ps[yp2][:]),
                    extra_waits=[("pe", pe_y)])
                gate[f"ps{yp2}"] = ("dve", d_y)
            gate[("wd", dp)] = ("pe", pe_y)

    rt2.close()
    xsc.close()

    fs = ExitStack()
    act_sp = sbr_(fs, "act_sp", (128, IC, CEXP), BF16)
    xgT = sbr_(fs, "xgT", (128, 2, KD, CEXP), BF16)
    ytk = sbr_(fs, "ytk", (128, CEXP // 128, DM), BF16)
    cgb = sbr_(fs, "cgb", (128, CEXP), BF16)
    cgrow = sbr_(fs, "cgrow", (1, 2 * CEXP), F32)
    silsp = sbr_(fs, "silsp", (128, 2, 512), BF16)
    ystage = sbr_(fs, "ystage", (128, 2, 128), BF16)

    CCH = [(0, 512), (512, 256)] if CEXP == 768 else [(0, 512), (512, CEXP - 512)]
    NCC = CEXP // 128

    P.phase("FS")
    yq.wait("sca0", sca_tot0)
    yq.wait("sca1", sca_tot1)
    for e_i in range(16):
        xp = e_i % 2
        xsem = f"xg{xp}"
        yq.wait_gate(gate.get(("xgT", xp)))
        for kd in range(KD):
            yq.op(lambda e, e_i=e_i, kd=kd, xp=xp: e.dma_start_transpose(
                out=xgT[:, xp, kd, :],
                in_=xg_dram[e_i * CEXP:(e_i + 1) * CEXP, kd * 128:(kd + 1) * 128]),
                inc=xsem, n=16)
        xg_done = cnt[xsem]
        yq.op(lambda e, e_i=e_i: e.dma_start(
            out=cgrow[:], in_=cg_dram[e_i * CEXP:(e_i + 1) * CEXP, :]
            .rearrange("c o -> (c o)").unsqueeze(0)), inc=xsem, n=16)
        cg_done = cnt[xsem]
        t.wait(xsem, cg_done)
        t.wait("dve", cnt["dve"])
        for ci, (c0, cw) in enumerate(CCH):
            cp = ci % 2
            t.wait_gate(gate.get(f"ps{cp}"))
            t.op(lambda e, c0=c0, cw=cw, cp=cp: e.matmul(
                ps[cp][:, 0:cw],
                ones_rf[:],
                cgrow[:].rearrange("o (c two) -> o c two", two=2)[:, c0:c0 + cw, 0],
                start=True, stop=True, skip_group_check=True), inc="pe")
            pe_cb = cnt["pe"]
            a_cb = s.do(lambda e, c0=c0, cw=cw, cp=cp: e.copy(
                out=cgb[:, c0:c0 + cw], in_=ps[cp][:, 0:cw]),
                extra_waits=[("pe", pe_cb)])
            gate[f"ps{cp}"] = ("act", a_cb)
        a_cbl = cnt["act"]
        for i in range(IC):
            ip = i % 2
            gsem, usem = f"wg{ip}", f"wu{ip}"
            g.wait_gate(gate.get(("wg", ip)))
            dma(wg_sb[:, ip], dr["wg_all"][e_i, i], gsem)
            g.wait_gate(gate.get(("wu", ip)))
            dma(wu_sb[:, ip], dr["wu_all"][e_i, i], usem)
            g_done, u_done = cnt[gsem], cnt[usem]
            t.wait(gsem, g_done)
            t.wait(usem, u_done)
            if i == 0:
                t.wait(xsem, cg_done)
            for ci, (c0, cw) in enumerate(CCH):
                tp = ci % 2
                t.wait_gate(gate.get(f"ps{tp}"))
                if i == 0:
                    t.wait("act", a_cbl)
                for kd in range(KD):
                    t.op(lambda e, kd=kd, ip=ip, c0=c0, cw=cw, tp=tp, xp=xp: e.matmul(
                        ps[tp][:, 0:cw], wg_sb[:, ip, kd, :],
                        xgT[:, xp, kd, c0:c0 + cw],
                        start=(kd == 0), stop=(kd == KD - 1), skip_group_check=True),
                        inc="pe" if kd == KD - 1 else None)
                pe_g = cnt["pe"]
                t.wait_gate(gate.get(("psu", tp)))
                for kd in range(KD):
                    t.op(lambda e, kd=kd, ip=ip, c0=c0, cw=cw, tp=tp, xp=xp: e.matmul(
                        ps[2 + tp][:, 0:cw], wu_sb[:, ip, kd, :],
                        xgT[:, xp, kd, c0:c0 + cw],
                        start=(kd == 0), stop=(kd == KD - 1), skip_group_check=True),
                        inc="pe" if kd == KD - 1 else None)
                pe_u = cnt["pe"]
                a_si = s.do(lambda e, cw=cw, tp=tp: e.activation(
                    out=silsp[:, tp, 0:cw], in_=ps[tp][:, 0:cw], func=AF.Silu),
                    extra_waits=[("pe", pe_g), gate.get(("sil", tp))])
                gate[f"ps{tp}"] = ("act", a_si)
                d_u = v.do(lambda e, cw=cw, tp=tp: e.tensor_mul(
                    silsp[:, tp, 0:cw], silsp[:, tp, 0:cw], ps[2 + tp][:, 0:cw]),
                    extra_waits=[("act", a_si), ("pe", pe_u)])
                gate[("psu", tp)] = ("dve", d_u)
                d_sc = v.do(lambda e, i=i, c0=c0, cw=cw, tp=tp: e.tensor_mul(
                    act_sp[:, i, c0:c0 + cw], silsp[:, tp, 0:cw],
                    cgb[:, c0:c0 + cw]))
                gate[("sil", tp)] = ("dve", d_sc)
            gate[("wg", ip)] = ("pe", pe_g)
            gate[("wu", ip)] = ("pe", pe_u)
        gate[("xgT", xp)] = ("pe", cnt["pe"])
        dv_A = cnt["dve"]
        first_b = True
        for dc in range(KD):
            dp = dc % 2
            dsem = f"wd{dp}"
            g.wait_gate(gate.get(("wd", dp)))
            dma(wd_sb[:, dp], dr["wd_all"][e_i, dc], dsem)
            d_done = cnt[dsem]
            t.wait(dsem, d_done)
            if first_b:
                t.wait("dve", dv_A)
                first_b = False
            for cc in range(NCC):
                yp2 = 4 + (cc % 2)
                t.wait_gate(gate.get(f"ps{yp2}"))
                for i in range(IC):
                    t.op(lambda e, i=i, dp=dp, cc=cc, yp2=yp2: e.matmul(
                        ps[yp2][:, 0:128], wd_sb[:, dp, i, :],
                        act_sp[:, i, cc * 128:(cc + 1) * 128],
                        start=(i == 0), stop=(i == IC - 1), skip_group_check=True),
                        inc="pe" if i == IC - 1 else None)
                pe_y = cnt["pe"]
                a_ys = s.do(lambda e, cc=cc, yp2=yp2: e.copy(
                    out=ystage[:, cc % 2, :], in_=ps[yp2][:, 0:128]),
                    extra_waits=[("pe", pe_y), gate.get(("yst", cc % 2))])
                gate[f"ps{yp2}"] = ("act", a_ys)
                tp2 = cc % 2
                t.wait("act", a_ys)
                t.wait_gate(gate.get(("pst", tp2)))
                t.op(lambda e, cc=cc, tp2=tp2: e.transpose(
                    pst[tp2][:], ystage[:, cc % 2, :], identb[:]), inc="pe")
                pe_tt = cnt["pe"]
                gate[("yst", cc % 2)] = ("pe", pe_tt)
                d_yt = v.do(lambda e, dc=dc, cc=cc, tp2=tp2: e.tensor_copy(
                    out=ytk[:, cc, dc * 128:(dc + 1) * 128], in_=pst[tp2][:]),
                    extra_waits=[("pe", pe_tt)] +
                        ([("ya", gate.get("ytk_rd", (None, 0))[1])]
                         if dc == 0 and cc < 2 and "ytk_rd" in gate else []))
                gate[("pst", tp2)] = ("dve", d_yt)
            gate[("wd", dp)] = ("pe", pe_y)
        dv_ytk = cnt["dve"]
        yq.wait("dve", dv_ytk)
        for cc in range(NCC):
            yq.op(lambda e, e_i=e_i, cc=cc: e.dma_start(
                out=y_all_dram[e_i * CEXP + cc * 128:e_i * CEXP + (cc + 1) * 128, :],
                in_=ytk[:, cc, :]), inc="ya", n=16)
        gate["ytk_rd"] = ("ya", cnt["ya"])
    ya_total = cnt["ya"]

    fs.close()

    epi = ExitStack()
    ytok = sbr_(epi, "ytok", (128, 2, DM), BF16)
    ysq = sbr_(epi, "ysq", (128, DM), BF16)
    yout = sbr_(epi, "yout", (128, 2, DM), F32)
    cols = sbr_(epi, "cols", (128, 8), F32)
    yr = sbr_(epi, "yr", (128, 2, 4, DM), BF16)

    P.phase("EPI")
    t.wait("ya", ya_total)
    s.wait("ya", ya_total)
    v.wait("ya", ya_total)
    regbox2 = {}
    g.op(lambda e: regbox2.__setitem__("b", e.to_reg(16 * CEXP - 1)) or e.engine_nop())
    g.wait("ya", ya_total)
    dv_racc_done = cnt["dve"]
    t.wait("dve", dv_racc_done)
    last_dv_y = 0
    last_comb = {}

    def emit_transposes(tc):
        yp = tc % 2
        g.wait_gate(gate.get(("yr", yp)))
        for r in range(4):
            g.op(lambda e, tc=tc, r=r, yp=yp: e.indirect_dma_start(
                out=yr[:, yp, r, :], out_offset=None,
                in_=y_all_dram,
                in_offset=IndirectOffsetOnAxis(sidx[:, tc, r:r + 1], 0),
                bounds_check=regbox2["b"], oob_is_err=False),
                inc=f"yg{yp}", n=16)
        yg_done = (f"yg{yp}", cnt[f"yg{yp}"])
        for dc in range(KD):
            tp = dc % 2
            t.wait_gate(gate.get(("pst", tp)))
            t.op(lambda e, dc=dc, tc=tc, tp=tp: e.transpose(
                pst[tp][:], racc[:, dc, tc * 128:(tc + 1) * 128], identb[:]),
                inc="pe")
            pe_t = cnt["pe"]
            if dc % 2 == 0:
                a_yt = s.do(lambda e, dc=dc, tp=tp, yp=yp: e.copy(
                    out=ytok[:, yp, dc * 128:(dc + 1) * 128], in_=pst[tp][:]),
                    extra_waits=[("pe", pe_t)] +
                        ([("dve", last_dv_y)] if dc == 0 and tc > 1 else []))
                gate[("pst", tp)] = ("act", a_yt)
            else:
                d_yt = v.do(lambda e, dc=dc, tp=tp, yp=yp: e.tensor_copy(
                    out=ytok[:, yp, dc * 128:(dc + 1) * 128], in_=pst[tp][:]),
                    extra_waits=[("pe", pe_t)])
                gate[("pst", tp)] = ("dve", d_yt)
        return cnt["act"], cnt["dve"], yg_done

    def emit_stats(tc, a_ytl, d_ytl, yg_done):
        yp = tc % 2
        op = tc % 2
        osem = f"out{op}"
        v.do(lambda e, yp=yp: e.tensor_add(ysq[:], yr[:, yp, 0, :], yr[:, yp, 1, :]),
             extra_waits=[yg_done, ("act", a_ytl)])
        v.do(lambda e, yp=yp: e.tensor_add(ytok[:, yp, :], ytok[:, yp, :], ysq[:]))
        v.do(lambda e, yp=yp: e.tensor_add(ysq[:], yr[:, yp, 2, :], yr[:, yp, 3, :]))
        dv_comb = v.do(lambda e, yp=yp: e.tensor_add(ytok[:, yp, :], ytok[:, yp, :],
                                                     ysq[:]))
        gate[("yr", yp)] = ("dve", dv_comb)
        v.do(lambda e, yp=yp: e.tensor_reduce(cols[:, 0:1], ytok[:, yp, :], AX,
                                              ALU.add))
        v.do(lambda e: e.tensor_scalar_mul(cols[:, 0:1], cols[:, 0:1], 1.0 / DM))
        a_q = s.do(lambda e, yp=yp: e.activation(out=ysq[:], in_=ytok[:, yp, :],
                                                 func=AF.Square),
                   extra_waits=[("dve", cnt["dve"])])
        v.do(lambda e: e.tensor_reduce(cols[:, 1:2], ysq[:], AX, ALU.add),
             extra_waits=[("act", a_q)])
        v.do(lambda e: e.tensor_scalar_mul(cols[:, 1:2], cols[:, 1:2], 1.0 / DM))
        v.do(lambda e: e.tensor_mul(cols[:, 2:3], cols[:, 0:1], cols[:, 0:1]))
        v.do(lambda e: e.tensor_sub(cols[:, 3:4], cols[:, 1:2], cols[:, 2:3]))
        dv_v = v.do(lambda e: e.tensor_scalar_add(cols[:, 3:4], cols[:, 3:4], EPS))
        s.do(lambda e: e.activation(out=cols[:, 4:5], in_=cols[:, 3:4],
                                    func=AF.Ln),
             extra_waits=[("dve", dv_v)])
        a_r = s.do(lambda e: e.activation(out=cols[:, 5:6], in_=cols[:, 4:5],
                                          func=AF.Exp, scale=-0.5))
        dv_y = v.do(lambda e, op=op, yp=yp: e.tensor_scalar(
            out=yout[:, op, :], in0=ytok[:, yp, :], scalar1=cols[:, 0:1],
            scalar2=cols[:, 5:6], op0=ALU.subtract, op1=ALU.mult),
            extra_waits=[("act", a_r), (osem, cnt[osem])])
        yq.wait("dve", dv_y)
        yq.op(lambda e, tc=tc, op=op: e.dma_start(
            out=y[tc * 128:(tc + 1) * 128, :], in_=yout[:, op, :]),
            inc=osem, n=16)
        return dv_y

    pend = None
    for tc in range(KD + 1):
        if tc < KD:
            marks = emit_transposes(tc)
        if pend is not None:
            last_dv_y = emit_stats(pend[0], *pend[1])
        pend = (tc, marks) if tc < KD else None
    yq.wait("out0", cnt["out0"])
    yq.wait("out1", cnt["out1"])

    epi.close()
    ffn.close()
    P.replay()
    es.close()
    return nc


_NC_CACHE = None


def kernel(**inputs):
    global _NC_CACHE
    hp = host_prep(inputs)
    if _NC_CACHE is None:
        _NC_CACHE = build_nc()
    nc = _NC_CACHE
    shared = {k: hp[k] for k in ["wqt", "wkt", "wvt", "wot", "routert", "shgw",
                                 "wg_all", "wu_all", "wd_all", "maskR",
                                 "pswap", "identb", "identf", "iota16"]}
    in_maps = []
    for c in range(NCORES):
        m = dict(shared)
        m["xm"] = hp["xm"][c]
        m["xh"] = hp["xh"][c]
        m["c2"] = hp["c2"][c]
        m["s2"] = hp["s2"][c]
        m["maskA0"] = hp["maskA0"] if c == 0 else hp["maskR"]
        in_maps.append(m)
    res = run_bass_kernel_spmd(nc, in_maps, core_ids=list(range(NCORES)))
    out = np.concatenate([res.results[c]["y"] for c in range(NCORES)], axis=0)
    return out.reshape(1, NCORES * T, DM)


# version 08
# speedup vs baseline: 1.5272x; 1.3726x over previous
import math
import numpy as np
import ml_dtypes
from contextlib import ExitStack

import concourse.bass as bass
import concourse.mybir as mybir
from concourse.bass_utils import run_bass_kernel_spmd
from concourse.bass import IndirectOffsetOnAxis

F32 = mybir.dt.float32
BF16 = mybir.dt.bfloat16
AF = mybir.ActivationFunctionType
ALU = mybir.AluOpType
AX = mybir.AxisListType.X

NCORES = 8
T = 2048
W = 512
NB = 4
D = 128
H = 16
HK = 4
DM = 2048
KD = 16
NE = 20
IC = 11
CEXP = 768
CAP = 20.0
EPS = 1e-5
INV_SQRT_D = 1.0 / math.sqrt(D)

BF = ml_dtypes.bfloat16



class Prog:

    def __init__(self, nc, es):
        self.nc = nc
        self.cnt = {}
        self.sems = {}
        self.phases = []
        self.cur = None

    def mksem(self, es, name):
        self.sems[name] = es.enter_context(self.nc.semaphore(name))
        self.cnt[name] = 0

    def phase(self, name):
        self.cur = {"name": name, "t": [], "v": [], "s": [], "g": [], "y": []}
        self.phases.append(self.cur)

    def replay(self):
        for ph in self.phases:
            with self.nc.Block() as blk:
                if ph["g"]:
                    @blk.gpsimd
                    def _(g, ops=ph["g"]):
                        for f in ops:
                            f(g)
                if ph["t"]:
                    @blk.tensor
                    def _(t, ops=ph["t"]):
                        for f in ops:
                            f(t)
                if ph["s"]:
                    @blk.scalar
                    def _(s, ops=ph["s"]):
                        for f in ops:
                            f(s)
                if ph["v"]:
                    @blk.vector
                    def _(v, ops=ph["v"]):
                        for f in ops:
                            f(v)
                if ph["y"]:
                    @blk.sync
                    def _(y, ops=ph["y"]):
                        for f in ops:
                            f(y)


class ES_:
    def __init__(self, prog, key, own=None):
        self.p = prog
        self.key = key
        self.own = own

    def raw(self, fn):
        self.p.cur[self.key].append(fn)

    def wait(self, sem, val):
        if val is None or val <= 0:
            return
        self.raw(lambda eng, s=self.p.sems[sem], v=val: eng.wait_ge(s, v))

    def wait_gate(self, gate):
        if gate:
            self.wait(gate[0], gate[1])

    def op(self, fn, inc=None, n=1):
        if inc is None:
            self.raw(lambda eng, f=fn: f(eng))
            return None
        self.p.cnt[inc] += n
        tgt = self.p.cnt[inc]
        self.raw(lambda eng, f=fn, s=self.p.sems[inc], m=n: f(eng).then_inc(s, m))
        return tgt

    def do(self, fn, extra_waits=()):
        for sv in extra_waits:
            if sv:
                self.wait(sv[0], sv[1])
        self.wait(self.own, self.p.cnt[self.own])
        return self.op(fn, inc=self.own)



def _tile_lhsT(wT, oc_chunks):
    din, dout = wT.shape
    assert din == DM and dout == oc_chunks * 128
    return np.ascontiguousarray(
        wT.reshape(KD, 128, oc_chunks, 128).transpose(2, 1, 0, 3))


def host_prep(inputs):
    d = {}
    x = np.asarray(inputs["L0_attn_x"])[0]
    xT = np.ascontiguousarray(x.T)
    xT4 = xT.reshape(KD, 128, NCORES * T)
    xm, xh = [], []
    for c in range(NCORES):
        xm.append(np.ascontiguousarray(
            xT4[:, :, c * T:(c + 1) * T].transpose(1, 0, 2)))
        h0 = c * T - W if c > 0 else 0
        xh.append(np.ascontiguousarray(
            xT4[:, :, h0:h0 + W].transpose(1, 0, 2)))
    d["xm"], d["xh"] = xm, xh

    d["wqt"] = _tile_lhsT(np.ascontiguousarray(np.asarray(inputs["L0_attn_wq"]).T), 16)
    d["wkt"] = _tile_lhsT(np.ascontiguousarray(np.asarray(inputs["L0_attn_wk"]).T), 4)
    d["wot"] = _tile_lhsT(np.ascontiguousarray(np.asarray(inputs["L0_attn_wo"]).T), 16)
    d["wvt"] = np.ascontiguousarray(
        np.ascontiguousarray(np.asarray(inputs["L0_attn_wv"]).T)
        .reshape(KD, 128, 512).transpose(1, 0, 2))
    d["routert"] = np.ascontiguousarray(
        np.ascontiguousarray(np.asarray(inputs["L0_ffn_router_w"]).T)
        .reshape(KD, 128, 16).transpose(1, 0, 2))
    d["shgw"] = np.ascontiguousarray(
        np.asarray(inputs["L0_ffn_sh_gate_w"]).reshape(KD, 128, 1).transpose(1, 0, 2))

    wg = np.asarray(inputs["L0_ffn_w_gate"])
    wu = np.asarray(inputs["L0_ffn_w_up"])
    wd = np.asarray(inputs["L0_ffn_w_down"])
    shg = np.asarray(inputs["L0_ffn_sh_gate"])
    shu = np.asarray(inputs["L0_ffn_sh_up"])
    shd = np.asarray(inputs["L0_ffn_sh_down"])

    def tile_gu(m):
        return m.reshape(KD, 128, IC, 128).transpose(2, 1, 0, 3)

    def tile_dn(m):
        return m.reshape(IC, 128, KD, 128).transpose(2, 1, 0, 3)

    wg_all = np.empty((NE, IC, 128, KD, 128), np.float32)
    wu_all = np.empty((NE, IC, 128, KD, 128), np.float32)
    wd_all = np.empty((NE, KD, 128, IC, 128), np.float32)
    for e in range(16):
        wg_all[e] = tile_gu(wg[e])
        wu_all[e] = tile_gu(wu[e])
        wd_all[e] = tile_dn(wd[e])
    for sx in range(4):
        wg_all[16 + sx] = tile_gu(shg[:, sx * 1408:(sx + 1) * 1408])
        wu_all[16 + sx] = tile_gu(shu[:, sx * 1408:(sx + 1) * 1408])
        wd_all[16 + sx] = tile_dn(shd[sx * 1408:(sx + 1) * 1408])
    d["wg_all"], d["wu_all"], d["wd_all"] = wg_all, wu_all, wd_all

    inv = 10000.0 ** (-np.arange(0, D, 2, dtype=np.float64) / D)
    c2s, s2s = [], []
    for c in range(NCORES):
        pos = np.arange(c * T - W, c * T + T, dtype=np.float64)
        ang = np.outer(pos, inv)
        cosv = np.cos(ang).T.astype(np.float32)
        sinv = np.sin(ang).T.astype(np.float32)
        c2 = np.empty((128, T + W), np.float32)
        s2 = np.empty((128, T + W), np.float32)
        c2[0::2] = cosv
        c2[1::2] = cosv
        s2[0::2] = -sinv
        s2[1::2] = sinv
        c2s.append(c2.astype(BF))
        s2s.append(s2.astype(BF))
    d["c2"], d["s2"] = c2s, s2s

    j = np.arange(1024)[:, None]
    a = np.arange(W)[None, :]
    mR = ((j < W) & (j >= a)) | ((j >= W) & (j - W <= a))
    mA0 = (j >= W) & (j - W <= a)
    d["maskR"] = np.ascontiguousarray(
        mR.astype(BF).reshape(8, 128, W).transpose(1, 0, 2))
    d["maskA0"] = np.ascontiguousarray(
        mA0.astype(BF).reshape(8, 128, W).transpose(1, 0, 2))

    d["iota16"] = np.tile(np.arange(16, dtype=np.float32), (128, 1))

    psw = np.zeros((128, 128), np.float32)
    i2 = np.arange(0, 128, 2)
    psw[i2, i2 + 1] = 1.0
    psw[i2 + 1, i2] = 1.0
    d["pswap"] = psw.astype(BF)
    d["identb"] = np.eye(128).astype(BF)
    d["identf"] = np.eye(128, dtype=np.float32)
    return d



def build_nc():
    nc = bass.Bass("TRN2", target_bir_lowering=False)

    dr = {}
    def din(name, shape, dt=F32):
        dr[name] = nc.dram_tensor(name, list(shape), dt, kind="ExternalInput").ap()
    din("xm", (128, KD, T)); din("xh", (128, KD, W))
    din("wqt", (16, 128, KD, 128)); din("wkt", (4, 128, KD, 128))
    din("wvt", (128, KD, 512)); din("wot", (16, 128, KD, 128))
    din("routert", (128, KD, 16)); din("shgw", (128, KD, 1))
    din("wg_all", (NE, IC, 128, KD, 128)); din("wu_all", (NE, IC, 128, KD, 128))
    din("wd_all", (NE, KD, 128, IC, 128))
    din("c2", (128, T + W), BF16); din("s2", (128, T + W), BF16)
    din("maskR", (128, 8, W), BF16); din("maskA0", (128, 8, W), BF16)
    din("pswap", (128, 128), BF16); din("identb", (128, 128), BF16)
    din("identf", (128, 128))
    y = nc.dram_tensor("y", [T, DM], F32, kind="ExternalOutput").ap()
    din("iota16", (128, 16))
    xg_dram = nc.dram_tensor("xg_scr", [16 * CEXP, DM], BF16, kind="Internal").ap()
    cg_dram = nc.dram_tensor("cg_scr", [16 * CEXP, 2], F32, kind="Internal").ap()
    y_all_dram = nc.dram_tensor("yall_scr", [16 * CEXP, DM], BF16, kind="Internal").ap()

    es = ExitStack()
    P = Prog(nc, es)
    for sname in ["p0", "wq0", "wq1", "wq2", "wg0", "wg1", "wu0", "wu1", "wd0", "wd1",
                  "cr0", "cr1", "cmb", "pe", "act", "dve", "out0", "out1",
                  "xg0", "xg1", "ya", "sca0", "sca1", "yg0", "yg1"]:
        P.mksem(es, sname)
    t = ES_(P, "t")
    v = ES_(P, "v", own="dve")
    s = ES_(P, "s", own="act")
    g = ES_(P, "g")
    yq = ES_(P, "y")
    cnt = P.cnt
    gate = {}

    def sb(stack, name, shape, dt):
        return stack.enter_context(nc.sbuf_tensor(name, list(shape), dt))

    ones_cb = sb(es, "ones_cb", (128, 1), BF16)
    ones_rb = sb(es, "ones_rb", (1, 128), BF16)
    ones_rf = sb(es, "ones_rf", (1, 128), F32)
    identb = sb(es, "identb_t", (128, 128), BF16)
    xsc = ExitStack()
    x_res = sb(xsc, "x_res", (128, KD, T), BF16)

    ps = [es.enter_context(nc.psum_tensor(f"ps{i}", [128, 512], F32))
          for i in range(2)]
    ps_sc = es.enter_context(nc.psum_tensor("ps_sc", [128, 1024], F32))
    ps.append(ps_sc.ap()[:, 0:512])
    ps.append(ps_sc.ap()[:, 512:1024])
    ps += [es.enter_context(nc.psum_tensor(f"ps{i}", [128, 512], F32))
           for i in (4, 5)]
    pst = [es.enter_context(nc.psum_tensor(f"pst{i}", [128, 128], BF16))
           for i in range(2)]

    def dma(ap_out, ap_in, sem):
        return g.op(lambda e, o=ap_out, i=ap_in: e.dma_start(out=o, in_=i),
                    inc=sem, n=16)

    att = ExitStack()
    pswap = sb(att, "pswap_t", (128, 128), BF16)
    wvt_sb = sb(att, "wvt_sb", (128, KD, 512), BF16)
    kT = sb(att, "kT", (128, 2, HK, W), BF16)
    v_tok = sb(att, "v_tok", (128, 2, 4, W), BF16)
    c2_sb = sb(att, "c2_sb", (128, W), BF16)
    s2_sb = sb(att, "s2_sb", (128, W), BF16)
    mask_sb = sb(att, "mask_sb", (128, 8, W), BF16)
    wqo_sb = sb(att, "wqo_sb", (128, 3, KD, 128), BF16)
    qT = sb(att, "qT", (128, 2, W), BF16)
    q_raw = sb(att, "q_raw", (128, 2, W), BF16)
    rtmp1 = sb(att, "rtmp1", (128, W), F32)
    rtmp2 = sb(att, "rtmp2", (128, W), F32)
    expT = sb(att, "expT", (128, 2, 8, W), BF16)
    etmp = sb(att, "etmp", (128, 2, 2 * W), BF16)
    ttmp = sb(att, "ttmp", (128, 2 * W), F32)
    rb_sb = sb(att, "rb_sb", (128, 2, W), F32)
    recip_row = sb(att, "recip_row", (1, 2, W), F32)
    attn_out = sb(att, "attn_out", (128, KD, W), BF16)
    s1_pp = sb(att, "s1_pp", (128, 2, W), BF16)
    sq_pp = sb(att, "sq_pp", (128, 2, W), BF16)
    lnrow = sb(att, "lnrow", (1, 3, W), F32)
    mb_sb = sb(att, "mb_sb", (128, W), F32)
    rb2_sb = sb(att, "rb2_sb", (128, W), F32)
    ln_t = sb(att, "ln_t", (128, W), F32)

    halo = ExitStack()
    x_halo = sb(halo, "x_halo", (128, KD, W), BF16)

    def x_blk(n):
        return [x_res[:, kd, n * W:(n + 1) * W] for kd in range(KD)]

    wplan = {"aps": [], "next": 0, "cnt": {}}

    def wreset(aps):
        wplan["aps"] = aps
        wplan["next"] = 0
        wplan["cnt"] = {}

    def wpump(upto):
        while wplan["next"] < min(upto, len(wplan["aps"])):
            i = wplan["next"]
            slot = i % 3
            g.wait_gate(gate.get(("wqo", slot)))
            dma(wqo_sb[:, slot], wplan["aps"][i], f"wq{slot}")
            wplan["cnt"][i] = cnt[f"wq{slot}"]
            wplan["next"] += 1

    def wwait(i):
        t.wait(f"wq{i % 3}", wplan["cnt"][i])

    def wdone(i, pe_val):
        gate[("wqo", i % 3)] = ("pe", pe_val)
        wpump(i + 4)

    def kv_block(n, x_src, widx0):
        slot = n % 2
        for kvh in range(HK):
            wi = widx0 + kvh
            sp3 = wi % 3
            wwait(wi)
            t.wait_gate(gate.get("ps0"))
            t.wait_gate(gate.get(("qraw_rd", kvh % 2)))
            for kd in range(KD):
                t.op(lambda e, kd=kd, sp3=sp3, xs=x_src: e.matmul(
                    ps[0][:], wqo_sb[:, sp3, kd, :], xs[kd],
                    start=(kd == 0), stop=(kd == KD - 1), skip_group_check=True),
                    inc="pe" if kd == KD - 1 else None)
            pe_k = cnt["pe"]
            wdone(wi, pe_k)
            pp = kvh % 2
            a_ev = s.do(lambda e, pp=pp: e.copy(out=q_raw[:, pp, :], in_=ps[0][:]),
                        extra_waits=[("pe", pe_k), ("dve", cnt["dve"])])
            gate["ps0"] = ("act", a_ev)
            t.wait("act", a_ev)
            t.wait_gate(gate.get("ps1"))
            t.op(lambda e, pp=pp: e.matmul(ps[1][:], pswap[:], q_raw[:, pp, :],
                                           start=True, stop=True, skip_group_check=True), inc="pe")
            pe_sh = cnt["pe"]
            v.do(lambda e, pp=pp: e.tensor_mul(
                rtmp1[:], q_raw[:, pp, :], c2_sb[:]),
                extra_waits=[("act", a_ev), ("p0", cnt["p0"])])
            d2 = v.do(lambda e: e.tensor_mul(
                rtmp2[:], ps[1][:], s2_sb[:]),
                extra_waits=[("pe", pe_sh)])
            gate["ps1"] = ("dve", d2)
            dk = v.do(lambda e, slot=slot, kvh=kvh: e.tensor_add(
                kT[:, slot, kvh, :], rtmp1[:], rtmp2[:]))
            gate[("qraw_rd", pp)] = ("dve", dk)
        for jc in range(4):
            pp = jc % 2
            if pp == 0:
                t.wait_gate(gate.get("ps_sc"))
            for kd in range(KD):
                t.op(lambda e, kd=kd, jc=jc, pp=pp, xs=x_src: e.matmul(
                    ps[2 + pp][:], xs[kd][:, jc * 128:(jc + 1) * 128],
                    wvt_sb[:, kd, :],
                    start=(kd == 0), stop=(kd == KD - 1), skip_group_check=True),
                    inc="pe" if kd == KD - 1 else None)
            pe_v = cnt["pe"]
            a_v = s.do(lambda e, slot=slot, jc=jc, pp=pp: e.copy(
                out=v_tok[:, slot, jc, :], in_=ps[2 + pp][:]),
                extra_waits=[("pe", pe_v)])
            gate["ps_sc"] = ("act", a_v)

    P.phase("P0")
    for kq in range(4):
        dma(x_res[:, kq * 4:(kq + 1) * 4, :], dr["xm"][:, kq * 4:(kq + 1) * 4, :], "p0")
    dma(x_halo[:], dr["xh"], "p0")
    dma(wvt_sb[:], dr["wvt"], "p0")
    dma(c2_sb[:], dr["c2"][:, 0:W], "p0")
    dma(s2_sb[:], dr["s2"][:, 0:W], "p0")
    dma(mask_sb[:], dr["maskA0"], "p0")
    dma(pswap[:], dr["pswap"], "p0")
    dma(identb[:], dr["identb"], "p0")
    p0_total = cnt["p0"]
    v.do(lambda e: e.memset(ones_cb[:], 1.0))
    v.do(lambda e: e.memset(ones_rf[:], 1.0))
    v.do(lambda e: e.memset(ones_rb[:], 1.0))
    t.wait("p0", p0_total)
    t.wait("dve", cnt["dve"])
    full_plan = [dr["wkt"][kvh] for kvh in range(HK)]
    for _n in range(NB):
        full_plan += [dr["wkt"][kvh] for kvh in range(HK)]
        full_plan += [dr["wqt"][h] for h in range(H)]
        full_plan += [dr["wot"][dc] for dc in range(KD)]
    wreset(full_plan)
    wpump(3)
    kv_block(-1, [x_halo[:, kd, :] for kd in range(KD)], 0)

    P.phase("ATT")
    zrow = x_halo[:, 0:4, :]
    dv_z = v.do(lambda e: e.memset(zrow, 0.0))
    g.wait("dve", dv_z)
    g.op(lambda e: e.dma_start(
        out=cg_dram[:].rearrange("a b -> (a b)").rearrange("(p n) -> p n", p=128),
        in_=zrow[:, 0, 0:192]), inc="p0", n=16)
    yq.wait("dve", dv_z)
    for zi in range(16 * CEXP // 128):
        yq.op(lambda e, zi=zi: e.dma_start(
            out=xg_dram[zi * 128:(zi + 1) * 128, :],
            in_=zrow.rearrange("p a b -> p (a b)")), inc="out0", n=16)
    zx_total = cnt["out0"]

    def emit_qproj(n, h, xb, wbase0):
        hp = h % 2
        wi = wbase0 + 4 + h
        sp3 = wi % 3
        wwait(wi)
        t.wait_gate(gate.get("ps0"))
        for kd in range(KD):
            t.op(lambda e, kd=kd, sp3=sp3, xb=xb: e.matmul(
                ps[0][:], wqo_sb[:, sp3, kd, :], xb[kd],
                start=(kd == 0), stop=(kd == KD - 1), skip_group_check=True),
                inc="pe" if kd == KD - 1 else None)
        pe_q = cnt["pe"]
        wdone(wi, pe_q)
        a_ev = s.do(lambda e, hp=hp: e.copy(out=q_raw[:, hp, :], in_=ps[0][:]),
                    extra_waits=[("pe", pe_q), ("dve", cnt["dve"])])
        gate["ps0"] = ("act", a_ev)
        t.wait("act", a_ev)
        t.wait_gate(gate.get("ps1"))
        t.op(lambda e, hp=hp: e.matmul(ps[1][:], pswap[:], q_raw[:, hp, :],
                                       start=True, stop=True, skip_group_check=True), inc="pe")
        pe_sh = cnt["pe"]
        v.do(lambda e, hp=hp: e.tensor_mul(
            rtmp1[:], q_raw[:, hp, :], c2_sb[:]),
            extra_waits=[("act", a_ev), ("p0", cnt["p0"])])
        d2 = v.do(lambda e: e.tensor_mul(
            rtmp2[:], ps[1][:], s2_sb[:]),
            extra_waits=[("pe", pe_sh)])
        gate["ps1"] = ("dve", d2)
        dv_q = v.do(lambda e, hp=hp: e.tensor_add(
            qT[:, hp, :], rtmp1[:], rtmp2[:]),
            extra_waits=[gate[("qT", hp)]] if ("qT", hp) in gate else ())
        gate[("qraw_rd", hp)] = ("dve", dv_q)
        return dv_q

    def emit_scores(n, h, dv_q, msk):
        kvh = h // 4
        hp = h % 2
        slot, pslot = n % 2, (n - 1) % 2
        t.wait("dve", dv_q)
        dms = []
        for jp in range(4):
            jc0 = 2 * jp
            pv = jp % 2
            t.wait_gate(gate.get("ps_sc"))
            for half in range(2):
                jc = jc0 + half
                sl = pslot if jc < 4 else slot
                jj = jc % 4
                t.op(lambda e, sl=sl, kvh=kvh, jj=jj, hp=hp, half=half: e.matmul(
                    ps[2 + half][:], kT[:, sl, kvh, jj * 128:(jj + 1) * 128],
                    qT[:, hp, :], start=True, stop=True, skip_group_check=True),
                    inc="pe" if half == 1 else None)
            pe_sc = cnt["pe"]
            a_t = s.do(lambda e: e.activation(
                out=ttmp[:], in_=ps_sc[:], func=AF.Tanh,
                scale=INV_SQRT_D / CAP), extra_waits=[("pe", pe_sc)])
            gate["ps_sc"] = ("act", a_t)
            a_e = s.do(lambda e, pv=pv: e.activation(
                out=etmp[:, pv, :], in_=ttmp[:], func=AF.Exp,
                scale=CAP), extra_waits=[gate.get(("etmp", pv))])
            d_m = v.do(lambda e, jc0=jc0, hp=hp, pv=pv, msk=msk: e.tensor_mul(
                expT[:, hp, jc0:jc0 + 2, :], etmp[:, pv, :].rearrange("p (a b) -> p a b", a=2),
                msk[:, jc0:jc0 + 2, :]),
                extra_waits=[("act", a_e), ("p0", cnt["p0"])] +
                    ([gate[("expT", hp)]] if jp == 0 and ("expT", hp) in gate
                     else []))
            gate[("etmp", pv)] = ("dve", d_m)
            dms.append((jc0, d_m))
            dms.append((jc0 + 1, d_m))
        return dms

    def emit_sumsout(n, h, dms):
        kvh = h // 4
        hp = h % 2
        slot, pslot = n % 2, (n - 1) % 2
        first = True
        for jc, d_m in dms:
            sl = pslot if jc < 4 else slot
            jj = jc % 4
            t.wait("dve", d_m)
            t.op(lambda e, jc=jc, hp=hp, st=first: e.matmul(
                ps[4][0:1, :], ones_cb[:], expT[:, hp, jc, :],
                start=st, stop=(jc == 7), skip_group_check=True), inc="pe")
            t.op(lambda e, sl=sl, jj=jj, kvh=kvh, jc=jc, hp=hp, st=first: e.matmul(
                ps[5][:], v_tok[:, sl, jj, kvh * 128:(kvh + 1) * 128],
                expT[:, hp, jc, :], start=st, stop=(jc == 7), skip_group_check=True),
                inc="pe" if jc == 7 else None)
            first = False
        pe_att = cnt["pe"]
        gate[("expT", hp)] = ("pe", pe_att)
        gate[("qT", hp)] = ("pe", pe_att)
        d_r = v.do(lambda e, hp=hp: e.reciprocal(
            out=recip_row[:, hp, :], in_=ps[4][0:1, :]),
            extra_waits=[("pe", pe_att)])
        t.wait("dve", d_r)
        t.wait_gate(gate.get("ps0"))
        t.op(lambda e, hp=hp: e.matmul(ps[0][:], ones_rf[:],
                                       recip_row[:, hp, :],
                                       start=True, stop=True, skip_group_check=True), inc="pe")
        pe_rb = cnt["pe"]
        a_rb = s.do(lambda e, hp=hp: e.copy(out=rb_sb[:, hp, :], in_=ps[0][:]),
                    extra_waits=[("pe", pe_rb)])
        gate["ps0"] = ("act", a_rb)
        d_ao = v.do(lambda e, h=h, hp=hp: e.tensor_mul(
            attn_out[:, h, :], ps[5][:], rb_sb[:, hp, :]),
            extra_waits=[("act", a_rb)])
        gate["ps5"] = ("dve", d_ao)
        return pe_att

    for n in range(NB):
        slot, pslot = n % 2, (n - 1) % 2
        tab0 = W + n * W
        xb = x_blk(n)
        g.wait_gate(gate.get("ropetab"))
        dma(c2_sb[:], dr["c2"][:, tab0:tab0 + W], "p0")
        dma(s2_sb[:], dr["s2"][:, tab0:tab0 + W], "p0")
        if n == 1:
            g.wait_gate(gate.get("maskrd"))
            dma(mask_sb[:], dr["maskR"], "p0")
        t.wait_gate(gate.get(("kvslot", slot)))
        wbase = 4 + 36 * n
        kv_block(n, xb, wbase)
        msk = mask_sb

        dv_q = emit_qproj(n, 0, xb, wbase)
        for h in range(H):
            dms = emit_scores(n, h, dv_q, msk)
            if h + 1 < H:
                dv_q = emit_qproj(n, h + 1, xb, wbase)
            pe_att = emit_sumsout(n, h, dms)
        gate[("kvslot", pslot)] = ("pe", pe_att)
        gate["ropetab"] = ("dve", cnt["dve"])
        gate["maskrd"] = ("dve", cnt["dve"])

        dv_attn_done = cnt["dve"]
        first = True
        for dc in range(KD):
            dp = dc % 2
            wi = wbase + 20 + dc
            sp3 = wi % 3
            wwait(wi)
            if dc == 0:
                t.wait("dve", dv_attn_done)
            t.wait_gate(gate.get(f"pso{dp}"))
            for kd in range(KD):
                t.op(lambda e, kd=kd, sp3=sp3, dp=dp: e.matmul(
                    ps[dp][:], wqo_sb[:, sp3, kd, :], attn_out[:, kd, :],
                    start=(kd == 0), stop=(kd == KD - 1), skip_group_check=True),
                    inc="pe" if kd == KD - 1 else None)
            pe_o = cnt["pe"]
            wdone(wi, pe_o)
            a_o = s.do(lambda e, dc=dc, dp=dp: e.copy(
                out=expT[:, dc // 8, dc % 8, :], in_=ps[dp][:]),
                extra_waits=[("pe", pe_o)])
            gate[f"pso{dp}"] = ("act", a_o)
            d_s1 = v.do(lambda e, dc=dc, n=n: e.tensor_add(
                s1_pp[:, dc % 2, :], x_res[:, dc, n * W:(n + 1) * W],
                expT[:, dc // 8, dc % 8, :]), extra_waits=[("act", a_o)])
            a_sq = s.do(lambda e, dc=dc: e.activation(
                out=sq_pp[:, dc % 2, :], in_=s1_pp[:, dc % 2, :], func=AF.Square),
                extra_waits=[("dve", d_s1)])
            t.wait("dve", d_s1)
            if dc == 0:
                t.wait_gate(gate.get("ps_sc"))
            t.op(lambda e, dc=dc, st=first: e.matmul(
                ps[2][0:1, :], ones_cb[:], s1_pp[:, dc % 2, :],
                start=st, stop=(dc == KD - 1), skip_group_check=True), inc="pe")
            t.wait("act", a_sq)
            t.op(lambda e, dc=dc, st=first: e.matmul(
                ps[3][0:1, :], ones_cb[:], sq_pp[:, dc % 2, :],
                start=st, stop=(dc == KD - 1), skip_group_check=True),
                inc="pe" if dc == KD - 1 else None)
            first = False
        pe_stats = cnt["pe"]
        wpump(wbase + 36 + 3)
        v.do(lambda e: e.tensor_scalar_mul(lnrow[:, 0, :], ps[2][0:1, :], 1.0 / DM),
             extra_waits=[("pe", pe_stats)])
        v.do(lambda e: e.tensor_scalar_mul(lnrow[:, 1, :], ps[3][0:1, :], 1.0 / DM))
        v.do(lambda e: e.tensor_mul(lnrow[:, 2, :], lnrow[:, 0, :], lnrow[:, 0, :]))
        v.do(lambda e: e.tensor_sub(lnrow[:, 1, :], lnrow[:, 1, :],
                                    lnrow[:, 2, :]))
        dv_var = v.do(lambda e: e.tensor_scalar_add(lnrow[:, 1, :],
                                                    lnrow[:, 1, :], EPS))
        gate["ps_sc"] = ("dve", dv_var)
        s.do(lambda e: e.activation(out=lnrow[:, 2, :], in_=lnrow[:, 1, :],
                                    func=AF.Ln),
             extra_waits=[("dve", dv_var)])
        a_rs = s.do(lambda e: e.activation(out=lnrow[:, 1, :], in_=lnrow[:, 2, :],
                                           func=AF.Exp, scale=-0.5))
        t.wait("act", a_rs)
        t.wait_gate(gate.get("ps4"))
        t.op(lambda e: e.matmul(ps[4][:], ones_rf[:], lnrow[:, 0, :],
                                start=True, stop=True, skip_group_check=True), inc="pe")
        t.wait_gate(gate.get("ps5"))
        t.op(lambda e: e.matmul(ps[5][:], ones_rf[:], lnrow[:, 1, :],
                                start=True, stop=True, skip_group_check=True), inc="pe")
        pe_bc = cnt["pe"]
        a_mb = s.do(lambda e: e.copy(out=mb_sb[:], in_=ps[4][:]),
                    extra_waits=[("pe", pe_bc)])
        a_rb2 = s.do(lambda e: e.copy(out=rb2_sb[:], in_=ps[5][:]))
        gate["ps4"] = ("act", a_mb)
        gate["ps5"] = ("act", a_rb2)
        for dc in range(KD):
            v.do(lambda e, dc=dc, n=n: e.tensor_add(
                s1_pp[:, dc % 2, :], x_res[:, dc, n * W:(n + 1) * W],
                expT[:, dc // 8, dc % 8, :]),
                extra_waits=[("act", a_rb2)] if dc == 0 else ())
            v.do(lambda e, dc=dc: e.tensor_sub(ln_t[:], s1_pp[:, dc % 2, :], mb_sb[:]))
            v.do(lambda e, dc=dc, n=n: e.tensor_mul(
                x_res[:, dc, n * W:(n + 1) * W], ln_t[:], rb2_sb[:]))

    halo.close()
    att.close()

    R = "right"
    def sbr_(stack, name, shape, dt):
        return stack.enter_context(nc.sbuf_tensor(name, list(shape), dt, side=R))

    ffn = ExitStack()
    racc = sbr_(ffn, "racc", (128, KD, T), BF16)
    sg_row = sbr_(ffn, "sg_row", (1, T), BF16)
    sidx = sbr_(ffn, "sidx", (128, 16, 4), mybir.dt.int32)
    wg_sb = sbr_(ffn, "wg_sb", (128, 2, KD, 128), BF16)
    wu_sb = sbr_(ffn, "wu_sb", (128, 2, KD, 128), BF16)
    wd_sb = sbr_(ffn, "wd_sb", (128, 2, IC, 128), BF16)

    rt = ExitStack()
    identf = sbr_(rt, "identf_t", (128, 128), F32)
    routert_sb = sbr_(rt, "routert_sb", (128, KD, 16), BF16)
    shgw_sb = sbr_(rt, "shgw_sb", (128, KD, 1), BF16)
    iota16_sb = sbr_(rt, "iota16_sb", (128, 16), F32)
    lg_sb = sbr_(rt, "lg_sb", (16, 4, 512), F32)
    ltok = sbr_(rt, "ltok", (128, 16, 16), F32)
    ptmp = sbr_(rt, "ptmp", (128, 16, 16), F32)
    selacc = sbr_(rt, "selacc", (128, 16, 16), F32)
    maskt = sbr_(rt, "maskt", (128, 16, 16), F32)
    comb_tok = sbr_(rt, "comb_tok", (128, 16, 16), F32)
    mx16 = sbr_(rt, "mx16", (128, 16), F32)
    ts16 = sbr_(rt, "ts16", (128, 16), F32)
    comb_d = sbr_(rt, "comb_d", (16, T), F32)
    maskr4 = sbr_(rt, "maskr4", (128, 16, 16, 4), F32)
    eidr4 = sbr_(rt, "eidr4", (128, 16, 4), F32)
    valr4 = sbr_(rt, "valr4", (128, 16, 4), F32)
    sidxf = sbr_(rt, "sidxf", (128, 16, 4), F32)
    vst2 = sbr_(rt, "vst2", (128, 16, 4, 2), F32)
    postok = sbr_(rt, "postok", (128, 16, 16), F32)
    cum16 = sbr_(rt, "cum16", (16, T), F32)
    tmp1616 = sbr_(rt, "tmp1616", (128, 16, 16), F32)
    x1tok = sbr_(rt, "x1tok", (128, 2, DM), BF16)

    P.phase("R")
    t.wait("out0", zx_total)
    s.wait("out0", zx_total)
    v.wait("out0", zx_total)
    dma(identf[:], dr["identf"], "p0")
    dma(routert_sb[:], dr["routert"], "p0")
    dma(shgw_sb[:], dr["shgw"], "p0")
    dma(iota16_sb[:], dr["iota16"], "p0")
    p0r = cnt["p0"]
    dv_ln_done = cnt["dve"]
    for kq in range(4):
        s.do(lambda e, kq=kq: e.copy(out=racc[:, kq * 4:(kq + 1) * 4, :],
                                     in_=x_res[:, kq * 4:(kq + 1) * 4, :]),
             extra_waits=[("dve", dv_ln_done)] if kq == 0 else ())
    t.wait("p0", p0r)
    for tq in range(4):
        t.wait_gate(gate.get("ps0"))
        for kd in range(KD):
            t.op(lambda e, kd=kd, tq=tq: e.matmul(
                ps[0][0:16, :], routert_sb[:, kd, :],
                x_res[:, kd, tq * 512:(tq + 1) * 512],
                start=(kd == 0), stop=(kd == KD - 1), skip_group_check=True),
                inc="pe" if kd == KD - 1 else None)
        pe_lg = cnt["pe"]
        a_lg = s.do(lambda e, tq=tq: e.copy(out=lg_sb[:, tq, :], in_=ps[0][0:16, :]),
                    extra_waits=[("pe", pe_lg)])
        gate["ps0"] = ("act", a_lg)
        t.wait_gate(gate.get("ps_sc"))
        for kd in range(KD):
            t.op(lambda e, kd=kd, tq=tq: e.matmul(
                ps[3][0:1, :], shgw_sb[:, kd, :],
                x_res[:, kd, tq * 512:(tq + 1) * 512],
                start=(kd == 0), stop=(kd == KD - 1), skip_group_check=True),
                inc="pe" if kd == KD - 1 else None)
        pe_sg = cnt["pe"]
        a_sg = s.do(lambda e, tq=tq: e.activation(
            out=sg_row[:, tq * 512:(tq + 1) * 512], in_=ps[3][0:1, :],
            func=AF.Sigmoid), extra_waits=[("pe", pe_sg)])
        gate["ps_sc"] = ("act", a_sg)
    for tc in range(16):
        t.wait_gate(gate.get("ps1"))
        t.op(lambda e, tc=tc: e.transpose(
            ps[1][:, 0:16], lg_sb[:, tc // 4, (tc % 4) * 128:(tc % 4 + 1) * 128],
            identf[0:16, 0:16]), inc="pe")
        pe_tr = cnt["pe"]
        a_ltk = s.do(lambda e, tc=tc: e.copy(out=ltok[:, tc, :], in_=ps[1][:, 0:16]),
                     extra_waits=[("pe", pe_tr)])
        gate["ps1"] = ("act", a_ltk)
    a_lt = cnt["act"]
    bc16 = lambda ap: ap.unsqueeze(2).broadcast_to([128, 16, 16])
    v.do(lambda e: e.tensor_reduce(mx16[:], ltok[:], AX, ALU.max),
         extra_waits=[("act", a_lt)])
    v.do(lambda e: e.tensor_tensor(ptmp[:], ltok[:], bc16(mx16[:]), ALU.subtract))
    a_ex = s.do(lambda e: e.activation(out=ptmp[:], in_=ptmp[:], func=AF.Exp),
                extra_waits=[("dve", cnt["dve"])])
    v.do(lambda e: e.tensor_reduce(mx16[:], ptmp[:], AX, ALU.add),
         extra_waits=[("act", a_ex)])
    v.do(lambda e: e.reciprocal(out=ts16[:], in_=mx16[:]))
    v.do(lambda e: e.tensor_tensor(ptmp[:], ptmp[:], bc16(ts16[:]), ALU.mult))
    v.do(lambda e: e.memset(selacc[:], 0.0))
    v.do(lambda e: e.memset(ts16[:], 0.0))
    for r in range(4):
        v.do(lambda e: e.tensor_reduce(mx16[:], ptmp[:], AX, ALU.max))
        v.do(lambda e: e.tensor_tensor(maskt[:], ptmp[:], bc16(mx16[:]), ALU.is_ge))
        v.do(lambda e, r=r: e.tensor_copy(out=maskr4[:, :, :, r], in_=maskt[:]))
        v.do(lambda e: e.tensor_tensor(
            tmp1616[:], maskt[:], iota16_sb[:].unsqueeze(1).broadcast_to([128, 16, 16]),
            ALU.mult))
        v.do(lambda e, r=r: e.tensor_reduce(eidr4[:, :, r], tmp1616[:], AX, ALU.add))
        v.do(lambda e, r=r: e.tensor_copy(out=valr4[:, :, r], in_=mx16[:]))
        v.do(lambda e: e.tensor_tensor(maskt[:], maskt[:], ptmp[:], ALU.mult))
        v.do(lambda e: e.tensor_add(selacc[:], selacc[:], maskt[:]))
        v.do(lambda e: e.tensor_add(ts16[:], ts16[:], mx16[:]))
        if r < 3:
            v.do(lambda e: e.tensor_sub(ptmp[:], ptmp[:], maskt[:]))
    v.do(lambda e: e.tensor_scalar_max(ts16[:], ts16[:], 1e-9))
    v.do(lambda e: e.reciprocal(out=ts16[:], in_=ts16[:]))
    for r in range(4):
        v.do(lambda e, r=r: e.tensor_tensor(valr4[:, :, r], valr4[:, :, r],
                                            ts16[:], ALU.mult))
    dv_cmb = v.do(lambda e: e.tensor_tensor(comb_tok[:], selacc[:],
                                            bc16(ts16[:]), ALU.mult))
    t.wait("dve", dv_cmb)
    for tc in range(16):
        t.wait_gate(gate.get("ps_sc"))
        t.op(lambda e, tc=tc: e.transpose(
            ps[2][0:16, 0:128], comb_tok[:, tc, :], identf[:]), inc="pe")
        pe_ct = cnt["pe"]
        a_cd = s.do(lambda e, tc=tc: e.copy(out=comb_d[:, tc * 128:(tc + 1) * 128],
                                            in_=ps[2][0:16, 0:128]),
                    extra_waits=[("pe", pe_ct)])
        gate["ps_sc"] = ("act", a_cd)
    a_cdall = cnt["act"]
    lg_flat = lg_sb[:].rearrange("e a b -> e (a b)")
    v.do(lambda e: e.tensor_single_scalar(out=comb_d[:], in_=comb_d[:], scalar=0.0,
                                          op=ALU.is_gt),
         extra_waits=[("act", a_cdall)])
    v.do(lambda e: e.tensor_tensor_scan(out=cum16[:], data0=comb_d[:],
                                        data1=comb_d[:], initial=0.0,
                                        op0=ALU.add, op1=ALU.max))
    v.do(lambda e: e.tensor_scalar_add(cum16[:], cum16[:], -1.0))
    v.do(lambda e: e.tensor_single_scalar(out=lg_flat, in_=cum16[:],
                                          scalar=float(CEXP), op=ALU.is_lt))
    v.do(lambda e: e.tensor_mul(lg_flat, lg_flat, comb_d[:]))
    v.do(lambda e: e.tensor_scalar(out=comb_d[:], in0=lg_flat, scalar1=-1.0,
                                   scalar2=-1.0e9, op0=ALU.add, op1=ALU.mult))
    v.do(lambda e: e.tensor_mul(cum16[:], cum16[:], lg_flat))
    dv_pos = v.do(lambda e: e.tensor_add(cum16[:], cum16[:], comb_d[:]))
    t.wait("dve", dv_pos)
    for tc in range(16):
        t.wait_gate(gate.get("ps1"))
        t.op(lambda e, tc=tc: e.transpose(
            ps[1][:, 0:16], cum16[0:16, tc * 128:(tc + 1) * 128],
            identf[0:16, 0:16]), inc="pe")
        pe_pt = cnt["pe"]
        a_pt = s.do(lambda e, tc=tc: e.copy(out=postok[:, tc, :], in_=ps[1][:, 0:16]),
                    extra_waits=[("pe", pe_pt)])
        gate["ps1"] = ("act", a_pt)
    a_ptall = cnt["act"]
    v.wait("act", a_ptall)
    for r in range(4):
        v.do(lambda e, r=r: e.tensor_tensor(
            tmp1616[:], maskr4[:, :, :, r], postok[:], ALU.mult))
        v.do(lambda e, r=r: e.tensor_reduce(sidxf[:, :, r], tmp1616[:], AX, ALU.add))
        v.do(lambda e, r=r: e.scalar_tensor_tensor(
            out=sidxf[:, :, r], in0=eidr4[:, :, r], scalar=float(CEXP),
            in1=sidxf[:, :, r], op0=ALU.mult, op1=ALU.add))
    dv_sidx = v.do(lambda e: e.tensor_copy(out=sidx[:], in_=sidxf[:]))
    v.do(lambda e: e.tensor_copy(out=vst2[:, :, :, 0], in_=valr4[:]))
    dv_vst = v.do(lambda e: e.tensor_copy(out=vst2[:, :, :, 1], in_=valr4[:]))

    g.wait("dve", dv_vst)
    g.wait("out0", zx_total)
    regbox = {}
    g.op(lambda e: regbox.__setitem__("b", e.to_reg(16 * CEXP - 1)) or e.engine_nop())
    for tc in range(16):
        yp = tc % 2
        for dc in range(KD):
            tp = dc % 2
            t.wait_gate(gate.get(("pst", tp)))
            t.op(lambda e, dc=dc, tc=tc, tp=tp: e.transpose(
                pst[tp][:], x_res[:, dc, tc * 128:(tc + 1) * 128], identb[:]),
                inc="pe")
            pe_t2 = cnt["pe"]
            if dc % 2 == 0:
                a_x = s.do(lambda e, dc=dc, tp=tp, yp=yp: e.copy(
                    out=x1tok[:, yp, dc * 128:(dc + 1) * 128], in_=pst[tp][:]),
                    extra_waits=[("pe", pe_t2)] +
                        ([gate[("x1tok", yp)]] if dc == 0 and ("x1tok", yp) in gate
                         else []))
                gate[("pst", tp)] = ("act", a_x)
            else:
                d_x = v.do(lambda e, dc=dc, tp=tp, yp=yp: e.tensor_copy(
                    out=x1tok[:, yp, dc * 128:(dc + 1) * 128], in_=pst[tp][:]),
                    extra_waits=[("pe", pe_t2)] +
                        ([gate[("x1tok", yp)]] if dc == 1 and ("x1tok", yp) in gate
                         else []))
                gate[("pst", tp)] = ("dve", d_x)
        g.wait("act", cnt["act"])
        g.wait("dve", cnt["dve"])
        ssem = f"sca{yp}"
        for r in range(4):
            g.op(lambda e, tc=tc, r=r, yp=yp: e.indirect_dma_start(
                out=xg_dram,
                out_offset=IndirectOffsetOnAxis(sidx[:, tc, r:r + 1], 0),
                in_=x1tok[:, yp, :], in_offset=None,
                bounds_check=regbox["b"], oob_is_err=False),
                inc=ssem, n=16)
            g.op(lambda e, tc=tc, r=r: e.indirect_dma_start(
                out=cg_dram,
                out_offset=IndirectOffsetOnAxis(sidx[:, tc, r:r + 1], 0),
                in_=vst2[:, tc, r, :], in_offset=None,
                bounds_check=regbox["b"], oob_is_err=False),
                inc=ssem, n=16)
        gate[("x1tok", yp)] = (ssem, cnt[ssem])
    sca_tot0, sca_tot1 = cnt["sca0"], cnt["sca1"]

    rt.close()

    rt2 = ExitStack()
    act_sh = sbr_(rt2, "act_sh", (128, IC, T), BF16)
    sgb = sbr_(rt2, "sgb", (128, T), BF16)
    silsh = sbr_(rt2, "silsh", (128, 2, 512), BF16)

    P.phase("F")
    t.wait("sca0", sca_tot0); t.wait("sca1", sca_tot1)
    s.wait("sca0", sca_tot0); s.wait("sca1", sca_tot1)
    v.wait("sca0", sca_tot0); v.wait("sca1", sca_tot1)
    for e_i in range(16, NE):
        src_row = lambda tq: sg_row[0:1, tq * 512:(tq + 1) * 512]
        t.wait("dve", cnt["dve"])
        if e_i == 16:
            t.wait_gate(gate.get("ps_sc"))
        for tq in range(4):
            cp = tq % 2
            t.wait_gate(gate.get(f"ps{cp}"))
            t.op(lambda e, tq=tq, cp=cp, src=src_row: e.matmul(
                ps[cp][:], ones_rb[:], src(tq), start=True, stop=True, skip_group_check=True), inc="pe")
            pe_cb = cnt["pe"]
            a_cb = s.do(lambda e, tq=tq, cp=cp: e.copy(
                out=sgb[:, tq * 512:(tq + 1) * 512], in_=ps[cp][:]),
                extra_waits=[("pe", pe_cb)])
            gate[f"ps{cp}"] = ("act", a_cb)
        a_cbl = cnt["act"]
        for i in range(IC):
            ip = i % 2
            gsem, usem = f"wg{ip}", f"wu{ip}"
            g.wait_gate(gate.get(("wg", ip)))
            dma(wg_sb[:, ip], dr["wg_all"][e_i, i], gsem)
            g.wait_gate(gate.get(("wu", ip)))
            dma(wu_sb[:, ip], dr["wu_all"][e_i, i], usem)
            g_done, u_done = cnt[gsem], cnt[usem]
            t.wait(gsem, g_done)
            t.wait(usem, u_done)
            for tq in range(4):
                tp = tq % 2
                t.wait_gate(gate.get(f"ps{tp}"))
                if i == 0 and tq < 2:
                    t.wait("act", a_cbl)
                for kd in range(KD):
                    t.op(lambda e, kd=kd, ip=ip, tq=tq, tp=tp: e.matmul(
                        ps[tp][:], wg_sb[:, ip, kd, :],
                        x_res[:, kd, tq * 512:(tq + 1) * 512],
                        start=(kd == 0), stop=(kd == KD - 1), skip_group_check=True),
                        inc="pe" if kd == KD - 1 else None)
                pe_g = cnt["pe"]
                t.wait_gate(gate.get(("psu", tp)))
                for kd in range(KD):
                    t.op(lambda e, kd=kd, ip=ip, tq=tq, tp=tp: e.matmul(
                        ps[2 + tp][:], wu_sb[:, ip, kd, :],
                        x_res[:, kd, tq * 512:(tq + 1) * 512],
                        start=(kd == 0), stop=(kd == KD - 1), skip_group_check=True),
                        inc="pe" if kd == KD - 1 else None)
                pe_u = cnt["pe"]
                a_si = s.do(lambda e, tp=tp: e.activation(
                    out=silsh[:, tp, :], in_=ps[tp][:], func=AF.Silu),
                    extra_waits=[("pe", pe_g), gate.get(("sil", tp))])
                gate[f"ps{tp}"] = ("act", a_si)
                d_u = v.do(lambda e, tp=tp: e.tensor_mul(
                    silsh[:, tp, :], silsh[:, tp, :], ps[2 + tp][:]),
                    extra_waits=[("act", a_si), ("pe", pe_u)])
                gate[("psu", tp)] = ("dve", d_u)
                d_sc = v.do(lambda e, i=i, tq=tq, tp=tp: e.tensor_mul(
                    act_sh[:, i, tq * 512:(tq + 1) * 512], silsh[:, tp, :],
                    sgb[:, tq * 512:(tq + 1) * 512]),
                    extra_waits=[("act", a_cbl)] if i == 0 else ())
                gate[("sil", tp)] = ("dve", d_sc)
            gate[("wg", ip)] = ("pe", pe_g)
            gate[("wu", ip)] = ("pe", pe_u)
        dv_A = cnt["dve"]
        first_b = True
        for dc in range(KD):
            dp = dc % 2
            dsem = f"wd{dp}"
            g.wait_gate(gate.get(("wd", dp)))
            dma(wd_sb[:, dp], dr["wd_all"][e_i, dc], dsem)
            d_done = cnt[dsem]
            t.wait(dsem, d_done)
            if first_b:
                t.wait("dve", dv_A)
                first_b = False
            for tq in range(4):
                yp2 = 4 + ((dc * 4 + tq) % 2)
                t.wait_gate(gate.get(f"ps{yp2}"))
                for i in range(IC):
                    t.op(lambda e, i=i, dp=dp, tq=tq, yp2=yp2: e.matmul(
                        ps[yp2][:], wd_sb[:, dp, i, :],
                        act_sh[:, i, tq * 512:(tq + 1) * 512],
                        start=(i == 0), stop=(i == IC - 1), skip_group_check=True),
                        inc="pe" if i == IC - 1 else None)
                pe_y = cnt["pe"]
                d_y = v.do(lambda e, dc=dc, tq=tq, yp2=yp2: e.tensor_add(
                    racc[:, dc, tq * 512:(tq + 1) * 512],
                    racc[:, dc, tq * 512:(tq + 1) * 512], ps[yp2][:]),
                    extra_waits=[("pe", pe_y)])
                gate[f"ps{yp2}"] = ("dve", d_y)
            gate[("wd", dp)] = ("pe", pe_y)

    rt2.close()
    xsc.close()

    fs = ExitStack()
    act_sp = sbr_(fs, "act_sp", (128, IC, CEXP), BF16)
    xgT = sbr_(fs, "xgT", (128, 2, KD, CEXP), BF16)
    ytk = sbr_(fs, "ytk", (128, CEXP // 128, DM), BF16)
    cgb = sbr_(fs, "cgb", (128, CEXP), BF16)
    cgrow = sbr_(fs, "cgrow", (1, 2, 2 * CEXP), F32)
    silsp = sbr_(fs, "silsp", (128, 2, 512), BF16)
    ystage = sbr_(fs, "ystage", (128, 4, 128), BF16)

    CCH = [(0, 512), (512, 256)] if CEXP == 768 else [(0, 512), (512, CEXP - 512)]
    NCC = CEXP // 128

    P.phase("FS")
    yq.wait("sca0", sca_tot0)
    yq.wait("sca1", sca_tot1)

    def emit_xg_loads(e_i):
        xp = e_i % 2
        xsem = f"xg{xp}"
        yq.wait_gate(gate.get(("xgT", xp)))
        for kd in range(KD):
            yq.op(lambda e, e_i=e_i, kd=kd, xp=xp: e.dma_start_transpose(
                out=xgT[:, xp, kd, :],
                in_=xg_dram[e_i * CEXP:(e_i + 1) * CEXP, kd * 128:(kd + 1) * 128]),
                inc=xsem, n=16)
        yq.op(lambda e, e_i=e_i, xp=xp: e.dma_start(
            out=cgrow[:, xp, :], in_=cg_dram[e_i * CEXP:(e_i + 1) * CEXP, :]
            .rearrange("c o -> (c o)").unsqueeze(0)), inc=xsem, n=16)
        return cnt[xsem]

    xg_ready = {0: emit_xg_loads(0)}
    xg_ready[1] = emit_xg_loads(1)
    for e_i in range(16):
        xp = e_i % 2
        xsem = f"xg{xp}"
        cg_done = xg_ready[e_i]
        t.wait(xsem, cg_done)
        t.wait("dve", cnt["dve"])
        for ci, (c0, cw) in enumerate(CCH):
            cp = ci % 2
            t.wait_gate(gate.get(f"ps{cp}"))
            t.op(lambda e, c0=c0, cw=cw, cp=cp, xp=xp: e.matmul(
                ps[cp][:, 0:cw],
                ones_rf[:],
                cgrow[:, xp, :].rearrange("o (c two) -> o c two", two=2)[:, c0:c0 + cw, 0],
                start=True, stop=True, skip_group_check=True), inc="pe")
            pe_cb = cnt["pe"]
            a_cb = s.do(lambda e, c0=c0, cw=cw, cp=cp: e.copy(
                out=cgb[:, c0:c0 + cw], in_=ps[cp][:, 0:cw]),
                extra_waits=[("pe", pe_cb)])
            gate[f"ps{cp}"] = ("act", a_cb)
        a_cbl = cnt["act"]
        for i in range(IC):
            ip = i % 2
            gsem, usem = f"wg{ip}", f"wu{ip}"
            g.wait_gate(gate.get(("wg", ip)))
            dma(wg_sb[:, ip], dr["wg_all"][e_i, i], gsem)
            g.wait_gate(gate.get(("wu", ip)))
            dma(wu_sb[:, ip], dr["wu_all"][e_i, i], usem)
            g_done, u_done = cnt[gsem], cnt[usem]
            t.wait(gsem, g_done)
            t.wait(usem, u_done)
            if i == 0:
                t.wait(xsem, cg_done)
            if i == 0:
                t.wait(xsem, cg_done)
            for ci, (c0, cw) in enumerate(CCH):
                tp = ci % 2
                t.wait_gate(gate.get(f"ps{tp}"))
                if i == 0:
                    t.wait("act", a_cbl)
                for kd in range(KD):
                    t.op(lambda e, kd=kd, ip=ip, c0=c0, cw=cw, tp=tp, xp=xp: e.matmul(
                        ps[tp][:, 0:cw], wg_sb[:, ip, kd, :],
                        xgT[:, xp, kd, c0:c0 + cw],
                        start=(kd == 0), stop=(kd == KD - 1), skip_group_check=True),
                        inc="pe" if kd == KD - 1 else None)
                pe_g = cnt["pe"]
                t.wait_gate(gate.get(("psu", tp)))
                for kd in range(KD):
                    t.op(lambda e, kd=kd, ip=ip, c0=c0, cw=cw, tp=tp, xp=xp: e.matmul(
                        ps[2 + tp][:, 0:cw], wu_sb[:, ip, kd, :],
                        xgT[:, xp, kd, c0:c0 + cw],
                        start=(kd == 0), stop=(kd == KD - 1), skip_group_check=True),
                        inc="pe" if kd == KD - 1 else None)
                pe_u = cnt["pe"]
                a_si = s.do(lambda e, cw=cw, tp=tp: e.activation(
                    out=silsp[:, tp, 0:cw], in_=ps[tp][:, 0:cw], func=AF.Silu),
                    extra_waits=[("pe", pe_g), gate.get(("sil", tp))])
                gate[f"ps{tp}"] = ("act", a_si)
                d_u = v.do(lambda e, cw=cw, tp=tp: e.tensor_mul(
                    silsp[:, tp, 0:cw], silsp[:, tp, 0:cw], ps[2 + tp][:, 0:cw]),
                    extra_waits=[("act", a_si), ("pe", pe_u)])
                gate[("psu", tp)] = ("dve", d_u)
                d_sc = v.do(lambda e, i=i, c0=c0, cw=cw, tp=tp: e.tensor_mul(
                    act_sp[:, i, c0:c0 + cw], silsp[:, tp, 0:cw],
                    cgb[:, c0:c0 + cw]))
                gate[("sil", tp)] = ("dve", d_sc)
            gate[("wg", ip)] = ("pe", pe_g)
            gate[("wu", ip)] = ("pe", pe_u)
        gate[("xgT", xp)] = ("pe", cnt["pe"])
        if e_i + 2 < 16:
            xg_ready[e_i + 2] = emit_xg_loads(e_i + 2)
        dv_A = cnt["dve"]
        first_b = True
        tfifo = []

        def pop_transpose():
            kdc, kcc, ka_ys, kslot = tfifo.pop(0)
            tp2 = kslot % 2
            t.wait("act", ka_ys)
            t.wait_gate(gate.get(("pst", tp2)))
            t.op(lambda e, kslot=kslot, tp2=tp2: e.transpose(
                pst[tp2][:], ystage[:, kslot, :], identb[:]), inc="pe")
            pe_tt = cnt["pe"]
            gate[("yst", kslot)] = ("pe", pe_tt)
            d_yt = v.do(lambda e, kdc=kdc, kcc=kcc, tp2=tp2: e.tensor_copy(
                out=ytk[:, kcc, kdc * 128:(kdc + 1) * 128], in_=pst[tp2][:]),
                extra_waits=[("pe", pe_tt)] +
                    ([("ya", gate.get("ytk_rd", (None, 0))[1])]
                     if kdc == 0 and kcc < 2 and "ytk_rd" in gate else []))
            gate[("pst", tp2)] = ("dve", d_yt)

        kidx = 0
        for dc in range(KD):
            dp = dc % 2
            dsem = f"wd{dp}"
            g.wait_gate(gate.get(("wd", dp)))
            dma(wd_sb[:, dp], dr["wd_all"][e_i, dc], dsem)
            d_done = cnt[dsem]
            t.wait(dsem, d_done)
            if first_b:
                t.wait("dve", dv_A)
                first_b = False
            for cc in range(NCC):
                yp2 = 4 + (cc % 2)
                kslot = kidx % 4
                kidx += 1
                t.wait_gate(gate.get(f"ps{yp2}"))
                for i in range(IC):
                    t.op(lambda e, i=i, dp=dp, cc=cc, yp2=yp2: e.matmul(
                        ps[yp2][:, 0:128], wd_sb[:, dp, i, :],
                        act_sp[:, i, cc * 128:(cc + 1) * 128],
                        start=(i == 0), stop=(i == IC - 1), skip_group_check=True),
                        inc="pe" if i == IC - 1 else None)
                pe_y = cnt["pe"]
                a_ys = s.do(lambda e, kslot=kslot, yp2=yp2: e.copy(
                    out=ystage[:, kslot, :], in_=ps[yp2][:, 0:128]),
                    extra_waits=[("pe", pe_y), gate.get(("yst", kslot))])
                gate[f"ps{yp2}"] = ("act", a_ys)
                tfifo.append((dc, cc, a_ys, kslot))
                if len(tfifo) >= 3:
                    pop_transpose()
            gate[("wd", dp)] = ("pe", pe_y)
        while tfifo:
            pop_transpose()
        dv_ytk = cnt["dve"]
        yq.wait("dve", dv_ytk)
        for cc in range(NCC):
            yq.op(lambda e, e_i=e_i, cc=cc: e.dma_start(
                out=y_all_dram[e_i * CEXP + cc * 128:e_i * CEXP + (cc + 1) * 128, :],
                in_=ytk[:, cc, :]), inc="ya", n=16)
        gate["ytk_rd"] = ("ya", cnt["ya"])
    ya_total = cnt["ya"]

    fs.close()

    epi = ExitStack()
    ytok = sbr_(epi, "ytok", (128, 2, DM), BF16)
    ysq = sbr_(epi, "ysq", (128, DM), BF16)
    yout = sbr_(epi, "yout", (128, 2, DM), F32)
    cols = sbr_(epi, "cols", (128, 8), F32)
    yr = sbr_(epi, "yr", (128, 2, 4, DM), BF16)

    P.phase("EPI")
    t.wait("ya", ya_total)
    s.wait("ya", ya_total)
    v.wait("ya", ya_total)
    regbox2 = {}
    g.op(lambda e: regbox2.__setitem__("b", e.to_reg(16 * CEXP - 1)) or e.engine_nop())
    g.wait("ya", ya_total)
    dv_racc_done = cnt["dve"]
    t.wait("dve", dv_racc_done)
    last_dv_y = 0
    last_comb = {}

    def emit_transposes(tc):
        yp = tc % 2
        g.wait_gate(gate.get(("yr", yp)))
        for r in range(4):
            g.op(lambda e, tc=tc, r=r, yp=yp: e.indirect_dma_start(
                out=yr[:, yp, r, :], out_offset=None,
                in_=y_all_dram,
                in_offset=IndirectOffsetOnAxis(sidx[:, tc, r:r + 1], 0),
                bounds_check=regbox2["b"], oob_is_err=False),
                inc=f"yg{yp}", n=16)
        yg_done = (f"yg{yp}", cnt[f"yg{yp}"])
        for dc in range(KD):
            tp = dc % 2
            t.wait_gate(gate.get(("pst", tp)))
            t.op(lambda e, dc=dc, tc=tc, tp=tp: e.transpose(
                pst[tp][:], racc[:, dc, tc * 128:(tc + 1) * 128], identb[:]),
                inc="pe")
            pe_t = cnt["pe"]
            if dc % 2 == 0:
                a_yt = s.do(lambda e, dc=dc, tp=tp, yp=yp: e.copy(
                    out=ytok[:, yp, dc * 128:(dc + 1) * 128], in_=pst[tp][:]),
                    extra_waits=[("pe", pe_t)] +
                        ([("dve", last_dv_y)] if dc == 0 and tc > 1 else []))
                gate[("pst", tp)] = ("act", a_yt)
            else:
                d_yt = v.do(lambda e, dc=dc, tp=tp, yp=yp: e.tensor_copy(
                    out=ytok[:, yp, dc * 128:(dc + 1) * 128], in_=pst[tp][:]),
                    extra_waits=[("pe", pe_t)])
                gate[("pst", tp)] = ("dve", d_yt)
        return cnt["act"], cnt["dve"], yg_done

    def emit_stats(tc, a_ytl, d_ytl, yg_done):
        yp = tc % 2
        op = tc % 2
        osem = f"out{op}"
        v.do(lambda e, yp=yp: e.tensor_add(ysq[:], yr[:, yp, 0, :], yr[:, yp, 1, :]),
             extra_waits=[yg_done, ("act", a_ytl)])
        v.do(lambda e, yp=yp: e.tensor_add(ytok[:, yp, :], ytok[:, yp, :], ysq[:]))
        v.do(lambda e, yp=yp: e.tensor_add(ysq[:], yr[:, yp, 2, :], yr[:, yp, 3, :]))
        dv_comb = v.do(lambda e, yp=yp: e.tensor_add(ytok[:, yp, :], ytok[:, yp, :],
                                                     ysq[:]))
        gate[("yr", yp)] = ("dve", dv_comb)
        v.do(lambda e, yp=yp: e.tensor_reduce(cols[:, 0:1], ytok[:, yp, :], AX,
                                              ALU.add))
        v.do(lambda e: e.tensor_scalar_mul(cols[:, 0:1], cols[:, 0:1], 1.0 / DM))
        a_q = s.do(lambda e, yp=yp: e.activation(out=ysq[:], in_=ytok[:, yp, :],
                                                 func=AF.Square),
                   extra_waits=[("dve", cnt["dve"])])
        v.do(lambda e: e.tensor_reduce(cols[:, 1:2], ysq[:], AX, ALU.add),
             extra_waits=[("act", a_q)])
        v.do(lambda e: e.tensor_scalar_mul(cols[:, 1:2], cols[:, 1:2], 1.0 / DM))
        v.do(lambda e: e.tensor_mul(cols[:, 2:3], cols[:, 0:1], cols[:, 0:1]))
        v.do(lambda e: e.tensor_sub(cols[:, 3:4], cols[:, 1:2], cols[:, 2:3]))
        dv_v = v.do(lambda e: e.tensor_scalar_add(cols[:, 3:4], cols[:, 3:4], EPS))
        s.do(lambda e: e.activation(out=cols[:, 4:5], in_=cols[:, 3:4],
                                    func=AF.Ln),
             extra_waits=[("dve", dv_v)])
        a_r = s.do(lambda e: e.activation(out=cols[:, 5:6], in_=cols[:, 4:5],
                                          func=AF.Exp, scale=-0.5))
        dv_y = v.do(lambda e, op=op, yp=yp: e.tensor_scalar(
            out=yout[:, op, :], in0=ytok[:, yp, :], scalar1=cols[:, 0:1],
            scalar2=cols[:, 5:6], op0=ALU.subtract, op1=ALU.mult),
            extra_waits=[("act", a_r), (osem, cnt[osem])])
        yq.wait("dve", dv_y)
        yq.op(lambda e, tc=tc, op=op: e.dma_start(
            out=y[tc * 128:(tc + 1) * 128, :], in_=yout[:, op, :]),
            inc=osem, n=16)
        return dv_y

    pend = None
    for tc in range(KD + 1):
        if tc < KD:
            marks = emit_transposes(tc)
        if pend is not None:
            last_dv_y = emit_stats(pend[0], *pend[1])
        pend = (tc, marks) if tc < KD else None
    yq.wait("out0", cnt["out0"])
    yq.wait("out1", cnt["out1"])

    epi.close()
    ffn.close()
    P.replay()
    es.close()
    return nc


_NC_CACHE = None


def kernel(**inputs):
    global _NC_CACHE
    hp = host_prep(inputs)
    if _NC_CACHE is None:
        _NC_CACHE = build_nc()
    nc = _NC_CACHE
    shared = {k: hp[k] for k in ["wqt", "wkt", "wvt", "wot", "routert", "shgw",
                                 "wg_all", "wu_all", "wd_all", "maskR",
                                 "pswap", "identb", "identf", "iota16"]}
    in_maps = []
    for c in range(NCORES):
        m = dict(shared)
        m["xm"] = hp["xm"][c]
        m["xh"] = hp["xh"][c]
        m["c2"] = hp["c2"][c]
        m["s2"] = hp["s2"][c]
        m["maskA0"] = hp["maskA0"] if c == 0 else hp["maskR"]
        in_maps.append(m)
    res = run_bass_kernel_spmd(nc, in_maps, core_ids=list(range(NCORES)))
    out = np.concatenate([res.results[c]["y"] for c in range(NCORES)], axis=0)
    return out.reshape(1, NCORES * T, DM)


# version 09
# speedup vs baseline: 1.5901x; 1.0412x over previous
import math
import numpy as np
import ml_dtypes
from contextlib import ExitStack

import concourse.bass as bass
import concourse.mybir as mybir
from concourse.bass_utils import run_bass_kernel_spmd
from concourse.bass import IndirectOffsetOnAxis

F32 = mybir.dt.float32
BF16 = mybir.dt.bfloat16
AF = mybir.ActivationFunctionType
ALU = mybir.AluOpType
AX = mybir.AxisListType.X

NCORES = 8
T = 2048
W = 512
NB = 4
D = 128
H = 16
HK = 4
DM = 2048
KD = 16
NE = 20
IC = 11
CEXP = 640
CAP = 20.0
EPS = 1e-5
INV_SQRT_D = 1.0 / math.sqrt(D)

BF = ml_dtypes.bfloat16



class Prog:

    def __init__(self, nc, es):
        self.nc = nc
        self.cnt = {}
        self.sems = {}
        self.phases = []
        self.cur = None

    def mksem(self, es, name):
        self.sems[name] = es.enter_context(self.nc.semaphore(name))
        self.cnt[name] = 0

    def phase(self, name):
        self.cur = {"name": name, "t": [], "v": [], "s": [], "g": [], "y": []}
        self.phases.append(self.cur)

    def replay(self):
        for ph in self.phases:
            with self.nc.Block() as blk:
                if ph["g"]:
                    @blk.gpsimd
                    def _(g, ops=ph["g"]):
                        for f in ops:
                            f(g)
                if ph["t"]:
                    @blk.tensor
                    def _(t, ops=ph["t"]):
                        for f in ops:
                            f(t)
                if ph["s"]:
                    @blk.scalar
                    def _(s, ops=ph["s"]):
                        for f in ops:
                            f(s)
                if ph["v"]:
                    @blk.vector
                    def _(v, ops=ph["v"]):
                        for f in ops:
                            f(v)
                if ph["y"]:
                    @blk.sync
                    def _(y, ops=ph["y"]):
                        for f in ops:
                            f(y)


class ES_:
    def __init__(self, prog, key, own=None):
        self.p = prog
        self.key = key
        self.own = own

    def raw(self, fn):
        self.p.cur[self.key].append(fn)

    def wait(self, sem, val):
        if val is None or val <= 0:
            return
        self.raw(lambda eng, s=self.p.sems[sem], v=val: eng.wait_ge(s, v))

    def wait_gate(self, gate):
        if gate:
            self.wait(gate[0], gate[1])

    def op(self, fn, inc=None, n=1):
        if inc is None:
            self.raw(lambda eng, f=fn: f(eng))
            return None
        self.p.cnt[inc] += n
        tgt = self.p.cnt[inc]
        self.raw(lambda eng, f=fn, s=self.p.sems[inc], m=n: f(eng).then_inc(s, m))
        return tgt

    def do(self, fn, extra_waits=()):
        for sv in extra_waits:
            if sv:
                self.wait(sv[0], sv[1])
        self.wait(self.own, self.p.cnt[self.own])
        return self.op(fn, inc=self.own)



def _tile_lhsT(wT, oc_chunks):
    din, dout = wT.shape
    assert din == DM and dout == oc_chunks * 128
    return np.ascontiguousarray(
        wT.reshape(KD, 128, oc_chunks, 128).transpose(2, 1, 0, 3))


def host_prep(inputs):
    d = {}
    x = np.asarray(inputs["L0_attn_x"])[0]
    xT = np.ascontiguousarray(x.T)
    xT4 = xT.reshape(KD, 128, NCORES * T)
    xm, xh = [], []
    for c in range(NCORES):
        xm.append(np.ascontiguousarray(
            xT4[:, :, c * T:(c + 1) * T].transpose(1, 0, 2)))
        h0 = c * T - W if c > 0 else 0
        xh.append(np.ascontiguousarray(
            xT4[:, :, h0:h0 + W].transpose(1, 0, 2)))
    d["xm"], d["xh"] = xm, xh

    d["wqt"] = _tile_lhsT(np.ascontiguousarray(np.asarray(inputs["L0_attn_wq"]).T), 16)
    d["wkt"] = _tile_lhsT(np.ascontiguousarray(np.asarray(inputs["L0_attn_wk"]).T), 4)
    d["wot"] = _tile_lhsT(np.ascontiguousarray(np.asarray(inputs["L0_attn_wo"]).T), 16)
    d["wvt"] = np.ascontiguousarray(
        np.ascontiguousarray(np.asarray(inputs["L0_attn_wv"]).T)
        .reshape(KD, 128, 512).transpose(1, 0, 2))
    d["routert"] = np.ascontiguousarray(
        np.ascontiguousarray(np.asarray(inputs["L0_ffn_router_w"]).T)
        .reshape(KD, 128, 16).transpose(1, 0, 2))
    d["shgw"] = np.ascontiguousarray(
        np.asarray(inputs["L0_ffn_sh_gate_w"]).reshape(KD, 128, 1).transpose(1, 0, 2))

    wg = np.asarray(inputs["L0_ffn_w_gate"])
    wu = np.asarray(inputs["L0_ffn_w_up"])
    wd = np.asarray(inputs["L0_ffn_w_down"])
    shg = np.asarray(inputs["L0_ffn_sh_gate"])
    shu = np.asarray(inputs["L0_ffn_sh_up"])
    shd = np.asarray(inputs["L0_ffn_sh_down"])

    def tile_gu(m):
        return m.reshape(KD, 128, IC, 128).transpose(2, 1, 0, 3)

    def tile_dn(m):
        return m.reshape(IC, 128, KD, 128).transpose(2, 1, 0, 3)

    wg_all = np.empty((NE, IC, 128, KD, 128), np.float32)
    wu_all = np.empty((NE, IC, 128, KD, 128), np.float32)
    wd_all = np.empty((NE, KD, 128, IC, 128), np.float32)
    for e in range(16):
        wg_all[e] = tile_gu(wg[e])
        wu_all[e] = tile_gu(wu[e])
        wd_all[e] = tile_dn(wd[e])
    for sx in range(4):
        wg_all[16 + sx] = tile_gu(shg[:, sx * 1408:(sx + 1) * 1408])
        wu_all[16 + sx] = tile_gu(shu[:, sx * 1408:(sx + 1) * 1408])
        wd_all[16 + sx] = tile_dn(shd[sx * 1408:(sx + 1) * 1408])
    d["wg_all"], d["wu_all"], d["wd_all"] = wg_all, wu_all, wd_all

    inv = 10000.0 ** (-np.arange(0, D, 2, dtype=np.float64) / D)
    c2s, s2s = [], []
    for c in range(NCORES):
        pos = np.arange(c * T - W, c * T + T, dtype=np.float64)
        ang = np.outer(pos, inv)
        cosv = np.cos(ang).T.astype(np.float32)
        sinv = np.sin(ang).T.astype(np.float32)
        c2 = np.empty((128, T + W), np.float32)
        s2 = np.empty((128, T + W), np.float32)
        c2[0::2] = cosv
        c2[1::2] = cosv
        s2[0::2] = -sinv
        s2[1::2] = sinv
        c2s.append(c2.astype(BF))
        s2s.append(s2.astype(BF))
    d["c2"], d["s2"] = c2s, s2s

    j = np.arange(1024)[:, None]
    a = np.arange(W)[None, :]
    mR = ((j < W) & (j >= a)) | ((j >= W) & (j - W <= a))
    mA0 = (j >= W) & (j - W <= a)
    d["maskR"] = np.ascontiguousarray(
        mR.astype(BF).reshape(8, 128, W).transpose(1, 0, 2))
    d["maskA0"] = np.ascontiguousarray(
        mA0.astype(BF).reshape(8, 128, W).transpose(1, 0, 2))

    d["iota16"] = np.tile(np.arange(16, dtype=np.float32), (128, 1))

    psw = np.zeros((128, 128), np.float32)
    i2 = np.arange(0, 128, 2)
    psw[i2, i2 + 1] = 1.0
    psw[i2 + 1, i2] = 1.0
    d["pswap"] = psw.astype(BF)
    d["identb"] = np.eye(128).astype(BF)
    d["identf"] = np.eye(128, dtype=np.float32)
    return d



def build_nc():
    nc = bass.Bass("TRN2", target_bir_lowering=False)

    dr = {}
    def din(name, shape, dt=F32):
        dr[name] = nc.dram_tensor(name, list(shape), dt, kind="ExternalInput").ap()
    din("xm", (128, KD, T)); din("xh", (128, KD, W))
    din("wqt", (16, 128, KD, 128)); din("wkt", (4, 128, KD, 128))
    din("wvt", (128, KD, 512)); din("wot", (16, 128, KD, 128))
    din("routert", (128, KD, 16)); din("shgw", (128, KD, 1))
    din("wg_all", (NE, IC, 128, KD, 128)); din("wu_all", (NE, IC, 128, KD, 128))
    din("wd_all", (NE, KD, 128, IC, 128))
    din("c2", (128, T + W), BF16); din("s2", (128, T + W), BF16)
    din("maskR", (128, 8, W), BF16); din("maskA0", (128, 8, W), BF16)
    din("pswap", (128, 128), BF16); din("identb", (128, 128), BF16)
    din("identf", (128, 128))
    y = nc.dram_tensor("y", [T, DM], F32, kind="ExternalOutput").ap()
    din("iota16", (128, 16))
    xg_dram = nc.dram_tensor("xg_scr", [16 * CEXP, DM], BF16, kind="Internal").ap()
    cg_dram = nc.dram_tensor("cg_scr", [16 * CEXP, 2], F32, kind="Internal").ap()
    y_all_dram = nc.dram_tensor("yall_scr", [16 * CEXP, DM], BF16, kind="Internal").ap()

    es = ExitStack()
    P = Prog(nc, es)
    for sname in ["p0", "wq0", "wq1", "wq2", "wg0", "wg1", "wu0", "wu1", "wd0", "wd1",
                  "cr0", "cr1", "cmb", "pe", "act", "dve", "out0", "out1",
                  "xg0", "xg1", "ya", "sca0", "sca1", "yg0", "yg1"]:
        P.mksem(es, sname)
    t = ES_(P, "t")
    v = ES_(P, "v", own="dve")
    s = ES_(P, "s", own="act")
    g = ES_(P, "g")
    yq = ES_(P, "y")
    cnt = P.cnt
    gate = {}

    def sb(stack, name, shape, dt):
        return stack.enter_context(nc.sbuf_tensor(name, list(shape), dt))

    ones_cb = sb(es, "ones_cb", (128, 1), BF16)
    ones_rb = sb(es, "ones_rb", (1, 128), BF16)
    ones_rf = sb(es, "ones_rf", (1, 128), F32)
    identb = sb(es, "identb_t", (128, 128), BF16)
    xsc = ExitStack()
    x_res = sb(xsc, "x_res", (128, KD, T), BF16)

    ps = [es.enter_context(nc.psum_tensor(f"ps{i}", [128, 512], F32))
          for i in range(2)]
    ps_sc = es.enter_context(nc.psum_tensor("ps_sc", [128, 1024], F32))
    ps.append(ps_sc.ap()[:, 0:512])
    ps.append(ps_sc.ap()[:, 512:1024])
    ps += [es.enter_context(nc.psum_tensor(f"ps{i}", [128, 512], F32))
           for i in (4, 5)]
    pst = [es.enter_context(nc.psum_tensor(f"pst{i}", [128, 128], BF16))
           for i in range(2)]

    def dma(ap_out, ap_in, sem):
        return g.op(lambda e, o=ap_out, i=ap_in: e.dma_start(out=o, in_=i),
                    inc=sem, n=16)

    att = ExitStack()
    pswap = sb(att, "pswap_t", (128, 128), BF16)
    wvt_sb = sb(att, "wvt_sb", (128, KD, 512), BF16)
    kT = sb(att, "kT", (128, 2, HK, W), BF16)
    v_tok = sb(att, "v_tok", (128, 2, 4, W), BF16)
    c2_sb = sb(att, "c2_sb", (128, W), BF16)
    s2_sb = sb(att, "s2_sb", (128, W), BF16)
    mask_sb = sb(att, "mask_sb", (128, 8, W), BF16)
    wqo_sb = sb(att, "wqo_sb", (128, 3, KD, 128), BF16)
    qT = sb(att, "qT", (128, 2, W), BF16)
    q_raw = sb(att, "q_raw", (128, 2, W), BF16)
    rtmp1 = sb(att, "rtmp1", (128, W), F32)
    rtmp2 = sb(att, "rtmp2", (128, W), F32)
    expT = sb(att, "expT", (128, 2, 8, W), BF16)
    etmp = sb(att, "etmp", (128, 2, 2 * W), BF16)
    ttmp = sb(att, "ttmp", (128, 2 * W), F32)
    rb_sb = sb(att, "rb_sb", (128, 2, W), F32)
    recip_row = sb(att, "recip_row", (1, 2, W), F32)
    attn_out = sb(att, "attn_out", (128, KD, W), BF16)
    s1_pp = sb(att, "s1_pp", (128, 4, W), BF16)
    sq_pp = sb(att, "sq_pp", (128, 2, W), BF16)
    lnrow = sb(att, "lnrow", (1, 3, W), F32)
    mb_sb = sb(att, "mb_sb", (128, W), F32)
    rb2_sb = sb(att, "rb2_sb", (128, W), F32)

    halo = ExitStack()
    x_halo = sb(halo, "x_halo", (128, KD, W), BF16)

    def x_blk(n):
        return [x_res[:, kd, n * W:(n + 1) * W] for kd in range(KD)]

    wplan = {"aps": [], "next": 0, "cnt": {}}

    def wreset(aps):
        wplan["aps"] = aps
        wplan["next"] = 0
        wplan["cnt"] = {}

    def wpump(upto):
        while wplan["next"] < min(upto, len(wplan["aps"])):
            i = wplan["next"]
            slot = i % 3
            g.wait_gate(gate.get(("wqo", slot)))
            dma(wqo_sb[:, slot], wplan["aps"][i], f"wq{slot}")
            wplan["cnt"][i] = cnt[f"wq{slot}"]
            wplan["next"] += 1

    def wwait(i):
        t.wait(f"wq{i % 3}", wplan["cnt"][i])

    def wdone(i, pe_val):
        gate[("wqo", i % 3)] = ("pe", pe_val)
        wpump(i + 4)

    def kv_block(n, x_src, widx0):
        slot = n % 2
        for kvh in range(HK):
            wi = widx0 + kvh
            sp3 = wi % 3
            wwait(wi)
            t.wait_gate(gate.get("ps0"))
            t.wait_gate(gate.get(("qraw_rd", kvh % 2)))
            for kd in range(KD):
                t.op(lambda e, kd=kd, sp3=sp3, xs=x_src: e.matmul(
                    ps[0][:], wqo_sb[:, sp3, kd, :], xs[kd],
                    start=(kd == 0), stop=(kd == KD - 1), skip_group_check=True),
                    inc="pe" if kd == KD - 1 else None)
            pe_k = cnt["pe"]
            wdone(wi, pe_k)
            pp = kvh % 2
            a_ev = s.do(lambda e, pp=pp: e.copy(out=q_raw[:, pp, :], in_=ps[0][:]),
                        extra_waits=[("pe", pe_k), ("dve", cnt["dve"])])
            gate["ps0"] = ("act", a_ev)
            t.wait("act", a_ev)
            t.wait_gate(gate.get("ps1"))
            t.op(lambda e, pp=pp: e.matmul(ps[1][:], pswap[:], q_raw[:, pp, :],
                                           start=True, stop=True, skip_group_check=True), inc="pe")
            pe_sh = cnt["pe"]
            v.do(lambda e, pp=pp: e.tensor_mul(
                rtmp1[:], q_raw[:, pp, :], c2_sb[:]),
                extra_waits=[("act", a_ev), ("p0", cnt["p0"])])
            d2 = v.do(lambda e: e.tensor_mul(
                rtmp2[:], ps[1][:], s2_sb[:]),
                extra_waits=[("pe", pe_sh)])
            gate["ps1"] = ("dve", d2)
            dk = v.do(lambda e, slot=slot, kvh=kvh: e.tensor_add(
                kT[:, slot, kvh, :], rtmp1[:], rtmp2[:]))
            gate[("qraw_rd", pp)] = ("dve", dk)
        for jc in range(4):
            pp = jc % 2
            if pp == 0:
                t.wait_gate(gate.get("ps_sc"))
            for kd in range(KD):
                t.op(lambda e, kd=kd, jc=jc, pp=pp, xs=x_src: e.matmul(
                    ps[2 + pp][:], xs[kd][:, jc * 128:(jc + 1) * 128],
                    wvt_sb[:, kd, :],
                    start=(kd == 0), stop=(kd == KD - 1), skip_group_check=True),
                    inc="pe" if kd == KD - 1 else None)
            pe_v = cnt["pe"]
            a_v = s.do(lambda e, slot=slot, jc=jc, pp=pp: e.copy(
                out=v_tok[:, slot, jc, :], in_=ps[2 + pp][:]),
                extra_waits=[("pe", pe_v)])
            gate["ps_sc"] = ("act", a_v)

    P.phase("P0")
    for kq in range(4):
        dma(x_res[:, kq * 4:(kq + 1) * 4, :], dr["xm"][:, kq * 4:(kq + 1) * 4, :], "p0")
    dma(x_halo[:], dr["xh"], "p0")
    dma(wvt_sb[:], dr["wvt"], "p0")
    dma(c2_sb[:], dr["c2"][:, 0:W], "p0")
    dma(s2_sb[:], dr["s2"][:, 0:W], "p0")
    dma(mask_sb[:], dr["maskA0"], "p0")
    dma(pswap[:], dr["pswap"], "p0")
    dma(identb[:], dr["identb"], "p0")
    p0_total = cnt["p0"]
    v.do(lambda e: e.memset(ones_cb[:], 1.0))
    v.do(lambda e: e.memset(ones_rf[:], 1.0))
    v.do(lambda e: e.memset(ones_rb[:], 1.0))
    t.wait("p0", p0_total)
    t.wait("dve", cnt["dve"])
    full_plan = [dr["wkt"][kvh] for kvh in range(HK)]
    for _n in range(NB):
        full_plan += [dr["wkt"][kvh] for kvh in range(HK)]
        full_plan += [dr["wqt"][h] for h in range(H)]
        full_plan += [dr["wot"][dc] for dc in range(KD)]
    wreset(full_plan)
    wpump(3)
    kv_block(-1, [x_halo[:, kd, :] for kd in range(KD)], 0)

    P.phase("ATT")
    zrow = x_halo[:, 0:4, :]
    dv_z = v.do(lambda e: e.memset(zrow, 0.0))
    g.wait("dve", dv_z)
    g.op(lambda e: e.dma_start(
        out=cg_dram[:].rearrange("a b -> (a b)").rearrange("(p n) -> p n", p=128),
        in_=zrow[:, 0, 0:16 * CEXP * 2 // 128]), inc="p0", n=16)
    yq.wait("dve", dv_z)
    for zi in range(16 * CEXP // 128):
        yq.op(lambda e, zi=zi: e.dma_start(
            out=xg_dram[zi * 128:(zi + 1) * 128, :],
            in_=zrow.rearrange("p a b -> p (a b)")), inc="out0", n=16)
    zx_total = cnt["out0"]

    def emit_qproj(n, h, xb, wbase0):
        hp = h % 2
        wi = wbase0 + 4 + h
        sp3 = wi % 3
        wwait(wi)
        t.wait_gate(gate.get("ps0"))
        for kd in range(KD):
            t.op(lambda e, kd=kd, sp3=sp3, xb=xb: e.matmul(
                ps[0][:], wqo_sb[:, sp3, kd, :], xb[kd],
                start=(kd == 0), stop=(kd == KD - 1), skip_group_check=True),
                inc="pe" if kd == KD - 1 else None)
        pe_q = cnt["pe"]
        wdone(wi, pe_q)
        a_ev = s.do(lambda e, hp=hp: e.copy(out=q_raw[:, hp, :], in_=ps[0][:]),
                    extra_waits=[("pe", pe_q), ("dve", cnt["dve"])])
        gate["ps0"] = ("act", a_ev)
        t.wait("act", a_ev)
        t.wait_gate(gate.get("ps1"))
        t.op(lambda e, hp=hp: e.matmul(ps[1][:], pswap[:], q_raw[:, hp, :],
                                       start=True, stop=True, skip_group_check=True), inc="pe")
        pe_sh = cnt["pe"]
        v.do(lambda e, hp=hp: e.tensor_mul(
            rtmp1[:], q_raw[:, hp, :], c2_sb[:]),
            extra_waits=[("act", a_ev), ("p0", cnt["p0"])])
        d2 = v.do(lambda e: e.tensor_mul(
            rtmp2[:], ps[1][:], s2_sb[:]),
            extra_waits=[("pe", pe_sh)])
        gate["ps1"] = ("dve", d2)
        dv_q = v.do(lambda e, hp=hp: e.tensor_add(
            qT[:, hp, :], rtmp1[:], rtmp2[:]),
            extra_waits=[gate[("qT", hp)]] if ("qT", hp) in gate else ())
        gate[("qraw_rd", hp)] = ("dve", dv_q)
        return dv_q

    def emit_scores(n, h, dv_q, msk):
        kvh = h // 4
        hp = h % 2
        slot, pslot = n % 2, (n - 1) % 2
        t.wait("dve", dv_q)
        dms = []
        for jp in range(4):
            jc0 = 2 * jp
            pv = jp % 2
            t.wait_gate(gate.get("ps_sc"))
            for half in range(2):
                jc = jc0 + half
                sl = pslot if jc < 4 else slot
                jj = jc % 4
                t.op(lambda e, sl=sl, kvh=kvh, jj=jj, hp=hp, half=half: e.matmul(
                    ps[2 + half][:], kT[:, sl, kvh, jj * 128:(jj + 1) * 128],
                    qT[:, hp, :], start=True, stop=True, skip_group_check=True),
                    inc="pe" if half == 1 else None)
            pe_sc = cnt["pe"]
            a_t = s.do(lambda e: e.activation(
                out=ttmp[:], in_=ps_sc[:], func=AF.Tanh,
                scale=INV_SQRT_D / CAP), extra_waits=[("pe", pe_sc)])
            gate["ps_sc"] = ("act", a_t)
            a_e = s.do(lambda e, pv=pv: e.activation(
                out=etmp[:, pv, :], in_=ttmp[:], func=AF.Exp,
                scale=CAP), extra_waits=[gate.get(("etmp", pv))])
            d_m = v.do(lambda e, jc0=jc0, hp=hp, pv=pv, msk=msk: e.tensor_mul(
                expT[:, hp, jc0:jc0 + 2, :], etmp[:, pv, :].rearrange("p (a b) -> p a b", a=2),
                msk[:, jc0:jc0 + 2, :]),
                extra_waits=[("act", a_e), ("p0", cnt["p0"])] +
                    ([gate[("expT", hp)]] if jp == 0 and ("expT", hp) in gate
                     else []))
            gate[("etmp", pv)] = ("dve", d_m)
            dms.append((jc0, d_m))
            dms.append((jc0 + 1, d_m))
        return dms

    def emit_sumsout(n, h, dms):
        kvh = h // 4
        hp = h % 2
        slot, pslot = n % 2, (n - 1) % 2
        first = True
        for jc, d_m in dms:
            sl = pslot if jc < 4 else slot
            jj = jc % 4
            t.wait("dve", d_m)
            t.op(lambda e, jc=jc, hp=hp, st=first: e.matmul(
                ps[4][0:1, :], ones_cb[:], expT[:, hp, jc, :],
                start=st, stop=(jc == 7), skip_group_check=True), inc="pe")
            t.op(lambda e, sl=sl, jj=jj, kvh=kvh, jc=jc, hp=hp, st=first: e.matmul(
                ps[5][:], v_tok[:, sl, jj, kvh * 128:(kvh + 1) * 128],
                expT[:, hp, jc, :], start=st, stop=(jc == 7), skip_group_check=True),
                inc="pe" if jc == 7 else None)
            first = False
        pe_att = cnt["pe"]
        gate[("expT", hp)] = ("pe", pe_att)
        gate[("qT", hp)] = ("pe", pe_att)
        d_r = v.do(lambda e, hp=hp: e.reciprocal(
            out=recip_row[:, hp, :], in_=ps[4][0:1, :]),
            extra_waits=[("pe", pe_att)])
        t.wait("dve", d_r)
        t.wait_gate(gate.get("ps0"))
        t.op(lambda e, hp=hp: e.matmul(ps[0][:], ones_rf[:],
                                       recip_row[:, hp, :],
                                       start=True, stop=True, skip_group_check=True), inc="pe")
        pe_rb = cnt["pe"]
        a_rb = s.do(lambda e, hp=hp: e.copy(out=rb_sb[:, hp, :], in_=ps[0][:]),
                    extra_waits=[("pe", pe_rb)])
        gate["ps0"] = ("act", a_rb)
        d_ao = v.do(lambda e, h=h, hp=hp: e.tensor_mul(
            attn_out[:, h, :], ps[5][:], rb_sb[:, hp, :]),
            extra_waits=[("act", a_rb)])
        gate["ps5"] = ("dve", d_ao)
        return pe_att

    for n in range(NB):
        slot, pslot = n % 2, (n - 1) % 2
        tab0 = W + n * W
        xb = x_blk(n)
        g.wait_gate(gate.get("ropetab"))
        dma(c2_sb[:], dr["c2"][:, tab0:tab0 + W], "p0")
        dma(s2_sb[:], dr["s2"][:, tab0:tab0 + W], "p0")
        if n == 1:
            g.wait_gate(gate.get("maskrd"))
            dma(mask_sb[:], dr["maskR"], "p0")
        t.wait_gate(gate.get(("kvslot", slot)))
        wbase = 4 + 36 * n
        kv_block(n, xb, wbase)
        msk = mask_sb

        dv_q = emit_qproj(n, 0, xb, wbase)
        for h in range(H):
            dms = emit_scores(n, h, dv_q, msk)
            if h + 1 < H:
                dv_q = emit_qproj(n, h + 1, xb, wbase)
            pe_att = emit_sumsout(n, h, dms)
        gate[("kvslot", pslot)] = ("pe", pe_att)
        gate["ropetab"] = ("dve", cnt["dve"])
        gate["maskrd"] = ("dve", cnt["dve"])

        dv_attn_done = cnt["dve"]
        lnfifo = []
        first = [True]

        def pop_stats(last=False):
            fdc, fd_s1 = lnfifo.pop(0)
            a_sq = s.do(lambda e, fdc=fdc: e.activation(
                out=sq_pp[:, fdc % 2, :], in_=s1_pp[:, fdc % 4, :], func=AF.Square),
                extra_waits=[("dve", fd_s1), gate.get(("sq", fdc % 2))])
            t.wait("dve", fd_s1)
            if fdc == 0:
                t.wait_gate(gate.get("ps_sc"))
            t.op(lambda e, fdc=fdc, st=first[0]: e.matmul(
                ps[2][0:1, :], ones_cb[:], s1_pp[:, fdc % 4, :],
                start=st, stop=(fdc == KD - 1), skip_group_check=True), inc="pe")
            t.wait("act", a_sq)
            t.op(lambda e, fdc=fdc, st=first[0]: e.matmul(
                ps[3][0:1, :], ones_cb[:], sq_pp[:, fdc % 2, :],
                start=st, stop=(fdc == KD - 1), skip_group_check=True), inc="pe")
            gate[("sq", fdc % 2)] = ("pe", cnt["pe"])
            gate[("s1", fdc % 4)] = ("pe", cnt["pe"])
            first[0] = False

        for dc in range(KD):
            dp = dc % 2
            wi = wbase + 20 + dc
            sp3 = wi % 3
            wwait(wi)
            if dc == 0:
                t.wait("dve", dv_attn_done)
            t.wait_gate(gate.get(f"pso{dp}"))
            for kd in range(KD):
                t.op(lambda e, kd=kd, sp3=sp3, dp=dp: e.matmul(
                    ps[dp][:], wqo_sb[:, sp3, kd, :], attn_out[:, kd, :],
                    start=(kd == 0), stop=(kd == KD - 1), skip_group_check=True),
                    inc="pe" if kd == KD - 1 else None)
            pe_o = cnt["pe"]
            wdone(wi, pe_o)
            a_o = s.do(lambda e, dc=dc, dp=dp: e.copy(
                out=expT[:, dc // 8, dc % 8, :], in_=ps[dp][:]),
                extra_waits=[("pe", pe_o)])
            gate[f"pso{dp}"] = ("act", a_o)
            d_s1 = v.do(lambda e, dc=dc, n=n: e.tensor_add(
                s1_pp[:, dc % 4, :], x_res[:, dc, n * W:(n + 1) * W],
                expT[:, dc // 8, dc % 8, :]),
                extra_waits=[("act", a_o), gate.get(("s1", dc % 4))])
            lnfifo.append((dc, d_s1))
            if len(lnfifo) >= 3:
                pop_stats()
        while lnfifo:
            pop_stats()
        pe_stats = cnt["pe"]
        wpump(wbase + 36 + 3)
        v.do(lambda e: e.tensor_scalar_mul(lnrow[:, 0, :], ps[2][0:1, :], 1.0 / DM),
             extra_waits=[("pe", pe_stats)])
        v.do(lambda e: e.tensor_scalar_mul(lnrow[:, 1, :], ps[3][0:1, :], 1.0 / DM))
        v.do(lambda e: e.tensor_mul(lnrow[:, 2, :], lnrow[:, 0, :], lnrow[:, 0, :]))
        v.do(lambda e: e.tensor_sub(lnrow[:, 1, :], lnrow[:, 1, :],
                                    lnrow[:, 2, :]))
        dv_var = v.do(lambda e: e.tensor_scalar_add(lnrow[:, 1, :],
                                                    lnrow[:, 1, :], EPS))
        gate["ps_sc"] = ("dve", dv_var)
        s.do(lambda e: e.activation(out=lnrow[:, 2, :], in_=lnrow[:, 1, :],
                                    func=AF.Ln),
             extra_waits=[("dve", dv_var)])
        a_rs = s.do(lambda e: e.activation(out=lnrow[:, 1, :], in_=lnrow[:, 2, :],
                                           func=AF.Exp, scale=-0.5))
        t.wait("act", a_rs)
        t.wait_gate(gate.get("ps4"))
        t.op(lambda e: e.matmul(ps[4][:], ones_rf[:], lnrow[:, 0, :],
                                start=True, stop=True, skip_group_check=True), inc="pe")
        t.wait_gate(gate.get("ps5"))
        t.op(lambda e: e.matmul(ps[5][:], ones_rf[:], lnrow[:, 1, :],
                                start=True, stop=True, skip_group_check=True), inc="pe")
        pe_bc = cnt["pe"]
        a_mb = s.do(lambda e: e.copy(out=mb_sb[:], in_=ps[4][:]),
                    extra_waits=[("pe", pe_bc)])
        a_rb2 = s.do(lambda e: e.copy(out=rb2_sb[:], in_=ps[5][:]))
        gate["ps4"] = ("act", a_mb)
        gate["ps5"] = ("act", a_rb2)
        for dc in range(KD):
            v.do(lambda e, dc=dc, n=n: e.tensor_add(
                s1_pp[:, dc % 4, :], x_res[:, dc, n * W:(n + 1) * W],
                expT[:, dc // 8, dc % 8, :]),
                extra_waits=[("act", a_rb2)] if dc == 0 else ())
            v.do(lambda e, dc=dc: e.tensor_sub(ttmp[:, 0:512], s1_pp[:, dc % 4, :], mb_sb[:]))
            v.do(lambda e, dc=dc, n=n: e.tensor_mul(
                x_res[:, dc, n * W:(n + 1) * W], ttmp[:, 0:512], rb2_sb[:]))

    halo.close()
    att.close()

    R = "right"
    def sbr_(stack, name, shape, dt):
        return stack.enter_context(nc.sbuf_tensor(name, list(shape), dt, side=R))

    ffn = ExitStack()
    racc = sbr_(ffn, "racc", (128, KD, T), BF16)
    sg_row = sbr_(ffn, "sg_row", (1, T), BF16)
    sidx = sbr_(ffn, "sidx", (128, 16, 4), mybir.dt.int32)
    wg_sb = sbr_(ffn, "wg_sb", (128, 2, KD, 128), BF16)
    wu_sb = sbr_(ffn, "wu_sb", (128, 2, KD, 128), BF16)
    wd_sb = sbr_(ffn, "wd_sb", (128, 2, IC, 128), BF16)

    rt = ExitStack()
    identf = sbr_(rt, "identf_t", (128, 128), F32)
    routert_sb = sbr_(rt, "routert_sb", (128, KD, 16), BF16)
    shgw_sb = sbr_(rt, "shgw_sb", (128, KD, 1), BF16)
    iota16_sb = sbr_(rt, "iota16_sb", (128, 16), F32)
    lg_sb = sbr_(rt, "lg_sb", (16, 4, 512), F32)
    ltok = sbr_(rt, "ltok", (128, 16, 16), F32)
    ptmp = sbr_(rt, "ptmp", (128, 16, 16), F32)
    selacc = sbr_(rt, "selacc", (128, 16, 16), F32)
    maskt = sbr_(rt, "maskt", (128, 16, 16), F32)
    comb_tok = sbr_(rt, "comb_tok", (128, 16, 16), F32)
    mx16 = sbr_(rt, "mx16", (128, 16), F32)
    ts16 = sbr_(rt, "ts16", (128, 16), F32)
    comb_d = sbr_(rt, "comb_d", (16, T), F32)
    maskr4 = sbr_(rt, "maskr4", (128, 16, 16, 4), F32)
    eidr4 = sbr_(rt, "eidr4", (128, 16, 4), F32)
    valr4 = sbr_(rt, "valr4", (128, 16, 4), F32)
    sidxf = sbr_(rt, "sidxf", (128, 16, 4), F32)
    vst2 = sbr_(rt, "vst2", (128, 16, 4, 2), F32)
    postok = sbr_(rt, "postok", (128, 16, 16), F32)
    cum16 = sbr_(rt, "cum16", (16, T), F32)
    tmp1616 = sbr_(rt, "tmp1616", (128, 16, 16), F32)
    x1tok = sbr_(rt, "x1tok", (128, 2, DM), BF16)

    P.phase("R")
    t.wait("out0", zx_total)
    s.wait("out0", zx_total)
    v.wait("out0", zx_total)
    dma(identf[:], dr["identf"], "p0")
    dma(routert_sb[:], dr["routert"], "p0")
    dma(shgw_sb[:], dr["shgw"], "p0")
    dma(iota16_sb[:], dr["iota16"], "p0")
    p0r = cnt["p0"]
    dv_ln_done = cnt["dve"]
    for kq in range(4):
        s.do(lambda e, kq=kq: e.copy(out=racc[:, kq * 4:(kq + 1) * 4, :],
                                     in_=x_res[:, kq * 4:(kq + 1) * 4, :]),
             extra_waits=[("dve", dv_ln_done)] if kq == 0 else ())
    t.wait("p0", p0r)
    for tq in range(4):
        t.wait_gate(gate.get("ps0"))
        for kd in range(KD):
            t.op(lambda e, kd=kd, tq=tq: e.matmul(
                ps[0][0:16, :], routert_sb[:, kd, :],
                x_res[:, kd, tq * 512:(tq + 1) * 512],
                start=(kd == 0), stop=(kd == KD - 1), skip_group_check=True),
                inc="pe" if kd == KD - 1 else None)
        pe_lg = cnt["pe"]
        a_lg = s.do(lambda e, tq=tq: e.copy(out=lg_sb[:, tq, :], in_=ps[0][0:16, :]),
                    extra_waits=[("pe", pe_lg)])
        gate["ps0"] = ("act", a_lg)
        t.wait_gate(gate.get("ps_sc"))
        for kd in range(KD):
            t.op(lambda e, kd=kd, tq=tq: e.matmul(
                ps[3][0:1, :], shgw_sb[:, kd, :],
                x_res[:, kd, tq * 512:(tq + 1) * 512],
                start=(kd == 0), stop=(kd == KD - 1), skip_group_check=True),
                inc="pe" if kd == KD - 1 else None)
        pe_sg = cnt["pe"]
        a_sg = s.do(lambda e, tq=tq: e.activation(
            out=sg_row[:, tq * 512:(tq + 1) * 512], in_=ps[3][0:1, :],
            func=AF.Sigmoid), extra_waits=[("pe", pe_sg)])
        gate["ps_sc"] = ("act", a_sg)
    for tc in range(16):
        t.wait_gate(gate.get("ps1"))
        t.op(lambda e, tc=tc: e.transpose(
            ps[1][:, 0:16], lg_sb[:, tc // 4, (tc % 4) * 128:(tc % 4 + 1) * 128],
            identf[0:16, 0:16]), inc="pe")
        pe_tr = cnt["pe"]
        a_ltk = s.do(lambda e, tc=tc: e.copy(out=ltok[:, tc, :], in_=ps[1][:, 0:16]),
                     extra_waits=[("pe", pe_tr)])
        gate["ps1"] = ("act", a_ltk)
    a_lt = cnt["act"]
    bc16 = lambda ap: ap.unsqueeze(2).broadcast_to([128, 16, 16])
    v.do(lambda e: e.tensor_reduce(mx16[:], ltok[:], AX, ALU.max),
         extra_waits=[("act", a_lt)])
    v.do(lambda e: e.tensor_tensor(ptmp[:], ltok[:], bc16(mx16[:]), ALU.subtract))
    a_ex = s.do(lambda e: e.activation(out=ptmp[:], in_=ptmp[:], func=AF.Exp),
                extra_waits=[("dve", cnt["dve"])])
    v.do(lambda e: e.tensor_reduce(mx16[:], ptmp[:], AX, ALU.add),
         extra_waits=[("act", a_ex)])
    v.do(lambda e: e.reciprocal(out=ts16[:], in_=mx16[:]))
    v.do(lambda e: e.tensor_tensor(ptmp[:], ptmp[:], bc16(ts16[:]), ALU.mult))
    v.do(lambda e: e.memset(selacc[:], 0.0))
    v.do(lambda e: e.memset(ts16[:], 0.0))
    for r in range(4):
        v.do(lambda e: e.tensor_reduce(mx16[:], ptmp[:], AX, ALU.max))
        v.do(lambda e: e.tensor_tensor(maskt[:], ptmp[:], bc16(mx16[:]), ALU.is_ge))
        v.do(lambda e, r=r: e.tensor_copy(out=maskr4[:, :, :, r], in_=maskt[:]))
        v.do(lambda e: e.tensor_tensor(
            tmp1616[:], maskt[:], iota16_sb[:].unsqueeze(1).broadcast_to([128, 16, 16]),
            ALU.mult))
        v.do(lambda e, r=r: e.tensor_reduce(eidr4[:, :, r], tmp1616[:], AX, ALU.add))
        v.do(lambda e, r=r: e.tensor_copy(out=valr4[:, :, r], in_=mx16[:]))
        v.do(lambda e: e.tensor_tensor(maskt[:], maskt[:], ptmp[:], ALU.mult))
        v.do(lambda e: e.tensor_add(selacc[:], selacc[:], maskt[:]))
        v.do(lambda e: e.tensor_add(ts16[:], ts16[:], mx16[:]))
        if r < 3:
            v.do(lambda e: e.tensor_sub(ptmp[:], ptmp[:], maskt[:]))
    v.do(lambda e: e.tensor_scalar_max(ts16[:], ts16[:], 1e-9))
    v.do(lambda e: e.reciprocal(out=ts16[:], in_=ts16[:]))
    for r in range(4):
        v.do(lambda e, r=r: e.tensor_tensor(valr4[:, :, r], valr4[:, :, r],
                                            ts16[:], ALU.mult))
    dv_cmb = v.do(lambda e: e.tensor_tensor(comb_tok[:], selacc[:],
                                            bc16(ts16[:]), ALU.mult))
    t.wait("dve", dv_cmb)
    for tc in range(16):
        t.wait_gate(gate.get("ps_sc"))
        t.op(lambda e, tc=tc: e.transpose(
            ps[2][0:16, 0:128], comb_tok[:, tc, :], identf[:]), inc="pe")
        pe_ct = cnt["pe"]
        a_cd = s.do(lambda e, tc=tc: e.copy(out=comb_d[:, tc * 128:(tc + 1) * 128],
                                            in_=ps[2][0:16, 0:128]),
                    extra_waits=[("pe", pe_ct)])
        gate["ps_sc"] = ("act", a_cd)
    a_cdall = cnt["act"]
    lg_flat = lg_sb[:].rearrange("e a b -> e (a b)")
    v.do(lambda e: e.tensor_single_scalar(out=comb_d[:], in_=comb_d[:], scalar=0.0,
                                          op=ALU.is_gt),
         extra_waits=[("act", a_cdall)])
    v.do(lambda e: e.tensor_tensor_scan(out=cum16[:], data0=comb_d[:],
                                        data1=comb_d[:], initial=0.0,
                                        op0=ALU.add, op1=ALU.max))
    v.do(lambda e: e.tensor_scalar_add(cum16[:], cum16[:], -1.0))
    v.do(lambda e: e.tensor_single_scalar(out=lg_flat, in_=cum16[:],
                                          scalar=float(CEXP), op=ALU.is_lt))
    v.do(lambda e: e.tensor_mul(lg_flat, lg_flat, comb_d[:]))
    v.do(lambda e: e.tensor_scalar(out=comb_d[:], in0=lg_flat, scalar1=-1.0,
                                   scalar2=-1.0e9, op0=ALU.add, op1=ALU.mult))
    v.do(lambda e: e.tensor_mul(cum16[:], cum16[:], lg_flat))
    dv_pos = v.do(lambda e: e.tensor_add(cum16[:], cum16[:], comb_d[:]))
    t.wait("dve", dv_pos)
    for tc in range(16):
        t.wait_gate(gate.get("ps1"))
        t.op(lambda e, tc=tc: e.transpose(
            ps[1][:, 0:16], cum16[0:16, tc * 128:(tc + 1) * 128],
            identf[0:16, 0:16]), inc="pe")
        pe_pt = cnt["pe"]
        a_pt = s.do(lambda e, tc=tc: e.copy(out=postok[:, tc, :], in_=ps[1][:, 0:16]),
                    extra_waits=[("pe", pe_pt)])
        gate["ps1"] = ("act", a_pt)
    a_ptall = cnt["act"]
    v.wait("act", a_ptall)
    for r in range(4):
        v.do(lambda e, r=r: e.tensor_tensor(
            tmp1616[:], maskr4[:, :, :, r], postok[:], ALU.mult))
        v.do(lambda e, r=r: e.tensor_reduce(sidxf[:, :, r], tmp1616[:], AX, ALU.add))
        v.do(lambda e, r=r: e.scalar_tensor_tensor(
            out=sidxf[:, :, r], in0=eidr4[:, :, r], scalar=float(CEXP),
            in1=sidxf[:, :, r], op0=ALU.mult, op1=ALU.add))
    dv_sidx = v.do(lambda e: e.tensor_copy(out=sidx[:], in_=sidxf[:]))
    v.do(lambda e: e.tensor_copy(out=vst2[:, :, :, 0], in_=valr4[:]))
    dv_vst = v.do(lambda e: e.tensor_copy(out=vst2[:, :, :, 1], in_=valr4[:]))

    g.wait("dve", dv_vst)
    g.wait("out0", zx_total)
    regbox = {}
    g.op(lambda e: regbox.__setitem__("b", e.to_reg(16 * CEXP - 1)) or e.engine_nop())
    for tc in range(16):
        yp = tc % 2
        for dc in range(KD):
            tp = dc % 2
            t.wait_gate(gate.get(("pst", tp)))
            t.op(lambda e, dc=dc, tc=tc, tp=tp: e.transpose(
                pst[tp][:], x_res[:, dc, tc * 128:(tc + 1) * 128], identb[:]),
                inc="pe")
            pe_t2 = cnt["pe"]
            if dc % 2 == 0:
                a_x = s.do(lambda e, dc=dc, tp=tp, yp=yp: e.copy(
                    out=x1tok[:, yp, dc * 128:(dc + 1) * 128], in_=pst[tp][:]),
                    extra_waits=[("pe", pe_t2)] +
                        ([gate[("x1tok", yp)]] if dc == 0 and ("x1tok", yp) in gate
                         else []))
                gate[("pst", tp)] = ("act", a_x)
            else:
                d_x = v.do(lambda e, dc=dc, tp=tp, yp=yp: e.tensor_copy(
                    out=x1tok[:, yp, dc * 128:(dc + 1) * 128], in_=pst[tp][:]),
                    extra_waits=[("pe", pe_t2)] +
                        ([gate[("x1tok", yp)]] if dc == 1 and ("x1tok", yp) in gate
                         else []))
                gate[("pst", tp)] = ("dve", d_x)
        g.wait("act", cnt["act"])
        g.wait("dve", cnt["dve"])
        ssem = f"sca{yp}"
        for r in range(4):
            g.op(lambda e, tc=tc, r=r, yp=yp: e.indirect_dma_start(
                out=xg_dram,
                out_offset=IndirectOffsetOnAxis(sidx[:, tc, r:r + 1], 0),
                in_=x1tok[:, yp, :], in_offset=None,
                bounds_check=regbox["b"], oob_is_err=False),
                inc=ssem, n=16)
            g.op(lambda e, tc=tc, r=r: e.indirect_dma_start(
                out=cg_dram,
                out_offset=IndirectOffsetOnAxis(sidx[:, tc, r:r + 1], 0),
                in_=vst2[:, tc, r, :], in_offset=None,
                bounds_check=regbox["b"], oob_is_err=False),
                inc=ssem, n=16)
        gate[("x1tok", yp)] = (ssem, cnt[ssem])
    sca_tot0, sca_tot1 = cnt["sca0"], cnt["sca1"]

    rt.close()

    rt2 = ExitStack()
    act_sh = sbr_(rt2, "act_sh", (128, IC, T), BF16)
    sgb = sbr_(rt2, "sgb", (128, T), BF16)
    silsh = sbr_(rt2, "silsh", (128, 2, 512), BF16)

    P.phase("F")
    t.wait("sca0", sca_tot0); t.wait("sca1", sca_tot1)
    s.wait("sca0", sca_tot0); s.wait("sca1", sca_tot1)
    v.wait("sca0", sca_tot0); v.wait("sca1", sca_tot1)
    for e_i in range(16, NE):
        src_row = lambda tq: sg_row[0:1, tq * 512:(tq + 1) * 512]
        t.wait("dve", cnt["dve"])
        if e_i == 16:
            t.wait_gate(gate.get("ps_sc"))
        for tq in range(4):
            cp = tq % 2
            t.wait_gate(gate.get(f"ps{cp}"))
            t.op(lambda e, tq=tq, cp=cp, src=src_row: e.matmul(
                ps[cp][:], ones_rb[:], src(tq), start=True, stop=True, skip_group_check=True), inc="pe")
            pe_cb = cnt["pe"]
            a_cb = s.do(lambda e, tq=tq, cp=cp: e.copy(
                out=sgb[:, tq * 512:(tq + 1) * 512], in_=ps[cp][:]),
                extra_waits=[("pe", pe_cb)])
            gate[f"ps{cp}"] = ("act", a_cb)
        a_cbl = cnt["act"]
        for i in range(IC):
            ip = i % 2
            gsem, usem = f"wg{ip}", f"wu{ip}"
            g.wait_gate(gate.get(("wg", ip)))
            dma(wg_sb[:, ip], dr["wg_all"][e_i, i], gsem)
            g.wait_gate(gate.get(("wu", ip)))
            dma(wu_sb[:, ip], dr["wu_all"][e_i, i], usem)
            g_done, u_done = cnt[gsem], cnt[usem]
            t.wait(gsem, g_done)
            t.wait(usem, u_done)
            for tq in range(4):
                tp = tq % 2
                t.wait_gate(gate.get(f"ps{tp}"))
                if i == 0 and tq < 2:
                    t.wait("act", a_cbl)
                for kd in range(KD):
                    t.op(lambda e, kd=kd, ip=ip, tq=tq, tp=tp: e.matmul(
                        ps[tp][:], wg_sb[:, ip, kd, :],
                        x_res[:, kd, tq * 512:(tq + 1) * 512],
                        start=(kd == 0), stop=(kd == KD - 1), skip_group_check=True),
                        inc="pe" if kd == KD - 1 else None)
                pe_g = cnt["pe"]
                t.wait_gate(gate.get(("psu", tp)))
                for kd in range(KD):
                    t.op(lambda e, kd=kd, ip=ip, tq=tq, tp=tp: e.matmul(
                        ps[2 + tp][:], wu_sb[:, ip, kd, :],
                        x_res[:, kd, tq * 512:(tq + 1) * 512],
                        start=(kd == 0), stop=(kd == KD - 1), skip_group_check=True),
                        inc="pe" if kd == KD - 1 else None)
                pe_u = cnt["pe"]
                a_si = s.do(lambda e, tp=tp: e.activation(
                    out=silsh[:, tp, :], in_=ps[tp][:], func=AF.Silu),
                    extra_waits=[("pe", pe_g), gate.get(("sil", tp))])
                gate[f"ps{tp}"] = ("act", a_si)
                d_u = v.do(lambda e, tp=tp: e.tensor_mul(
                    silsh[:, tp, :], silsh[:, tp, :], ps[2 + tp][:]),
                    extra_waits=[("act", a_si), ("pe", pe_u)])
                gate[("psu", tp)] = ("dve", d_u)
                d_sc = v.do(lambda e, i=i, tq=tq, tp=tp: e.tensor_mul(
                    act_sh[:, i, tq * 512:(tq + 1) * 512], silsh[:, tp, :],
                    sgb[:, tq * 512:(tq + 1) * 512]),
                    extra_waits=[("act", a_cbl)] if i == 0 else ())
                gate[("sil", tp)] = ("dve", d_sc)
            gate[("wg", ip)] = ("pe", pe_g)
            gate[("wu", ip)] = ("pe", pe_u)
        dv_A = cnt["dve"]
        first_b = True
        for dc in range(KD):
            dp = dc % 2
            dsem = f"wd{dp}"
            g.wait_gate(gate.get(("wd", dp)))
            dma(wd_sb[:, dp], dr["wd_all"][e_i, dc], dsem)
            d_done = cnt[dsem]
            t.wait(dsem, d_done)
            if first_b:
                t.wait("dve", dv_A)
                first_b = False
            for tq in range(4):
                yp2 = 4 + ((dc * 4 + tq) % 2)
                t.wait_gate(gate.get(f"ps{yp2}"))
                for i in range(IC):
                    t.op(lambda e, i=i, dp=dp, tq=tq, yp2=yp2: e.matmul(
                        ps[yp2][:], wd_sb[:, dp, i, :],
                        act_sh[:, i, tq * 512:(tq + 1) * 512],
                        start=(i == 0), stop=(i == IC - 1), skip_group_check=True),
                        inc="pe" if i == IC - 1 else None)
                pe_y = cnt["pe"]
                d_y = v.do(lambda e, dc=dc, tq=tq, yp2=yp2: e.tensor_add(
                    racc[:, dc, tq * 512:(tq + 1) * 512],
                    racc[:, dc, tq * 512:(tq + 1) * 512], ps[yp2][:]),
                    extra_waits=[("pe", pe_y)])
                gate[f"ps{yp2}"] = ("dve", d_y)
            gate[("wd", dp)] = ("pe", pe_y)

    rt2.close()
    xsc.close()

    fs = ExitStack()
    act_sp = sbr_(fs, "act_sp", (128, IC, CEXP), BF16)
    xgT = sbr_(fs, "xgT", (128, 2, KD, CEXP), BF16)
    ytk = sbr_(fs, "ytk", (128, CEXP // 128, DM), BF16)
    cgb = sbr_(fs, "cgb", (128, CEXP), BF16)
    cgrow = sbr_(fs, "cgrow", (1, 2, 2 * CEXP), F32)
    silsp = sbr_(fs, "silsp", (128, 2, 512), BF16)
    ystage = sbr_(fs, "ystage", (128, 4, 128), BF16)

    CCH = [(0, 512), (512, CEXP - 512)]
    NCC = CEXP // 128

    P.phase("FS")
    yq.wait("sca0", sca_tot0)
    yq.wait("sca1", sca_tot1)

    def emit_xg_loads(e_i):
        xp = e_i % 2
        xsem = f"xg{xp}"
        yq.wait_gate(gate.get(("xgT", xp)))
        for kd in range(KD):
            yq.op(lambda e, e_i=e_i, kd=kd, xp=xp: e.dma_start_transpose(
                out=xgT[:, xp, kd, :],
                in_=xg_dram[e_i * CEXP:(e_i + 1) * CEXP, kd * 128:(kd + 1) * 128]),
                inc=xsem, n=16)
        yq.op(lambda e, e_i=e_i, xp=xp: e.dma_start(
            out=cgrow[:, xp, :], in_=cg_dram[e_i * CEXP:(e_i + 1) * CEXP, :]
            .rearrange("c o -> (c o)").unsqueeze(0)), inc=xsem, n=16)
        return cnt[xsem]

    xg_ready = {0: emit_xg_loads(0)}
    xg_ready[1] = emit_xg_loads(1)
    for e_i in range(16):
        xp = e_i % 2
        xsem = f"xg{xp}"
        cg_done = xg_ready[e_i]
        t.wait(xsem, cg_done)
        t.wait("dve", cnt["dve"])
        for ci, (c0, cw) in enumerate(CCH):
            cp = ci % 2
            t.wait_gate(gate.get(f"ps{cp}"))
            t.op(lambda e, c0=c0, cw=cw, cp=cp, xp=xp: e.matmul(
                ps[cp][:, 0:cw],
                ones_rf[:],
                cgrow[:, xp, :].rearrange("o (c two) -> o c two", two=2)[:, c0:c0 + cw, 0],
                start=True, stop=True, skip_group_check=True), inc="pe")
            pe_cb = cnt["pe"]
            a_cb = s.do(lambda e, c0=c0, cw=cw, cp=cp: e.copy(
                out=cgb[:, c0:c0 + cw], in_=ps[cp][:, 0:cw]),
                extra_waits=[("pe", pe_cb)])
            gate[f"ps{cp}"] = ("act", a_cb)
        a_cbl = cnt["act"]
        for i in range(IC):
            ip = i % 2
            gsem, usem = f"wg{ip}", f"wu{ip}"
            g.wait_gate(gate.get(("wg", ip)))
            dma(wg_sb[:, ip], dr["wg_all"][e_i, i], gsem)
            g.wait_gate(gate.get(("wu", ip)))
            dma(wu_sb[:, ip], dr["wu_all"][e_i, i], usem)
            g_done, u_done = cnt[gsem], cnt[usem]
            t.wait(gsem, g_done)
            t.wait(usem, u_done)
            if i == 0:
                t.wait(xsem, cg_done)
            if i == 0:
                t.wait(xsem, cg_done)
            for ci, (c0, cw) in enumerate(CCH):
                tp = ci % 2
                t.wait_gate(gate.get(f"ps{tp}"))
                if i == 0:
                    t.wait("act", a_cbl)
                for kd in range(KD):
                    t.op(lambda e, kd=kd, ip=ip, c0=c0, cw=cw, tp=tp, xp=xp: e.matmul(
                        ps[tp][:, 0:cw], wg_sb[:, ip, kd, :],
                        xgT[:, xp, kd, c0:c0 + cw],
                        start=(kd == 0), stop=(kd == KD - 1), skip_group_check=True),
                        inc="pe" if kd == KD - 1 else None)
                pe_g = cnt["pe"]
                t.wait_gate(gate.get(("psu", tp)))
                for kd in range(KD):
                    t.op(lambda e, kd=kd, ip=ip, c0=c0, cw=cw, tp=tp, xp=xp: e.matmul(
                        ps[2 + tp][:, 0:cw], wu_sb[:, ip, kd, :],
                        xgT[:, xp, kd, c0:c0 + cw],
                        start=(kd == 0), stop=(kd == KD - 1), skip_group_check=True),
                        inc="pe" if kd == KD - 1 else None)
                pe_u = cnt["pe"]
                a_si = s.do(lambda e, cw=cw, tp=tp: e.activation(
                    out=silsp[:, tp, 0:cw], in_=ps[tp][:, 0:cw], func=AF.Silu),
                    extra_waits=[("pe", pe_g), gate.get(("sil", tp))])
                gate[f"ps{tp}"] = ("act", a_si)
                d_u = v.do(lambda e, cw=cw, tp=tp: e.tensor_mul(
                    silsp[:, tp, 0:cw], silsp[:, tp, 0:cw], ps[2 + tp][:, 0:cw]),
                    extra_waits=[("act", a_si), ("pe", pe_u)])
                gate[("psu", tp)] = ("dve", d_u)
                d_sc = v.do(lambda e, i=i, c0=c0, cw=cw, tp=tp: e.tensor_mul(
                    act_sp[:, i, c0:c0 + cw], silsp[:, tp, 0:cw],
                    cgb[:, c0:c0 + cw]))
                gate[("sil", tp)] = ("dve", d_sc)
            gate[("wg", ip)] = ("pe", pe_g)
            gate[("wu", ip)] = ("pe", pe_u)
        gate[("xgT", xp)] = ("pe", cnt["pe"])
        if e_i + 2 < 16:
            xg_ready[e_i + 2] = emit_xg_loads(e_i + 2)
        dv_A = cnt["dve"]
        first_b = True
        tfifo = []

        def pop_transpose():
            kdc, kcc, ka_ys, kslot = tfifo.pop(0)
            tp2 = kslot % 2
            t.wait("act", ka_ys)
            t.wait_gate(gate.get(("pst", tp2)))
            t.op(lambda e, kslot=kslot, tp2=tp2: e.transpose(
                pst[tp2][:], ystage[:, kslot, :], identb[:]), inc="pe")
            pe_tt = cnt["pe"]
            gate[("yst", kslot)] = ("pe", pe_tt)
            d_yt = v.do(lambda e, kdc=kdc, kcc=kcc, tp2=tp2: e.tensor_copy(
                out=ytk[:, kcc, kdc * 128:(kdc + 1) * 128], in_=pst[tp2][:]),
                extra_waits=[("pe", pe_tt)] +
                    ([("ya", gate.get("ytk_rd", (None, 0))[1])]
                     if kdc == 0 and kcc < 2 and "ytk_rd" in gate else []))
            gate[("pst", tp2)] = ("dve", d_yt)

        kidx = 0
        for dc in range(KD):
            dp = dc % 2
            dsem = f"wd{dp}"
            g.wait_gate(gate.get(("wd", dp)))
            dma(wd_sb[:, dp], dr["wd_all"][e_i, dc], dsem)
            d_done = cnt[dsem]
            t.wait(dsem, d_done)
            if first_b:
                t.wait("dve", dv_A)
                first_b = False
            for cc in range(NCC):
                yp2 = 4 + (cc % 2)
                kslot = kidx % 4
                kidx += 1
                t.wait_gate(gate.get(f"ps{yp2}"))
                for i in range(IC):
                    t.op(lambda e, i=i, dp=dp, cc=cc, yp2=yp2: e.matmul(
                        ps[yp2][:, 0:128], wd_sb[:, dp, i, :],
                        act_sp[:, i, cc * 128:(cc + 1) * 128],
                        start=(i == 0), stop=(i == IC - 1), skip_group_check=True),
                        inc="pe" if i == IC - 1 else None)
                pe_y = cnt["pe"]
                a_ys = s.do(lambda e, kslot=kslot, yp2=yp2: e.copy(
                    out=ystage[:, kslot, :], in_=ps[yp2][:, 0:128]),
                    extra_waits=[("pe", pe_y), gate.get(("yst", kslot))])
                gate[f"ps{yp2}"] = ("act", a_ys)
                tfifo.append((dc, cc, a_ys, kslot))
                if len(tfifo) >= 3:
                    pop_transpose()
            gate[("wd", dp)] = ("pe", pe_y)
        while tfifo:
            pop_transpose()
        dv_ytk = cnt["dve"]
        yq.wait("dve", dv_ytk)
        for cc in range(NCC):
            yq.op(lambda e, e_i=e_i, cc=cc: e.dma_start(
                out=y_all_dram[e_i * CEXP + cc * 128:e_i * CEXP + (cc + 1) * 128, :],
                in_=ytk[:, cc, :]), inc="ya", n=16)
        gate["ytk_rd"] = ("ya", cnt["ya"])
    ya_total = cnt["ya"]

    fs.close()

    epi = ExitStack()
    ytok = sbr_(epi, "ytok", (128, 2, DM), BF16)
    ysq = sbr_(epi, "ysq", (128, DM), BF16)
    yout = sbr_(epi, "yout", (128, 2, DM), F32)
    cols = sbr_(epi, "cols", (128, 8), F32)
    yr = sbr_(epi, "yr", (128, 2, 4, DM), BF16)

    P.phase("EPI")
    t.wait("ya", ya_total)
    s.wait("ya", ya_total)
    v.wait("ya", ya_total)
    regbox2 = {}
    g.op(lambda e: regbox2.__setitem__("b", e.to_reg(16 * CEXP - 1)) or e.engine_nop())
    g.wait("ya", ya_total)
    dv_racc_done = cnt["dve"]
    t.wait("dve", dv_racc_done)
    last_dv_y = 0
    last_comb = {}

    def emit_transposes(tc):
        yp = tc % 2
        g.wait_gate(gate.get(("yr", yp)))
        for r in range(4):
            g.op(lambda e, tc=tc, r=r, yp=yp: e.indirect_dma_start(
                out=yr[:, yp, r, :], out_offset=None,
                in_=y_all_dram,
                in_offset=IndirectOffsetOnAxis(sidx[:, tc, r:r + 1], 0),
                bounds_check=regbox2["b"], oob_is_err=False),
                inc=f"yg{yp}", n=16)
        yg_done = (f"yg{yp}", cnt[f"yg{yp}"])
        for dc in range(KD):
            tp = dc % 2
            t.wait_gate(gate.get(("pst", tp)))
            t.op(lambda e, dc=dc, tc=tc, tp=tp: e.transpose(
                pst[tp][:], racc[:, dc, tc * 128:(tc + 1) * 128], identb[:]),
                inc="pe")
            pe_t = cnt["pe"]
            if dc % 2 == 0:
                a_yt = s.do(lambda e, dc=dc, tp=tp, yp=yp: e.copy(
                    out=ytok[:, yp, dc * 128:(dc + 1) * 128], in_=pst[tp][:]),
                    extra_waits=[("pe", pe_t)] +
                        ([("dve", last_dv_y)] if dc == 0 and tc > 1 else []))
                gate[("pst", tp)] = ("act", a_yt)
            else:
                d_yt = v.do(lambda e, dc=dc, tp=tp, yp=yp: e.tensor_copy(
                    out=ytok[:, yp, dc * 128:(dc + 1) * 128], in_=pst[tp][:]),
                    extra_waits=[("pe", pe_t)])
                gate[("pst", tp)] = ("dve", d_yt)
        return cnt["act"], cnt["dve"], yg_done

    def emit_stats(tc, a_ytl, d_ytl, yg_done):
        yp = tc % 2
        op = tc % 2
        osem = f"out{op}"
        v.do(lambda e, yp=yp: e.tensor_add(ysq[:], yr[:, yp, 0, :], yr[:, yp, 1, :]),
             extra_waits=[yg_done, ("act", a_ytl)])
        v.do(lambda e, yp=yp: e.tensor_add(ytok[:, yp, :], ytok[:, yp, :], ysq[:]))
        v.do(lambda e, yp=yp: e.tensor_add(ysq[:], yr[:, yp, 2, :], yr[:, yp, 3, :]))
        dv_comb = v.do(lambda e, yp=yp: e.tensor_add(ytok[:, yp, :], ytok[:, yp, :],
                                                     ysq[:]))
        gate[("yr", yp)] = ("dve", dv_comb)
        v.do(lambda e, yp=yp: e.tensor_reduce(cols[:, 0:1], ytok[:, yp, :], AX,
                                              ALU.add))
        v.do(lambda e: e.tensor_scalar_mul(cols[:, 0:1], cols[:, 0:1], 1.0 / DM))
        a_q = s.do(lambda e, yp=yp: e.activation(out=ysq[:], in_=ytok[:, yp, :],
                                                 func=AF.Square),
                   extra_waits=[("dve", cnt["dve"])])
        v.do(lambda e: e.tensor_reduce(cols[:, 1:2], ysq[:], AX, ALU.add),
             extra_waits=[("act", a_q)])
        v.do(lambda e: e.tensor_scalar_mul(cols[:, 1:2], cols[:, 1:2], 1.0 / DM))
        v.do(lambda e: e.tensor_mul(cols[:, 2:3], cols[:, 0:1], cols[:, 0:1]))
        v.do(lambda e: e.tensor_sub(cols[:, 3:4], cols[:, 1:2], cols[:, 2:3]))
        dv_v = v.do(lambda e: e.tensor_scalar_add(cols[:, 3:4], cols[:, 3:4], EPS))
        s.do(lambda e: e.activation(out=cols[:, 4:5], in_=cols[:, 3:4],
                                    func=AF.Ln),
             extra_waits=[("dve", dv_v)])
        a_r = s.do(lambda e: e.activation(out=cols[:, 5:6], in_=cols[:, 4:5],
                                          func=AF.Exp, scale=-0.5))
        dv_y = v.do(lambda e, op=op, yp=yp: e.tensor_scalar(
            out=yout[:, op, :], in0=ytok[:, yp, :], scalar1=cols[:, 0:1],
            scalar2=cols[:, 5:6], op0=ALU.subtract, op1=ALU.mult),
            extra_waits=[("act", a_r), (osem, cnt[osem])])
        yq.wait("dve", dv_y)
        yq.op(lambda e, tc=tc, op=op: e.dma_start(
            out=y[tc * 128:(tc + 1) * 128, :], in_=yout[:, op, :]),
            inc=osem, n=16)
        return dv_y

    pend = None
    for tc in range(KD + 1):
        if tc < KD:
            marks = emit_transposes(tc)
        if pend is not None:
            last_dv_y = emit_stats(pend[0], *pend[1])
        pend = (tc, marks) if tc < KD else None
    yq.wait("out0", cnt["out0"])
    yq.wait("out1", cnt["out1"])

    epi.close()
    ffn.close()
    P.replay()
    es.close()
    return nc


_NC_CACHE = None


def kernel(**inputs):
    global _NC_CACHE
    hp = host_prep(inputs)
    if _NC_CACHE is None:
        _NC_CACHE = build_nc()
    nc = _NC_CACHE
    shared = {k: hp[k] for k in ["wqt", "wkt", "wvt", "wot", "routert", "shgw",
                                 "wg_all", "wu_all", "wd_all", "maskR",
                                 "pswap", "identb", "identf", "iota16"]}
    in_maps = []
    for c in range(NCORES):
        m = dict(shared)
        m["xm"] = hp["xm"][c]
        m["xh"] = hp["xh"][c]
        m["c2"] = hp["c2"][c]
        m["s2"] = hp["s2"][c]
        m["maskA0"] = hp["maskA0"] if c == 0 else hp["maskR"]
        in_maps.append(m)
    res = run_bass_kernel_spmd(nc, in_maps, core_ids=list(range(NCORES)))
    out = np.concatenate([res.results[c]["y"] for c in range(NCORES)], axis=0)
    return out.reshape(1, NCORES * T, DM)


# version 11
# speedup vs baseline: 1.6583x; 1.0429x over previous
import math
import numpy as np
import ml_dtypes
from contextlib import ExitStack

import concourse.bass as bass
import concourse.mybir as mybir
from concourse.bass_utils import run_bass_kernel_spmd
from concourse.bass import IndirectOffsetOnAxis

F32 = mybir.dt.float32
BF16 = mybir.dt.bfloat16
AF = mybir.ActivationFunctionType
ALU = mybir.AluOpType
AX = mybir.AxisListType.X

NCORES = 8
T = 2048
W = 512
NB = 4
D = 128
H = 16
HK = 4
DM = 2048
KD = 16
NE = 20
IC = 11
CEXP = 640
CAP = 20.0
EPS = 1e-5
INV_SQRT_D = 1.0 / math.sqrt(D)

BF = ml_dtypes.bfloat16



class Prog:

    def __init__(self, nc, es):
        self.nc = nc
        self.cnt = {}
        self.sems = {}
        self.phases = []
        self.cur = None

    def mksem(self, es, name):
        self.sems[name] = es.enter_context(self.nc.semaphore(name))
        self.cnt[name] = 0

    def phase(self, name):
        self.cur = {"name": name, "t": [], "v": [], "s": [], "g": [], "y": []}
        self.phases.append(self.cur)

    def replay(self):
        for ph in self.phases:
            with self.nc.Block() as blk:
                if ph["g"]:
                    @blk.gpsimd
                    def _(g, ops=ph["g"]):
                        for f in ops:
                            f(g)
                if ph["t"]:
                    @blk.tensor
                    def _(t, ops=ph["t"]):
                        for f in ops:
                            f(t)
                if ph["s"]:
                    @blk.scalar
                    def _(s, ops=ph["s"]):
                        for f in ops:
                            f(s)
                if ph["v"]:
                    @blk.vector
                    def _(v, ops=ph["v"]):
                        for f in ops:
                            f(v)
                if ph["y"]:
                    @blk.sync
                    def _(y, ops=ph["y"]):
                        for f in ops:
                            f(y)


class ES_:
    def __init__(self, prog, key, own=None):
        self.p = prog
        self.key = key
        self.own = own

    def raw(self, fn):
        self.p.cur[self.key].append(fn)

    def wait(self, sem, val):
        if val is None or val <= 0:
            return
        self.raw(lambda eng, s=self.p.sems[sem], v=val: eng.wait_ge(s, v))

    def wait_gate(self, gate):
        if gate:
            self.wait(gate[0], gate[1])

    def op(self, fn, inc=None, n=1):
        if inc is None:
            self.raw(lambda eng, f=fn: f(eng))
            return None
        self.p.cnt[inc] += n
        tgt = self.p.cnt[inc]
        self.raw(lambda eng, f=fn, s=self.p.sems[inc], m=n: f(eng).then_inc(s, m))
        return tgt

    def do(self, fn, extra_waits=()):
        for sv in extra_waits:
            if sv:
                self.wait(sv[0], sv[1])
        self.wait(self.own, self.p.cnt[self.own])
        return self.op(fn, inc=self.own)



def _tile_lhsT(wT, oc_chunks):
    din, dout = wT.shape
    assert din == DM and dout == oc_chunks * 128
    return np.ascontiguousarray(
        wT.reshape(KD, 128, oc_chunks, 128).transpose(2, 1, 0, 3))


def host_prep(inputs):
    d = {}
    x = np.asarray(inputs["L0_attn_x"])[0]
    xT = np.ascontiguousarray(x.T)
    xT4 = xT.reshape(KD, 128, NCORES * T)
    xm, xh = [], []
    for c in range(NCORES):
        xm.append(np.ascontiguousarray(
            xT4[:, :, c * T:(c + 1) * T].transpose(1, 0, 2)))
        h0 = c * T - W if c > 0 else 0
        xh.append(np.ascontiguousarray(
            xT4[:, :, h0:h0 + W].transpose(1, 0, 2)))
    d["xm"], d["xh"] = xm, xh

    d["wqt"] = _tile_lhsT(np.ascontiguousarray(np.asarray(inputs["L0_attn_wq"]).T), 16)
    d["wkt"] = _tile_lhsT(np.ascontiguousarray(np.asarray(inputs["L0_attn_wk"]).T), 4)
    d["wot"] = _tile_lhsT(np.ascontiguousarray(np.asarray(inputs["L0_attn_wo"]).T), 16)
    d["wvt"] = np.ascontiguousarray(
        np.ascontiguousarray(np.asarray(inputs["L0_attn_wv"]).T)
        .reshape(KD, 128, 512).transpose(1, 0, 2))
    d["routert"] = np.ascontiguousarray(
        np.ascontiguousarray(np.asarray(inputs["L0_ffn_router_w"]).T)
        .reshape(KD, 128, 16).transpose(1, 0, 2))
    d["shgw"] = np.ascontiguousarray(
        np.asarray(inputs["L0_ffn_sh_gate_w"]).reshape(KD, 128, 1).transpose(1, 0, 2))

    wg = np.asarray(inputs["L0_ffn_w_gate"])
    wu = np.asarray(inputs["L0_ffn_w_up"])
    wd = np.asarray(inputs["L0_ffn_w_down"])
    shg = np.asarray(inputs["L0_ffn_sh_gate"])
    shu = np.asarray(inputs["L0_ffn_sh_up"])
    shd = np.asarray(inputs["L0_ffn_sh_down"])

    def tile_gu(m):
        return m.reshape(KD, 128, IC, 128).transpose(2, 1, 0, 3)

    def tile_dn(m):
        return m.reshape(IC, 128, KD, 128).transpose(2, 1, 0, 3)

    wg_all = np.empty((NE, IC, 128, KD, 128), np.float32)
    wu_all = np.empty((NE, IC, 128, KD, 128), np.float32)
    wd_all = np.empty((NE, KD, 128, IC, 128), np.float32)
    for e in range(16):
        wg_all[e] = tile_gu(wg[e])
        wu_all[e] = tile_gu(wu[e])
        wd_all[e] = tile_dn(wd[e])
    for sx in range(4):
        wg_all[16 + sx] = tile_gu(shg[:, sx * 1408:(sx + 1) * 1408])
        wu_all[16 + sx] = tile_gu(shu[:, sx * 1408:(sx + 1) * 1408])
        wd_all[16 + sx] = tile_dn(shd[sx * 1408:(sx + 1) * 1408])
    d["wg_all"], d["wu_all"], d["wd_all"] = wg_all, wu_all, wd_all

    inv = 10000.0 ** (-np.arange(0, D, 2, dtype=np.float64) / D)
    c2s, s2s = [], []
    for c in range(NCORES):
        pos = np.arange(c * T - W, c * T + T, dtype=np.float64)
        ang = np.outer(pos, inv)
        cosv = np.cos(ang).T.astype(np.float32)
        sinv = np.sin(ang).T.astype(np.float32)
        c2 = np.empty((128, T + W), np.float32)
        s2 = np.empty((128, T + W), np.float32)
        c2[0::2] = cosv
        c2[1::2] = cosv
        s2[0::2] = -sinv
        s2[1::2] = sinv
        c2s.append(c2.astype(BF))
        s2s.append(s2.astype(BF))
    d["c2"], d["s2"] = c2s, s2s

    j = np.arange(1024)[:, None]
    a = np.arange(W)[None, :]
    mR = ((j < W) & (j >= a)) | ((j >= W) & (j - W <= a))
    mA0 = (j >= W) & (j - W <= a)
    d["maskR"] = np.ascontiguousarray(
        mR.astype(BF).reshape(8, 128, W).transpose(1, 0, 2))
    d["maskA0"] = np.ascontiguousarray(
        mA0.astype(BF).reshape(8, 128, W).transpose(1, 0, 2))

    d["iota16"] = np.tile(np.arange(16, dtype=np.float32), (128, 1))

    psw = np.zeros((128, 128), np.float32)
    i2 = np.arange(0, 128, 2)
    psw[i2, i2 + 1] = 1.0
    psw[i2 + 1, i2] = 1.0
    d["pswap"] = psw.astype(BF)
    d["identb"] = np.eye(128).astype(BF)
    d["identf"] = np.eye(128, dtype=np.float32)
    return d



def build_nc():
    nc = bass.Bass("TRN2", target_bir_lowering=False)

    dr = {}
    def din(name, shape, dt=F32):
        dr[name] = nc.dram_tensor(name, list(shape), dt, kind="ExternalInput").ap()
    din("xm", (128, KD, T)); din("xh", (128, KD, W))
    din("wqt", (16, 128, KD, 128)); din("wkt", (4, 128, KD, 128))
    din("wvt", (128, KD, 512)); din("wot", (16, 128, KD, 128))
    din("routert", (128, KD, 16)); din("shgw", (128, KD, 1))
    din("wg_all", (NE, IC, 128, KD, 128)); din("wu_all", (NE, IC, 128, KD, 128))
    din("wd_all", (NE, KD, 128, IC, 128))
    din("c2", (128, T + W), BF16); din("s2", (128, T + W), BF16)
    din("maskR", (128, 8, W), BF16); din("maskA0", (128, 8, W), BF16)
    din("pswap", (128, 128), BF16); din("identb", (128, 128), BF16)
    din("identf", (128, 128))
    y = nc.dram_tensor("y", [T, DM], F32, kind="ExternalOutput").ap()
    din("iota16", (128, 16))
    xg_dram = nc.dram_tensor("xg_scr", [16 * CEXP, DM], BF16, kind="Internal").ap()
    cg_dram = nc.dram_tensor("cg_scr", [16 * CEXP, 2], F32, kind="Internal").ap()
    y_all_dram = nc.dram_tensor("yall_scr", [16 * CEXP, DM], BF16, kind="Internal").ap()

    es = ExitStack()
    P = Prog(nc, es)
    for sname in ["p0", "wq0", "wq1", "wq2", "wg0", "wg1", "wu0", "wu1", "wd0", "wd1",
                  "cr0", "cr1", "cmb", "pe", "act", "dve", "out0", "out1",
                  "xg0", "xg1", "ya", "sca0", "sca1", "yg0", "yg1"]:
        P.mksem(es, sname)
    t = ES_(P, "t")
    v = ES_(P, "v", own="dve")
    s = ES_(P, "s", own="act")
    g = ES_(P, "g")
    yq = ES_(P, "y")
    cnt = P.cnt
    gate = {}

    def sb(stack, name, shape, dt):
        return stack.enter_context(nc.sbuf_tensor(name, list(shape), dt))

    ones_cb = sb(es, "ones_cb", (128, 1), BF16)
    ones_rb = sb(es, "ones_rb", (1, 128), BF16)
    ones_rf = sb(es, "ones_rf", (1, 128), F32)
    identb = sb(es, "identb_t", (128, 128), BF16)
    xsc = ExitStack()
    x_res = sb(xsc, "x_res", (128, KD, T), BF16)

    ps = [es.enter_context(nc.psum_tensor(f"ps{i}", [128, 512], F32))
          for i in range(2)]
    ps_sc = es.enter_context(nc.psum_tensor("ps_sc", [128, 1024], F32))
    ps.append(ps_sc.ap()[:, 0:512])
    ps.append(ps_sc.ap()[:, 512:1024])
    ps += [es.enter_context(nc.psum_tensor(f"ps{i}", [128, 512], F32))
           for i in (4, 5)]
    pst = [es.enter_context(nc.psum_tensor(f"pst{i}", [128, 128], BF16))
           for i in range(2)]

    def dma(ap_out, ap_in, sem):
        return g.op(lambda e, o=ap_out, i=ap_in: e.dma_start(out=o, in_=i),
                    inc=sem, n=16)

    att = ExitStack()
    pswap = sb(att, "pswap_t", (128, 128), BF16)
    wvt_sb = sb(att, "wvt_sb", (128, KD, 512), BF16)
    kT = sb(att, "kT", (128, 2, HK, W), BF16)
    v_tok = sb(att, "v_tok", (128, 2, 4, W), BF16)
    c2_sb = sb(att, "c2_sb", (128, W), BF16)
    s2_sb = sb(att, "s2_sb", (128, W), BF16)
    mask_sb = sb(att, "mask_sb", (128, 8, W), BF16)
    wqo_sb = sb(att, "wqo_sb", (128, 3, KD, 128), BF16)
    qT = sb(att, "qT", (128, 2, W), BF16)
    q_raw = sb(att, "q_raw", (128, 2, W), BF16)
    rtmp1 = sb(att, "rtmp1", (128, W), F32)
    rtmp2 = sb(att, "rtmp2", (128, W), F32)
    expT = sb(att, "expT", (128, 2, 8, W), BF16)
    etmp = sb(att, "etmp", (128, 2, 2 * W), BF16)
    ttmp = sb(att, "ttmp", (128, 2 * W), F32)
    rb_sb = sb(att, "rb_sb", (128, 2, W), F32)
    recip_row = sb(att, "recip_row", (1, 2, W), F32)
    attn_out = sb(att, "attn_out", (128, KD, W), BF16)
    s1_pp = sb(att, "s1_pp", (128, 4, W), BF16)
    sq_pp = sb(att, "sq_pp", (128, 2, W), BF16)
    lnrow = sb(att, "lnrow", (1, 3, W), F32)
    mb_sb = sb(att, "mb_sb", (128, W), F32)
    rb2_sb = sb(att, "rb2_sb", (128, W), F32)

    halo = ExitStack()
    x_halo = sb(halo, "x_halo", (128, KD, W), BF16)

    def x_blk(n):
        return [x_res[:, kd, n * W:(n + 1) * W] for kd in range(KD)]

    wplan = {"aps": [], "next": 0, "cnt": {}}

    def wreset(aps):
        wplan["aps"] = aps
        wplan["next"] = 0
        wplan["cnt"] = {}

    def wpump(upto):
        while wplan["next"] < min(upto, len(wplan["aps"])):
            i = wplan["next"]
            slot = i % 3
            g.wait_gate(gate.get(("wqo", slot)))
            dma(wqo_sb[:, slot], wplan["aps"][i], f"wq{slot}")
            wplan["cnt"][i] = cnt[f"wq{slot}"]
            wplan["next"] += 1

    def wwait(i):
        t.wait(f"wq{i % 3}", wplan["cnt"][i])

    def wdone(i, pe_val):
        gate[("wqo", i % 3)] = ("pe", pe_val)
        wpump(i + 4)

    def kv_block(n, x_src, widx0):
        slot = n % 2
        for kvh in range(HK):
            wi = widx0 + kvh
            sp3 = wi % 3
            wwait(wi)
            t.wait_gate(gate.get("ps0"))
            t.wait_gate(gate.get(("qraw_rd", kvh % 2)))
            for kd in range(KD):
                t.op(lambda e, kd=kd, sp3=sp3, xs=x_src: e.matmul(
                    ps[0][:], wqo_sb[:, sp3, kd, :], xs[kd],
                    start=(kd == 0), stop=(kd == KD - 1), skip_group_check=True),
                    inc="pe" if kd == KD - 1 else None)
            pe_k = cnt["pe"]
            wdone(wi, pe_k)
            pp = kvh % 2
            a_ev = s.do(lambda e, pp=pp: e.copy(out=q_raw[:, pp, :], in_=ps[0][:]),
                        extra_waits=[("pe", pe_k), ("dve", cnt["dve"])])
            gate["ps0"] = ("act", a_ev)
            t.wait("act", a_ev)
            t.wait_gate(gate.get("ps1"))
            t.op(lambda e, pp=pp: e.matmul(ps[1][:], pswap[:], q_raw[:, pp, :],
                                           start=True, stop=True, skip_group_check=True), inc="pe")
            pe_sh = cnt["pe"]
            v.do(lambda e, pp=pp: e.tensor_mul(
                rtmp1[:], q_raw[:, pp, :], c2_sb[:]),
                extra_waits=[("act", a_ev), ("p0", cnt["p0"])])
            d2 = v.do(lambda e: e.tensor_mul(
                rtmp2[:], ps[1][:], s2_sb[:]),
                extra_waits=[("pe", pe_sh)])
            gate["ps1"] = ("dve", d2)
            dk = v.do(lambda e, slot=slot, kvh=kvh: e.tensor_add(
                kT[:, slot, kvh, :], rtmp1[:], rtmp2[:]))
            gate[("qraw_rd", pp)] = ("dve", dk)
        for jc in range(4):
            pp = jc % 2
            if pp == 0:
                t.wait_gate(gate.get("ps_sc"))
            for kd in range(KD):
                t.op(lambda e, kd=kd, jc=jc, pp=pp, xs=x_src: e.matmul(
                    ps[2 + pp][:], xs[kd][:, jc * 128:(jc + 1) * 128],
                    wvt_sb[:, kd, :],
                    start=(kd == 0), stop=(kd == KD - 1), skip_group_check=True),
                    inc="pe" if kd == KD - 1 else None)
            pe_v = cnt["pe"]
            a_v = s.do(lambda e, slot=slot, jc=jc, pp=pp: e.copy(
                out=v_tok[:, slot, jc, :], in_=ps[2 + pp][:]),
                extra_waits=[("pe", pe_v)])
            gate["ps_sc"] = ("act", a_v)

    P.phase("P0")
    for kq in range(4):
        dma(x_res[:, kq * 4:(kq + 1) * 4, :], dr["xm"][:, kq * 4:(kq + 1) * 4, :], "p0")
    dma(x_halo[:], dr["xh"], "p0")
    dma(wvt_sb[:], dr["wvt"], "p0")
    dma(c2_sb[:], dr["c2"][:, 0:W], "p0")
    dma(s2_sb[:], dr["s2"][:, 0:W], "p0")
    dma(mask_sb[:], dr["maskA0"], "p0")
    dma(pswap[:], dr["pswap"], "p0")
    dma(identb[:], dr["identb"], "p0")
    p0_total = cnt["p0"]
    v.do(lambda e: e.memset(ones_cb[:], 1.0))
    v.do(lambda e: e.memset(ones_rf[:], 1.0))
    v.do(lambda e: e.memset(ones_rb[:], 1.0))
    t.wait("p0", p0_total)
    t.wait("dve", cnt["dve"])
    full_plan = [dr["wkt"][kvh] for kvh in range(HK)]
    for _n in range(NB):
        full_plan += [dr["wkt"][kvh] for kvh in range(HK)]
        full_plan += [dr["wqt"][h] for h in range(H)]
        full_plan += [dr["wot"][dc] for dc in range(KD)]
    wreset(full_plan)
    wpump(3)
    kv_block(-1, [x_halo[:, kd, :] for kd in range(KD)], 0)

    P.phase("ATT")
    zrow = x_halo[:, 0:4, :]
    dv_z = v.do(lambda e: e.memset(zrow, 0.0))
    g.wait("dve", dv_z)
    g.op(lambda e: e.dma_start(
        out=cg_dram[:].rearrange("a b -> (a b)").rearrange("(p n) -> p n", p=128),
        in_=zrow[:, 0, 0:16 * CEXP * 2 // 128]), inc="p0", n=16)
    yq.wait("dve", dv_z)
    for zi in range(16 * CEXP // 128):
        yq.op(lambda e, zi=zi: e.dma_start(
            out=xg_dram[zi * 128:(zi + 1) * 128, :],
            in_=zrow.rearrange("p a b -> p (a b)")), inc="out0", n=16)
    zx_total = cnt["out0"]

    def emit_qproj(n, h, xb, wbase0):
        hp = h % 2
        wi = wbase0 + 4 + h
        sp3 = wi % 3
        wwait(wi)
        t.wait_gate(gate.get("ps0"))
        for kd in range(KD):
            t.op(lambda e, kd=kd, sp3=sp3, xb=xb: e.matmul(
                ps[0][:], wqo_sb[:, sp3, kd, :], xb[kd],
                start=(kd == 0), stop=(kd == KD - 1), skip_group_check=True),
                inc="pe" if kd == KD - 1 else None)
        pe_q = cnt["pe"]
        wdone(wi, pe_q)
        a_ev = s.do(lambda e, hp=hp: e.copy(out=q_raw[:, hp, :], in_=ps[0][:]),
                    extra_waits=[("pe", pe_q), ("dve", cnt["dve"])])
        gate["ps0"] = ("act", a_ev)
        t.wait("act", a_ev)
        t.wait_gate(gate.get("ps1"))
        t.op(lambda e, hp=hp: e.matmul(ps[1][:], pswap[:], q_raw[:, hp, :],
                                       start=True, stop=True, skip_group_check=True), inc="pe")
        pe_sh = cnt["pe"]
        v.do(lambda e, hp=hp: e.tensor_mul(
            rtmp1[:], q_raw[:, hp, :], c2_sb[:]),
            extra_waits=[("act", a_ev), ("p0", cnt["p0"])])
        d2 = v.do(lambda e: e.tensor_mul(
            rtmp2[:], ps[1][:], s2_sb[:]),
            extra_waits=[("pe", pe_sh)])
        gate["ps1"] = ("dve", d2)
        dv_q = v.do(lambda e, hp=hp: e.tensor_add(
            qT[:, hp, :], rtmp1[:], rtmp2[:]),
            extra_waits=[gate[("qT", hp)]] if ("qT", hp) in gate else ())
        gate[("qraw_rd", hp)] = ("dve", dv_q)
        return dv_q

    def emit_scores(n, h, dv_q, msk):
        kvh = h // 4
        hp = h % 2
        slot, pslot = n % 2, (n - 1) % 2
        t.wait("dve", dv_q)
        dms = []
        for jp in range(4):
            jc0 = 2 * jp
            pv = jp % 2
            t.wait_gate(gate.get("ps_sc"))
            for half in range(2):
                jc = jc0 + half
                sl = pslot if jc < 4 else slot
                jj = jc % 4
                t.op(lambda e, sl=sl, kvh=kvh, jj=jj, hp=hp, half=half: e.matmul(
                    ps[2 + half][:], kT[:, sl, kvh, jj * 128:(jj + 1) * 128],
                    qT[:, hp, :], start=True, stop=True, skip_group_check=True),
                    inc="pe" if half == 1 else None)
            pe_sc = cnt["pe"]
            a_t = s.do(lambda e: e.activation(
                out=ttmp[:], in_=ps_sc[:], func=AF.Tanh,
                scale=INV_SQRT_D / CAP), extra_waits=[("pe", pe_sc)])
            gate["ps_sc"] = ("act", a_t)
            a_e = s.do(lambda e, pv=pv: e.activation(
                out=etmp[:, pv, :], in_=ttmp[:], func=AF.Exp,
                scale=CAP), extra_waits=[gate.get(("etmp", pv))])
            d_m = v.do(lambda e, jc0=jc0, hp=hp, pv=pv, msk=msk: e.tensor_mul(
                expT[:, hp, jc0:jc0 + 2, :], etmp[:, pv, :].rearrange("p (a b) -> p a b", a=2),
                msk[:, jc0:jc0 + 2, :]),
                extra_waits=[("act", a_e), ("p0", cnt["p0"])] +
                    ([gate[("expT", hp)]] if jp == 0 and ("expT", hp) in gate
                     else []))
            gate[("etmp", pv)] = ("dve", d_m)
            dms.append((jc0, d_m))
            dms.append((jc0 + 1, d_m))
        return dms

    def emit_sumsout(n, h, dms):
        kvh = h // 4
        hp = h % 2
        slot, pslot = n % 2, (n - 1) % 2
        first = True
        for jc, d_m in dms:
            t.wait("dve", d_m)
            t.op(lambda e, jc=jc, hp=hp, st=first: e.matmul(
                ps[4][0:1, :], ones_cb[:], expT[:, hp, jc, :],
                start=st, stop=(jc == 7), skip_group_check=True), inc="pe")
            first = False
        pe_sums = cnt["pe"]
        d_r = v.do(lambda e, hp=hp: e.reciprocal(
            out=recip_row[:, hp, :], in_=ps[4][0:1, :]),
            extra_waits=[("pe", pe_sums)])
        first = True
        for jc, d_m in dms:
            sl = pslot if jc < 4 else slot
            jj = jc % 4
            t.op(lambda e, sl=sl, jj=jj, kvh=kvh, jc=jc, hp=hp, st=first: e.matmul(
                ps[5][:], v_tok[:, sl, jj, kvh * 128:(kvh + 1) * 128],
                expT[:, hp, jc, :], start=st, stop=(jc == 7), skip_group_check=True),
                inc="pe" if jc == 7 else None)
            first = False
        pe_att = cnt["pe"]
        gate[("expT", hp)] = ("pe", pe_att)
        gate[("qT", hp)] = ("pe", pe_att)
        t.wait("dve", d_r)
        t.wait_gate(gate.get("ps0"))
        t.op(lambda e, hp=hp: e.matmul(ps[0][:], ones_rf[:],
                                       recip_row[:, hp, :],
                                       start=True, stop=True, skip_group_check=True), inc="pe")
        pe_rb = cnt["pe"]
        a_rb = s.do(lambda e, hp=hp: e.copy(out=rb_sb[:, hp, :], in_=ps[0][:]),
                    extra_waits=[("pe", pe_rb)])
        gate["ps0"] = ("act", a_rb)
        d_ao = v.do(lambda e, h=h, hp=hp: e.tensor_mul(
            attn_out[:, h, :], ps[5][:], rb_sb[:, hp, :]),
            extra_waits=[("act", a_rb)])
        gate["ps5"] = ("dve", d_ao)
        return pe_att

    for n in range(NB):
        slot, pslot = n % 2, (n - 1) % 2
        tab0 = W + n * W
        xb = x_blk(n)
        g.wait_gate(gate.get("ropetab"))
        dma(c2_sb[:], dr["c2"][:, tab0:tab0 + W], "p0")
        dma(s2_sb[:], dr["s2"][:, tab0:tab0 + W], "p0")
        if n == 1:
            g.wait_gate(gate.get("maskrd"))
            dma(mask_sb[:], dr["maskR"], "p0")
        t.wait_gate(gate.get(("kvslot", slot)))
        wbase = 4 + 36 * n
        kv_block(n, xb, wbase)
        msk = mask_sb

        dv_q = emit_qproj(n, 0, xb, wbase)
        for h in range(H):
            dms = emit_scores(n, h, dv_q, msk)
            if h + 1 < H:
                dv_q = emit_qproj(n, h + 1, xb, wbase)
            pe_att = emit_sumsout(n, h, dms)
        gate[("kvslot", pslot)] = ("pe", pe_att)
        gate["ropetab"] = ("dve", cnt["dve"])
        gate["maskrd"] = ("dve", cnt["dve"])

        dv_attn_done = cnt["dve"]
        lnfifo = []
        first = [True]

        def pop_stats(last=False):
            fdc, fd_s1 = lnfifo.pop(0)
            a_sq = s.do(lambda e, fdc=fdc: e.activation(
                out=sq_pp[:, fdc % 2, :], in_=s1_pp[:, fdc % 4, :], func=AF.Square),
                extra_waits=[("dve", fd_s1), gate.get(("sq", fdc % 2))])
            t.wait("dve", fd_s1)
            if fdc == 0:
                t.wait_gate(gate.get("ps_sc"))
            t.op(lambda e, fdc=fdc, st=first[0]: e.matmul(
                ps[2][0:1, :], ones_cb[:], s1_pp[:, fdc % 4, :],
                start=st, stop=(fdc == KD - 1), skip_group_check=True), inc="pe")
            t.wait("act", a_sq)
            t.op(lambda e, fdc=fdc, st=first[0]: e.matmul(
                ps[3][0:1, :], ones_cb[:], sq_pp[:, fdc % 2, :],
                start=st, stop=(fdc == KD - 1), skip_group_check=True), inc="pe")
            gate[("sq", fdc % 2)] = ("pe", cnt["pe"])
            gate[("s1", fdc % 4)] = ("pe", cnt["pe"])
            first[0] = False

        for dc in range(KD):
            dp = dc % 2
            wi = wbase + 20 + dc
            sp3 = wi % 3
            wwait(wi)
            if dc == 0:
                t.wait("dve", dv_attn_done)
            t.wait_gate(gate.get(f"pso{dp}"))
            for kd in range(KD):
                t.op(lambda e, kd=kd, sp3=sp3, dp=dp: e.matmul(
                    ps[dp][:], wqo_sb[:, sp3, kd, :], attn_out[:, kd, :],
                    start=(kd == 0), stop=(kd == KD - 1), skip_group_check=True),
                    inc="pe" if kd == KD - 1 else None)
            pe_o = cnt["pe"]
            wdone(wi, pe_o)
            a_o = s.do(lambda e, dc=dc, dp=dp: e.copy(
                out=expT[:, dc // 8, dc % 8, :], in_=ps[dp][:]),
                extra_waits=[("pe", pe_o)])
            gate[f"pso{dp}"] = ("act", a_o)
            d_s1 = v.do(lambda e, dc=dc, n=n: e.tensor_add(
                s1_pp[:, dc % 4, :], x_res[:, dc, n * W:(n + 1) * W],
                expT[:, dc // 8, dc % 8, :]),
                extra_waits=[("act", a_o), gate.get(("s1", dc % 4))])
            lnfifo.append((dc, d_s1))
            if len(lnfifo) >= 3:
                pop_stats()
        while lnfifo:
            pop_stats()
        pe_stats = cnt["pe"]
        wpump(wbase + 36 + 3)
        v.do(lambda e: e.tensor_scalar_mul(lnrow[:, 0, :], ps[2][0:1, :], 1.0 / DM),
             extra_waits=[("pe", pe_stats)])
        v.do(lambda e: e.tensor_scalar_mul(lnrow[:, 1, :], ps[3][0:1, :], 1.0 / DM))
        v.do(lambda e: e.tensor_mul(lnrow[:, 2, :], lnrow[:, 0, :], lnrow[:, 0, :]))
        v.do(lambda e: e.tensor_sub(lnrow[:, 1, :], lnrow[:, 1, :],
                                    lnrow[:, 2, :]))
        dv_var = v.do(lambda e: e.tensor_scalar_add(lnrow[:, 1, :],
                                                    lnrow[:, 1, :], EPS))
        gate["ps_sc"] = ("dve", dv_var)
        s.do(lambda e: e.activation(out=lnrow[:, 2, :], in_=lnrow[:, 1, :],
                                    func=AF.Ln),
             extra_waits=[("dve", dv_var)])
        a_rs = s.do(lambda e: e.activation(out=lnrow[:, 1, :], in_=lnrow[:, 2, :],
                                           func=AF.Exp, scale=-0.5))
        t.wait("act", a_rs)
        t.wait_gate(gate.get("ps4"))
        t.op(lambda e: e.matmul(ps[4][:], ones_rf[:], lnrow[:, 0, :],
                                start=True, stop=True, skip_group_check=True), inc="pe")
        t.wait_gate(gate.get("ps5"))
        t.op(lambda e: e.matmul(ps[5][:], ones_rf[:], lnrow[:, 1, :],
                                start=True, stop=True, skip_group_check=True), inc="pe")
        pe_bc = cnt["pe"]
        a_mb = s.do(lambda e: e.copy(out=mb_sb[:], in_=ps[4][:]),
                    extra_waits=[("pe", pe_bc)])
        a_rb2 = s.do(lambda e: e.copy(out=rb2_sb[:], in_=ps[5][:]))
        gate["ps4"] = ("act", a_mb)
        gate["ps5"] = ("act", a_rb2)
        for dc in range(KD):
            v.do(lambda e, dc=dc, n=n: e.tensor_add(
                s1_pp[:, dc % 4, :], x_res[:, dc, n * W:(n + 1) * W],
                expT[:, dc // 8, dc % 8, :]),
                extra_waits=[("act", a_rb2)] if dc == 0 else ())
            v.do(lambda e, dc=dc: e.tensor_sub(ttmp[:, 0:512], s1_pp[:, dc % 4, :], mb_sb[:]))
            v.do(lambda e, dc=dc, n=n: e.tensor_mul(
                x_res[:, dc, n * W:(n + 1) * W], ttmp[:, 0:512], rb2_sb[:]))

    halo.close()
    att.close()

    R = "right"
    def sbr_(stack, name, shape, dt):
        return stack.enter_context(nc.sbuf_tensor(name, list(shape), dt, side=R))

    ffn = ExitStack()
    racc = sbr_(ffn, "racc", (128, KD, T), BF16)
    sg_row = sbr_(ffn, "sg_row", (1, T), BF16)
    sidx = sbr_(ffn, "sidx", (128, 16, 4), mybir.dt.int32)
    wg_sb = sbr_(ffn, "wg_sb", (128, 2, KD, 128), BF16)
    wu_sb = sbr_(ffn, "wu_sb", (128, 2, KD, 128), BF16)
    wd_sb = sbr_(ffn, "wd_sb", (128, 2, IC, 128), BF16)

    rt = ExitStack()
    identf = sbr_(rt, "identf_t", (128, 128), F32)
    routert_sb = sbr_(rt, "routert_sb", (128, KD, 16), BF16)
    shgw_sb = sbr_(rt, "shgw_sb", (128, KD, 1), BF16)
    iota16_sb = sbr_(rt, "iota16_sb", (128, 16), F32)
    lg_sb = sbr_(rt, "lg_sb", (16, 4, 512), F32)
    ltok = sbr_(rt, "ltok", (128, 16, 16), F32)
    ptmp = sbr_(rt, "ptmp", (128, 16, 16), F32)
    selacc = sbr_(rt, "selacc", (128, 16, 16), F32)
    maskt = sbr_(rt, "maskt", (128, 16, 16), F32)
    comb_tok = sbr_(rt, "comb_tok", (128, 16, 16), F32)
    mx16 = sbr_(rt, "mx16", (128, 16), F32)
    ts16 = sbr_(rt, "ts16", (128, 16), F32)
    comb_d = sbr_(rt, "comb_d", (16, T), F32)
    maskr4 = sbr_(rt, "maskr4", (128, 16, 16, 4), F32)
    eidr4 = sbr_(rt, "eidr4", (128, 16, 4), F32)
    valr4 = sbr_(rt, "valr4", (128, 16, 4), F32)
    sidxf = sbr_(rt, "sidxf", (128, 16, 4), F32)
    vst2 = sbr_(rt, "vst2", (128, 16, 4, 2), F32)
    postok = sbr_(rt, "postok", (128, 16, 16), F32)
    cum16 = sbr_(rt, "cum16", (16, T), F32)
    tmp1616 = sbr_(rt, "tmp1616", (128, 16, 16), F32)
    x1tok = sbr_(rt, "x1tok", (128, 2, DM), BF16)

    P.phase("R")
    t.wait("out0", zx_total)
    s.wait("out0", zx_total)
    v.wait("out0", zx_total)
    dma(identf[:], dr["identf"], "p0")
    dma(routert_sb[:], dr["routert"], "p0")
    dma(shgw_sb[:], dr["shgw"], "p0")
    dma(iota16_sb[:], dr["iota16"], "p0")
    p0r = cnt["p0"]
    dv_ln_done = cnt["dve"]
    for kq in range(4):
        s.do(lambda e, kq=kq: e.copy(out=racc[:, kq * 4:(kq + 1) * 4, :],
                                     in_=x_res[:, kq * 4:(kq + 1) * 4, :]),
             extra_waits=[("dve", dv_ln_done)] if kq == 0 else ())
    t.wait("p0", p0r)
    for tq in range(4):
        t.wait_gate(gate.get("ps0"))
        for kd in range(KD):
            t.op(lambda e, kd=kd, tq=tq: e.matmul(
                ps[0][0:16, :], routert_sb[:, kd, :],
                x_res[:, kd, tq * 512:(tq + 1) * 512],
                start=(kd == 0), stop=(kd == KD - 1), skip_group_check=True),
                inc="pe" if kd == KD - 1 else None)
        pe_lg = cnt["pe"]
        a_lg = s.do(lambda e, tq=tq: e.copy(out=lg_sb[:, tq, :], in_=ps[0][0:16, :]),
                    extra_waits=[("pe", pe_lg)])
        gate["ps0"] = ("act", a_lg)
        t.wait_gate(gate.get("ps_sc"))
        for kd in range(KD):
            t.op(lambda e, kd=kd, tq=tq: e.matmul(
                ps[3][0:1, :], shgw_sb[:, kd, :],
                x_res[:, kd, tq * 512:(tq + 1) * 512],
                start=(kd == 0), stop=(kd == KD - 1), skip_group_check=True),
                inc="pe" if kd == KD - 1 else None)
        pe_sg = cnt["pe"]
        a_sg = s.do(lambda e, tq=tq: e.activation(
            out=sg_row[:, tq * 512:(tq + 1) * 512], in_=ps[3][0:1, :],
            func=AF.Sigmoid), extra_waits=[("pe", pe_sg)])
        gate["ps_sc"] = ("act", a_sg)
    for tc in range(16):
        t.wait_gate(gate.get("ps1"))
        t.op(lambda e, tc=tc: e.transpose(
            ps[1][:, 0:16], lg_sb[:, tc // 4, (tc % 4) * 128:(tc % 4 + 1) * 128],
            identf[0:16, 0:16]), inc="pe")
        pe_tr = cnt["pe"]
        a_ltk = s.do(lambda e, tc=tc: e.copy(out=ltok[:, tc, :], in_=ps[1][:, 0:16]),
                     extra_waits=[("pe", pe_tr)])
        gate["ps1"] = ("act", a_ltk)
    a_lt = cnt["act"]
    bc16 = lambda ap: ap.unsqueeze(2).broadcast_to([128, 16, 16])
    v.do(lambda e: e.tensor_reduce(mx16[:], ltok[:], AX, ALU.max),
         extra_waits=[("act", a_lt)])
    v.do(lambda e: e.tensor_tensor(ptmp[:], ltok[:], bc16(mx16[:]), ALU.subtract))
    a_ex = s.do(lambda e: e.activation(out=ptmp[:], in_=ptmp[:], func=AF.Exp),
                extra_waits=[("dve", cnt["dve"])])
    v.do(lambda e: e.tensor_reduce(mx16[:], ptmp[:], AX, ALU.add),
         extra_waits=[("act", a_ex)])
    v.do(lambda e: e.reciprocal(out=ts16[:], in_=mx16[:]))
    v.do(lambda e: e.tensor_tensor(ptmp[:], ptmp[:], bc16(ts16[:]), ALU.mult))
    v.do(lambda e: e.memset(selacc[:], 0.0))
    v.do(lambda e: e.memset(ts16[:], 0.0))
    for r in range(4):
        v.do(lambda e: e.tensor_reduce(mx16[:], ptmp[:], AX, ALU.max))
        v.do(lambda e: e.tensor_tensor(maskt[:], ptmp[:], bc16(mx16[:]), ALU.is_ge))
        v.do(lambda e, r=r: e.tensor_copy(out=maskr4[:, :, :, r], in_=maskt[:]))
        v.do(lambda e: e.tensor_tensor(
            tmp1616[:], maskt[:], iota16_sb[:].unsqueeze(1).broadcast_to([128, 16, 16]),
            ALU.mult))
        v.do(lambda e, r=r: e.tensor_reduce(eidr4[:, :, r], tmp1616[:], AX, ALU.add))
        v.do(lambda e, r=r: e.tensor_copy(out=valr4[:, :, r], in_=mx16[:]))
        v.do(lambda e: e.tensor_tensor(maskt[:], maskt[:], ptmp[:], ALU.mult))
        v.do(lambda e: e.tensor_add(selacc[:], selacc[:], maskt[:]))
        v.do(lambda e: e.tensor_add(ts16[:], ts16[:], mx16[:]))
        if r < 3:
            v.do(lambda e: e.tensor_sub(ptmp[:], ptmp[:], maskt[:]))
    v.do(lambda e: e.tensor_scalar_max(ts16[:], ts16[:], 1e-9))
    v.do(lambda e: e.reciprocal(out=ts16[:], in_=ts16[:]))
    for r in range(4):
        v.do(lambda e, r=r: e.tensor_tensor(valr4[:, :, r], valr4[:, :, r],
                                            ts16[:], ALU.mult))
    dv_cmb = v.do(lambda e: e.tensor_tensor(comb_tok[:], selacc[:],
                                            bc16(ts16[:]), ALU.mult))
    t.wait("dve", dv_cmb)
    for tc in range(16):
        t.wait_gate(gate.get("ps_sc"))
        t.op(lambda e, tc=tc: e.transpose(
            ps[2][0:16, 0:128], comb_tok[:, tc, :], identf[:]), inc="pe")
        pe_ct = cnt["pe"]
        a_cd = s.do(lambda e, tc=tc: e.copy(out=comb_d[:, tc * 128:(tc + 1) * 128],
                                            in_=ps[2][0:16, 0:128]),
                    extra_waits=[("pe", pe_ct)])
        gate["ps_sc"] = ("act", a_cd)
    a_cdall = cnt["act"]
    lg_flat = lg_sb[:].rearrange("e a b -> e (a b)")
    v.do(lambda e: e.tensor_single_scalar(out=comb_d[:], in_=comb_d[:], scalar=0.0,
                                          op=ALU.is_gt),
         extra_waits=[("act", a_cdall)])
    v.do(lambda e: e.tensor_tensor_scan(out=cum16[:], data0=comb_d[:],
                                        data1=comb_d[:], initial=0.0,
                                        op0=ALU.add, op1=ALU.max))
    v.do(lambda e: e.tensor_scalar_add(cum16[:], cum16[:], -1.0))
    v.do(lambda e: e.tensor_single_scalar(out=lg_flat, in_=cum16[:],
                                          scalar=float(CEXP), op=ALU.is_lt))
    v.do(lambda e: e.tensor_mul(lg_flat, lg_flat, comb_d[:]))
    v.do(lambda e: e.tensor_scalar(out=comb_d[:], in0=lg_flat, scalar1=-1.0,
                                   scalar2=-1.0e9, op0=ALU.add, op1=ALU.mult))
    v.do(lambda e: e.tensor_mul(cum16[:], cum16[:], lg_flat))
    dv_pos = v.do(lambda e: e.tensor_add(cum16[:], cum16[:], comb_d[:]))
    t.wait("dve", dv_pos)
    for tc in range(16):
        t.wait_gate(gate.get("ps1"))
        t.op(lambda e, tc=tc: e.transpose(
            ps[1][:, 0:16], cum16[0:16, tc * 128:(tc + 1) * 128],
            identf[0:16, 0:16]), inc="pe")
        pe_pt = cnt["pe"]
        a_pt = s.do(lambda e, tc=tc: e.copy(out=postok[:, tc, :], in_=ps[1][:, 0:16]),
                    extra_waits=[("pe", pe_pt)])
        gate["ps1"] = ("act", a_pt)
    a_ptall = cnt["act"]
    v.wait("act", a_ptall)
    for r in range(4):
        v.do(lambda e, r=r: e.tensor_tensor(
            tmp1616[:], maskr4[:, :, :, r], postok[:], ALU.mult))
        v.do(lambda e, r=r: e.tensor_reduce(sidxf[:, :, r], tmp1616[:], AX, ALU.add))
        v.do(lambda e, r=r: e.scalar_tensor_tensor(
            out=sidxf[:, :, r], in0=eidr4[:, :, r], scalar=float(CEXP),
            in1=sidxf[:, :, r], op0=ALU.mult, op1=ALU.add))
    dv_sidx = v.do(lambda e: e.tensor_copy(out=sidx[:], in_=sidxf[:]))
    v.do(lambda e: e.tensor_copy(out=vst2[:, :, :, 0], in_=valr4[:]))
    dv_vst = v.do(lambda e: e.tensor_copy(out=vst2[:, :, :, 1], in_=valr4[:]))

    g.wait("dve", dv_vst)
    g.wait("out0", zx_total)
    regbox = {}
    g.op(lambda e: regbox.__setitem__("b", e.to_reg(16 * CEXP - 1)) or e.engine_nop())
    for tc in range(16):
        yp = tc % 2
        for dc in range(KD):
            tp = dc % 2
            t.wait_gate(gate.get(("pst", tp)))
            t.op(lambda e, dc=dc, tc=tc, tp=tp: e.transpose(
                pst[tp][:], x_res[:, dc, tc * 128:(tc + 1) * 128], identb[:]),
                inc="pe")
            pe_t2 = cnt["pe"]
            if dc % 2 == 0:
                a_x = s.do(lambda e, dc=dc, tp=tp, yp=yp: e.copy(
                    out=x1tok[:, yp, dc * 128:(dc + 1) * 128], in_=pst[tp][:]),
                    extra_waits=[("pe", pe_t2)] +
                        ([gate[("x1tok", yp)]] if dc == 0 and ("x1tok", yp) in gate
                         else []))
                gate[("pst", tp)] = ("act", a_x)
            else:
                d_x = v.do(lambda e, dc=dc, tp=tp, yp=yp: e.tensor_copy(
                    out=x1tok[:, yp, dc * 128:(dc + 1) * 128], in_=pst[tp][:]),
                    extra_waits=[("pe", pe_t2)] +
                        ([gate[("x1tok", yp)]] if dc == 1 and ("x1tok", yp) in gate
                         else []))
                gate[("pst", tp)] = ("dve", d_x)
        g.wait("act", cnt["act"])
        g.wait("dve", cnt["dve"])
        ssem = f"sca{yp}"
        for r in range(4):
            g.op(lambda e, tc=tc, r=r, yp=yp: e.indirect_dma_start(
                out=xg_dram,
                out_offset=IndirectOffsetOnAxis(sidx[:, tc, r:r + 1], 0),
                in_=x1tok[:, yp, :], in_offset=None,
                bounds_check=regbox["b"], oob_is_err=False),
                inc=ssem, n=16)
            g.op(lambda e, tc=tc, r=r: e.indirect_dma_start(
                out=cg_dram,
                out_offset=IndirectOffsetOnAxis(sidx[:, tc, r:r + 1], 0),
                in_=vst2[:, tc, r, :], in_offset=None,
                bounds_check=regbox["b"], oob_is_err=False),
                inc=ssem, n=16)
        gate[("x1tok", yp)] = (ssem, cnt[ssem])
    sca_tot0, sca_tot1 = cnt["sca0"], cnt["sca1"]

    rt.close()

    rt2 = ExitStack()
    act_sh = sbr_(rt2, "act_sh", (128, IC, T), BF16)
    sgb = sbr_(rt2, "sgb", (128, T), BF16)
    silsh = sbr_(rt2, "silsh", (128, 2, 512), BF16)

    P.phase("F")
    t.wait("sca0", sca_tot0); t.wait("sca1", sca_tot1)
    s.wait("sca0", sca_tot0); s.wait("sca1", sca_tot1)
    v.wait("sca0", sca_tot0); v.wait("sca1", sca_tot1)
    for e_i in range(16, NE):
        src_row = lambda tq: sg_row[0:1, tq * 512:(tq + 1) * 512]
        t.wait("dve", cnt["dve"])
        if e_i == 16:
            t.wait_gate(gate.get("ps_sc"))
        for tq in range(4):
            cp = tq % 2
            t.wait_gate(gate.get(f"ps{cp}"))
            t.op(lambda e, tq=tq, cp=cp, src=src_row: e.matmul(
                ps[cp][:], ones_rb[:], src(tq), start=True, stop=True, skip_group_check=True), inc="pe")
            pe_cb = cnt["pe"]
            a_cb = s.do(lambda e, tq=tq, cp=cp: e.copy(
                out=sgb[:, tq * 512:(tq + 1) * 512], in_=ps[cp][:]),
                extra_waits=[("pe", pe_cb)])
            gate[f"ps{cp}"] = ("act", a_cb)
        a_cbl = cnt["act"]
        for i in range(IC):
            ip = i % 2
            gsem, usem = f"wg{ip}", f"wu{ip}"
            g.wait_gate(gate.get(("wg", ip)))
            dma(wg_sb[:, ip], dr["wg_all"][e_i, i], gsem)
            g.wait_gate(gate.get(("wu", ip)))
            dma(wu_sb[:, ip], dr["wu_all"][e_i, i], usem)
            g_done, u_done = cnt[gsem], cnt[usem]
            t.wait(gsem, g_done)
            t.wait(usem, u_done)
            for tq in range(4):
                tp = tq % 2
                t.wait_gate(gate.get(f"ps{tp}"))
                if i == 0 and tq < 2:
                    t.wait("act", a_cbl)
                for kd in range(KD):
                    t.op(lambda e, kd=kd, ip=ip, tq=tq, tp=tp: e.matmul(
                        ps[tp][:], wg_sb[:, ip, kd, :],
                        x_res[:, kd, tq * 512:(tq + 1) * 512],
                        start=(kd == 0), stop=(kd == KD - 1), skip_group_check=True),
                        inc="pe" if kd == KD - 1 else None)
                pe_g = cnt["pe"]
                t.wait_gate(gate.get(("psu", tp)))
                for kd in range(KD):
                    t.op(lambda e, kd=kd, ip=ip, tq=tq, tp=tp: e.matmul(
                        ps[2 + tp][:], wu_sb[:, ip, kd, :],
                        x_res[:, kd, tq * 512:(tq + 1) * 512],
                        start=(kd == 0), stop=(kd == KD - 1), skip_group_check=True),
                        inc="pe" if kd == KD - 1 else None)
                pe_u = cnt["pe"]
                a_si = s.do(lambda e, tp=tp: e.activation(
                    out=silsh[:, tp, :], in_=ps[tp][:], func=AF.Silu),
                    extra_waits=[("pe", pe_g), gate.get(("sil", tp))])
                gate[f"ps{tp}"] = ("act", a_si)
                d_u = v.do(lambda e, tp=tp: e.tensor_mul(
                    silsh[:, tp, :], silsh[:, tp, :], ps[2 + tp][:]),
                    extra_waits=[("act", a_si), ("pe", pe_u)])
                gate[("psu", tp)] = ("dve", d_u)
                d_sc = v.do(lambda e, i=i, tq=tq, tp=tp: e.tensor_mul(
                    act_sh[:, i, tq * 512:(tq + 1) * 512], silsh[:, tp, :],
                    sgb[:, tq * 512:(tq + 1) * 512]),
                    extra_waits=[("act", a_cbl)] if i == 0 else ())
                gate[("sil", tp)] = ("dve", d_sc)
            gate[("wg", ip)] = ("pe", pe_g)
            gate[("wu", ip)] = ("pe", pe_u)
        dv_A = cnt["dve"]
        first_b = True
        for dc in range(KD):
            dp = dc % 2
            dsem = f"wd{dp}"
            g.wait_gate(gate.get(("wd", dp)))
            dma(wd_sb[:, dp], dr["wd_all"][e_i, dc], dsem)
            d_done = cnt[dsem]
            t.wait(dsem, d_done)
            if first_b:
                t.wait("dve", dv_A)
                first_b = False
            for tq in range(4):
                yp2 = 4 + ((dc * 4 + tq) % 2)
                t.wait_gate(gate.get(f"ps{yp2}"))
                for i in range(IC):
                    t.op(lambda e, i=i, dp=dp, tq=tq, yp2=yp2: e.matmul(
                        ps[yp2][:], wd_sb[:, dp, i, :],
                        act_sh[:, i, tq * 512:(tq + 1) * 512],
                        start=(i == 0), stop=(i == IC - 1), skip_group_check=True),
                        inc="pe" if i == IC - 1 else None)
                pe_y = cnt["pe"]
                d_y = v.do(lambda e, dc=dc, tq=tq, yp2=yp2: e.tensor_add(
                    racc[:, dc, tq * 512:(tq + 1) * 512],
                    racc[:, dc, tq * 512:(tq + 1) * 512], ps[yp2][:]),
                    extra_waits=[("pe", pe_y)])
                gate[f"ps{yp2}"] = ("dve", d_y)
            gate[("wd", dp)] = ("pe", pe_y)

    rt2.close()
    xsc.close()

    fs = ExitStack()
    act_sp = sbr_(fs, "act_sp", (128, IC, CEXP), BF16)
    xgT = sbr_(fs, "xgT", (128, 2, KD, CEXP), BF16)
    ytk = sbr_(fs, "ytk", (128, CEXP // 128, DM), BF16)
    cgb = sbr_(fs, "cgb", (128, CEXP), BF16)
    cgrow = sbr_(fs, "cgrow", (1, 2, 2 * CEXP), F32)
    silsp = sbr_(fs, "silsp", (128, 2, 512), BF16)
    ystage = sbr_(fs, "ystage", (128, 4, 128), BF16)

    CCH = [(0, 512), (512, CEXP - 512)]
    NCC = CEXP // 128

    P.phase("FS")
    yq.wait("sca0", sca_tot0)
    yq.wait("sca1", sca_tot1)

    def emit_xg_loads(e_i):
        xp = e_i % 2
        xsem = f"xg{xp}"
        yq.wait_gate(gate.get(("xgT", xp)))
        for kd in range(KD):
            yq.op(lambda e, e_i=e_i, kd=kd, xp=xp: e.dma_start_transpose(
                out=xgT[:, xp, kd, :],
                in_=xg_dram[e_i * CEXP:(e_i + 1) * CEXP, kd * 128:(kd + 1) * 128]),
                inc=xsem, n=16)
        yq.op(lambda e, e_i=e_i, xp=xp: e.dma_start(
            out=cgrow[:, xp, :], in_=cg_dram[e_i * CEXP:(e_i + 1) * CEXP, :]
            .rearrange("c o -> (c o)").unsqueeze(0)), inc=xsem, n=16)
        return cnt[xsem]

    xg_ready = {0: emit_xg_loads(0)}
    xg_ready[1] = emit_xg_loads(1)
    for e_i in range(16):
        xp = e_i % 2
        xsem = f"xg{xp}"
        cg_done = xg_ready[e_i]
        t.wait(xsem, cg_done)
        t.wait("dve", cnt["dve"])
        for ci, (c0, cw) in enumerate(CCH):
            cp = ci % 2
            t.wait_gate(gate.get(f"ps{cp}"))
            t.op(lambda e, c0=c0, cw=cw, cp=cp, xp=xp: e.matmul(
                ps[cp][:, 0:cw],
                ones_rf[:],
                cgrow[:, xp, :].rearrange("o (c two) -> o c two", two=2)[:, c0:c0 + cw, 0],
                start=True, stop=True, skip_group_check=True), inc="pe")
            pe_cb = cnt["pe"]
            a_cb = s.do(lambda e, c0=c0, cw=cw, cp=cp: e.copy(
                out=cgb[:, c0:c0 + cw], in_=ps[cp][:, 0:cw]),
                extra_waits=[("pe", pe_cb)])
            gate[f"ps{cp}"] = ("act", a_cb)
        a_cbl = cnt["act"]
        for i in range(IC):
            ip = i % 2
            gsem, usem = f"wg{ip}", f"wu{ip}"
            g.wait_gate(gate.get(("wg", ip)))
            dma(wg_sb[:, ip], dr["wg_all"][e_i, i], gsem)
            g.wait_gate(gate.get(("wu", ip)))
            dma(wu_sb[:, ip], dr["wu_all"][e_i, i], usem)
            g_done, u_done = cnt[gsem], cnt[usem]
            t.wait(gsem, g_done)
            t.wait(usem, u_done)
            if i == 0:
                t.wait(xsem, cg_done)
            if i == 0:
                t.wait(xsem, cg_done)
            for ci, (c0, cw) in enumerate(CCH):
                tp = ci % 2
                t.wait_gate(gate.get(f"ps{tp}"))
                if i == 0:
                    t.wait("act", a_cbl)
                for kd in range(KD):
                    t.op(lambda e, kd=kd, ip=ip, c0=c0, cw=cw, tp=tp, xp=xp: e.matmul(
                        ps[tp][:, 0:cw], wg_sb[:, ip, kd, :],
                        xgT[:, xp, kd, c0:c0 + cw],
                        start=(kd == 0), stop=(kd == KD - 1), skip_group_check=True),
                        inc="pe" if kd == KD - 1 else None)
                pe_g = cnt["pe"]
                t.wait_gate(gate.get(("psu", tp)))
                for kd in range(KD):
                    t.op(lambda e, kd=kd, ip=ip, c0=c0, cw=cw, tp=tp, xp=xp: e.matmul(
                        ps[2 + tp][:, 0:cw], wu_sb[:, ip, kd, :],
                        xgT[:, xp, kd, c0:c0 + cw],
                        start=(kd == 0), stop=(kd == KD - 1), skip_group_check=True),
                        inc="pe" if kd == KD - 1 else None)
                pe_u = cnt["pe"]
                a_si = s.do(lambda e, cw=cw, tp=tp: e.activation(
                    out=silsp[:, tp, 0:cw], in_=ps[tp][:, 0:cw], func=AF.Silu),
                    extra_waits=[("pe", pe_g), gate.get(("sil", tp))])
                gate[f"ps{tp}"] = ("act", a_si)
                d_u = v.do(lambda e, cw=cw, tp=tp: e.tensor_mul(
                    silsp[:, tp, 0:cw], silsp[:, tp, 0:cw], ps[2 + tp][:, 0:cw]),
                    extra_waits=[("act", a_si), ("pe", pe_u)])
                gate[("psu", tp)] = ("dve", d_u)
                d_sc = v.do(lambda e, i=i, c0=c0, cw=cw, tp=tp: e.tensor_mul(
                    act_sp[:, i, c0:c0 + cw], silsp[:, tp, 0:cw],
                    cgb[:, c0:c0 + cw]))
                gate[("sil", tp)] = ("dve", d_sc)
            gate[("wg", ip)] = ("pe", pe_g)
            gate[("wu", ip)] = ("pe", pe_u)
        gate[("xgT", xp)] = ("pe", cnt["pe"])
        if e_i + 2 < 16:
            xg_ready[e_i + 2] = emit_xg_loads(e_i + 2)
        dv_A = cnt["dve"]
        first_b = True
        tfifo = []

        def pop_transpose():
            kdc, kcc, ka_ys, kslot = tfifo.pop(0)
            tp2 = kslot % 2
            t.wait("act", ka_ys)
            t.wait_gate(gate.get(("pst", tp2)))
            t.op(lambda e, kslot=kslot, tp2=tp2: e.transpose(
                pst[tp2][:], ystage[:, kslot, :], identb[:]), inc="pe")
            pe_tt = cnt["pe"]
            gate[("yst", kslot)] = ("pe", pe_tt)
            d_yt = v.do(lambda e, kdc=kdc, kcc=kcc, tp2=tp2: e.tensor_copy(
                out=ytk[:, kcc, kdc * 128:(kdc + 1) * 128], in_=pst[tp2][:]),
                extra_waits=[("pe", pe_tt)] +
                    ([("ya", gate.get("ytk_rd", (None, 0))[1])]
                     if kdc == 0 and kcc < 2 and "ytk_rd" in gate else []))
            gate[("pst", tp2)] = ("dve", d_yt)

        kidx = 0
        for dc in range(KD):
            dp = dc % 2
            dsem = f"wd{dp}"
            g.wait_gate(gate.get(("wd", dp)))
            dma(wd_sb[:, dp], dr["wd_all"][e_i, dc], dsem)
            d_done = cnt[dsem]
            t.wait(dsem, d_done)
            if first_b:
                t.wait("dve", dv_A)
                first_b = False
            for cc in range(NCC):
                yp2 = 4 + (cc % 2)
                kslot = kidx % 4
                kidx += 1
                t.wait_gate(gate.get(f"ps{yp2}"))
                for i in range(IC):
                    t.op(lambda e, i=i, dp=dp, cc=cc, yp2=yp2: e.matmul(
                        ps[yp2][:, 0:128], wd_sb[:, dp, i, :],
                        act_sp[:, i, cc * 128:(cc + 1) * 128],
                        start=(i == 0), stop=(i == IC - 1), skip_group_check=True),
                        inc="pe" if i == IC - 1 else None)
                pe_y = cnt["pe"]
                a_ys = s.do(lambda e, kslot=kslot, yp2=yp2: e.copy(
                    out=ystage[:, kslot, :], in_=ps[yp2][:, 0:128]),
                    extra_waits=[("pe", pe_y), gate.get(("yst", kslot))])
                gate[f"ps{yp2}"] = ("act", a_ys)
                tfifo.append((dc, cc, a_ys, kslot))
                if len(tfifo) >= 3:
                    pop_transpose()
            gate[("wd", dp)] = ("pe", pe_y)
        while tfifo:
            pop_transpose()
        dv_ytk = cnt["dve"]
        yq.wait("dve", dv_ytk)
        for cc in range(NCC):
            yq.op(lambda e, e_i=e_i, cc=cc: e.dma_start(
                out=y_all_dram[e_i * CEXP + cc * 128:e_i * CEXP + (cc + 1) * 128, :],
                in_=ytk[:, cc, :]), inc="ya", n=16)
        gate["ytk_rd"] = ("ya", cnt["ya"])
    ya_total = cnt["ya"]

    fs.close()

    epi = ExitStack()
    ytok = sbr_(epi, "ytok", (128, 2, DM), BF16)
    ysq = sbr_(epi, "ysq", (128, DM), BF16)
    yout = sbr_(epi, "yout", (128, 2, DM), F32)
    cols = sbr_(epi, "cols", (128, 8), F32)
    yr = sbr_(epi, "yr", (128, 2, 4, DM), BF16)

    P.phase("EPI")
    t.wait("ya", ya_total)
    s.wait("ya", ya_total)
    v.wait("ya", ya_total)
    regbox2 = {}
    g.op(lambda e: regbox2.__setitem__("b", e.to_reg(16 * CEXP - 1)) or e.engine_nop())
    g.wait("ya", ya_total)
    dv_racc_done = cnt["dve"]
    t.wait("dve", dv_racc_done)
    last_dv_y = 0
    last_comb = {}

    def emit_transposes(tc):
        yp = tc % 2
        g.wait_gate(gate.get(("yr", yp)))
        for r in range(4):
            g.op(lambda e, tc=tc, r=r, yp=yp: e.indirect_dma_start(
                out=yr[:, yp, r, :], out_offset=None,
                in_=y_all_dram,
                in_offset=IndirectOffsetOnAxis(sidx[:, tc, r:r + 1], 0),
                bounds_check=regbox2["b"], oob_is_err=False),
                inc=f"yg{yp}", n=16)
        yg_done = (f"yg{yp}", cnt[f"yg{yp}"])
        for dc in range(KD):
            tp = dc % 2
            t.wait_gate(gate.get(("pst", tp)))
            t.op(lambda e, dc=dc, tc=tc, tp=tp: e.transpose(
                pst[tp][:], racc[:, dc, tc * 128:(tc + 1) * 128], identb[:]),
                inc="pe")
            pe_t = cnt["pe"]
            if dc % 2 == 0:
                a_yt = s.do(lambda e, dc=dc, tp=tp, yp=yp: e.copy(
                    out=ytok[:, yp, dc * 128:(dc + 1) * 128], in_=pst[tp][:]),
                    extra_waits=[("pe", pe_t)] +
                        ([("dve", last_dv_y)] if dc == 0 and tc > 1 else []))
                gate[("pst", tp)] = ("act", a_yt)
            else:
                d_yt = v.do(lambda e, dc=dc, tp=tp, yp=yp: e.tensor_copy(
                    out=ytok[:, yp, dc * 128:(dc + 1) * 128], in_=pst[tp][:]),
                    extra_waits=[("pe", pe_t)])
                gate[("pst", tp)] = ("dve", d_yt)
        return cnt["act"], cnt["dve"], yg_done

    def emit_stats(tc, a_ytl, d_ytl, yg_done):
        yp = tc % 2
        op = tc % 2
        osem = f"out{op}"
        v.do(lambda e, yp=yp: e.tensor_add(ysq[:], yr[:, yp, 0, :], yr[:, yp, 1, :]),
             extra_waits=[yg_done, ("act", a_ytl)])
        v.do(lambda e, yp=yp: e.tensor_add(ytok[:, yp, :], ytok[:, yp, :], ysq[:]))
        v.do(lambda e, yp=yp: e.tensor_add(ysq[:], yr[:, yp, 2, :], yr[:, yp, 3, :]))
        dv_comb = v.do(lambda e, yp=yp: e.tensor_add(ytok[:, yp, :], ytok[:, yp, :],
                                                     ysq[:]))
        gate[("yr", yp)] = ("dve", dv_comb)
        v.do(lambda e, yp=yp: e.tensor_reduce(cols[:, 0:1], ytok[:, yp, :], AX,
                                              ALU.add))
        v.do(lambda e: e.tensor_scalar_mul(cols[:, 0:1], cols[:, 0:1], 1.0 / DM))
        a_q = s.do(lambda e, yp=yp: e.activation(out=ysq[:], in_=ytok[:, yp, :],
                                                 func=AF.Square),
                   extra_waits=[("dve", cnt["dve"])])
        v.do(lambda e: e.tensor_reduce(cols[:, 1:2], ysq[:], AX, ALU.add),
             extra_waits=[("act", a_q)])
        v.do(lambda e: e.tensor_scalar_mul(cols[:, 1:2], cols[:, 1:2], 1.0 / DM))
        v.do(lambda e: e.tensor_mul(cols[:, 2:3], cols[:, 0:1], cols[:, 0:1]))
        v.do(lambda e: e.tensor_sub(cols[:, 3:4], cols[:, 1:2], cols[:, 2:3]))
        dv_v = v.do(lambda e: e.tensor_scalar_add(cols[:, 3:4], cols[:, 3:4], EPS))
        s.do(lambda e: e.activation(out=cols[:, 4:5], in_=cols[:, 3:4],
                                    func=AF.Ln),
             extra_waits=[("dve", dv_v)])
        a_r = s.do(lambda e: e.activation(out=cols[:, 5:6], in_=cols[:, 4:5],
                                          func=AF.Exp, scale=-0.5))
        dv_y = v.do(lambda e, op=op, yp=yp: e.tensor_scalar(
            out=yout[:, op, :], in0=ytok[:, yp, :], scalar1=cols[:, 0:1],
            scalar2=cols[:, 5:6], op0=ALU.subtract, op1=ALU.mult),
            extra_waits=[("act", a_r), (osem, cnt[osem])])
        yq.wait("dve", dv_y)
        yq.op(lambda e, tc=tc, op=op: e.dma_start(
            out=y[tc * 128:(tc + 1) * 128, :], in_=yout[:, op, :]),
            inc=osem, n=16)
        return dv_y

    pend = None
    for tc in range(KD + 1):
        if tc < KD:
            marks = emit_transposes(tc)
        if pend is not None:
            last_dv_y = emit_stats(pend[0], *pend[1])
        pend = (tc, marks) if tc < KD else None
    yq.wait("out0", cnt["out0"])
    yq.wait("out1", cnt["out1"])

    epi.close()
    ffn.close()
    P.replay()
    es.close()
    return nc


_NC_CACHE = None


def kernel(**inputs):
    global _NC_CACHE
    hp = host_prep(inputs)
    if _NC_CACHE is None:
        _NC_CACHE = build_nc()
    nc = _NC_CACHE
    shared = {k: hp[k] for k in ["wqt", "wkt", "wvt", "wot", "routert", "shgw",
                                 "wg_all", "wu_all", "wd_all", "maskR",
                                 "pswap", "identb", "identf", "iota16"]}
    in_maps = []
    for c in range(NCORES):
        m = dict(shared)
        m["xm"] = hp["xm"][c]
        m["xh"] = hp["xh"][c]
        m["c2"] = hp["c2"][c]
        m["s2"] = hp["s2"][c]
        m["maskA0"] = hp["maskA0"] if c == 0 else hp["maskR"]
        in_maps.append(m)
    res = run_bass_kernel_spmd(nc, in_maps, core_ids=list(range(NCORES)))
    out = np.concatenate([res.results[c]["y"] for c in range(NCORES)], axis=0)
    return out.reshape(1, NCORES * T, DM)
